# version 02
# baseline (speedup 1.0000x reference)
import contextlib
import math

import numpy as np

import concourse.bass as bass
import concourse.mybir as mybir
from concourse.bass_utils import run_bass_kernel_spmd

F32 = mybir.dt.float32
BF16 = mybir.dt.bfloat16
AF = mybir.ActivationFunctionType
OP = mybir.AluOpType
AX = mybir.AxisListType

N, D = 2048, 1024
NT, NDT = N // 128, D // 128
INNER, HEADS, DH = 1024, 16, 64
WIN, NW, NWINS, NCTX = 512, 4, 3, 1536
G, NQ, NKV = 2, 256, 1024
LOGK_Q = math.log(288.0)
LOGK_KV = math.log(1152.0)
CE_SCHED = [max(4.0 * 0.7 ** i, 0.03) for i in range(20)]
FE = 0.03
KVH = 4
E_EXP, S_SLOT = 16, 16
NEG30 = -1.0e30
BISECT_ITERS = 27

L0A_CHUNKS = 8
LOC_HALVES = 2
LOC_WINDOWS = 4
EMIT_ROUTERS = 5
BUILD_UPTO = 5



class Op:
    __slots__ = ("eng", "fn", "kind", "tag", "need_inc", "seq")

    def __init__(self, eng, fn, kind, tag, seq):
        self.eng, self.fn, self.kind, self.tag = eng, fn, kind, tag
        self.need_inc = False
        self.seq = seq


def _overlap(a, b):
    return a[1] > b[0] and b[1] > a[0] and a[3] > b[2] and b[3] > a[2]


class Sched:
    ENGS = ("tensor", "vector", "scalar", "gpsimd", "sync")

    def __init__(self):
        self.ops = []
        self.w_act = {}
        self.r_act = {}
        self.deps = {}

    def op(self, eng, fn, reads=(), writes=(), kind="op", tag=""):
        o = Op(eng, fn, kind, tag, len(self.ops))
        dep = set()
        for buf, rect in reads:
            for wr, wo in self.w_act.get(buf, ()):
                if _overlap(rect, wr):
                    dep.add(wo)
            self.r_act.setdefault(buf, []).append((rect, o))
        for buf, rect in writes:
            wl = self.w_act.setdefault(buf, [])
            keep = []
            for wr, wo in wl:
                if _overlap(rect, wr):
                    dep.add(wo)
                    keep.append((wr, wo))
                else:
                    keep.append((wr, wo))
            rl = self.r_act.get(buf, [])
            nrl = []
            for rr, ro in rl:
                if _overlap(rect, rr):
                    if ro is not o:
                        dep.add(ro)
                    if not (rect[0] <= rr[0] and rr[1] <= rect[1]
                            and rect[2] <= rr[2] and rr[3] <= rect[3]):
                        nrl.append((rr, ro))
                else:
                    nrl.append((rr, ro))
            self.r_act[buf] = nrl
            keep = [(wr, wo) for (wr, wo) in keep
                    if not (rect[0] <= wr[0] and wr[1] <= rect[1] and rect[2] <= wr[2] and wr[3] <= rect[3])]
            keep.append((rect, o))
            self.w_act[buf] = keep
        dep2 = set()
        for d in dep:
            if d is o:
                continue
            dep2.add(d)
        self.deps[o] = dep2
        self.ops.append(o)
        return o

    def dma(self, queue, fn, reads=(), writes=(), tag=""):
        return self.op(queue, fn, reads, writes, kind="dma", tag=tag)

    def finalize(self, blk, sems, dma_sems):
        for o in self.ops:
            for d in self.deps[o]:
                d.need_inc = True
        inc_idx = {}
        dmai = {e: 0 for e in self.ENGS}
        engi = {e: 0 for e in self.ENGS}
        for o in self.ops:
            if o.kind == "dma":
                dmai[o.eng] += 1
                inc_idx[o] = ("dma", o.eng, dmai[o.eng] * 16)
            elif o.need_inc:
                engi[o.eng] += 1
                inc_idx[o] = ("eng", o.eng, engi[o.eng])
        plans = {e: [] for e in self.ENGS}
        waited = {e: {} for e in self.ENGS}
        nwaits = 0
        for o in self.ops:
            waits = []
            for d in sorted(self.deps[o], key=lambda dd: dd.seq):
                ii = inc_idx.get(d)
                if ii is None:
                    continue
                kind, peng, val = ii
                key = (kind, peng)
                if val > waited[o.eng].get(key, 0):
                    waited[o.eng][key] = val
                    waits.append((kind, peng, val))
                    nwaits += 1
            plans[o.eng].append((waits, o))
        for e in self.ENGS:
            plan = plans[e]

            def section(eng, plan=plan):
                for waits, o in plan:
                    for kind, peng, val in waits:
                        h = dma_sems[peng] if kind == "dma" else sems[peng]
                        eng.wait_ge(h, val)
                    ii = inc_idx.get(o)
                    if ii is not None and ii[0] == "dma" and ii[2] > 16:
                        eng.wait_ge(dma_sems[ii[1]], ii[2] - 16)
                    instr = o.fn(eng)
                    if ii is not None:
                        if ii[0] == "dma":
                            instr.then_inc(dma_sems[ii[1]], 16)
                        else:
                            instr.then_inc(sems[ii[1]], 1)
            getattr(blk, e)(section)
        return {"ops": len(self.ops), "incs": sum(o.need_inc for o in self.ops), "waits": nwaits}


def _sel(e, out, mask, on_true, on_false):
    return e.select(out, mask, on_true, on_false, add_drain=True)


def R(buf, r0, r1, c0, c1):
    return (buf, (r0, r1, c0, c1))



def _host_tables():
    import ml_dtypes
    bf = ml_dtypes.bfloat16
    t = {}
    t["ident_f32"] = np.eye(128, dtype=np.float32)
    t["ident_bf"] = np.eye(128, dtype=bf)
    t["ones_row_bf"] = np.ones((1, 2048), dtype=bf)
    t["ones_row_f32"] = np.ones((1, 128), dtype=np.float32)
    t["ones_col_bf"] = np.ones((128, 1), dtype=bf)
    t["pos_row"] = np.arange(N, dtype=np.float32)[None, :]
    kk = np.arange(128)[:, None]
    qq = np.arange(512)[None, :]
    cm = np.zeros((4, 128, 512), dtype=np.float32)
    for r in range(4):
        cm[r] = np.where(128 * r + kk > qq, 0.0, 1.0)
    t["cmask"] = np.ascontiguousarray(cm.transpose(1, 0, 2).reshape(128, 4 * 512)).astype(bf)
    bm = np.zeros((8, 128, 512), dtype=np.float32)
    for kt in range(8):
        kr = 128 * kt + kk
        bm[kt] = np.where((kr < qq) | (kr > qq + 512), 0.0, 1.0)
    t["bmask"] = np.ascontiguousarray(bm.transpose(1, 0, 2).reshape(128, 8 * 512)).astype(bf)
    t["j128_col"] = np.tile(np.arange(32, dtype=np.float32) / 32.0, 4)[:, None]
    i1536 = np.arange(1536, dtype=np.float32)
    t["satkey_row"] = (1.0 + (1536.0 - i1536) * 1e-5).astype(np.float32)[None, :]
    t["zerokey_row"] = (-30.0 - i1536 * 1e-5).astype(np.float32)[None, :]
    t["logk_col"] = np.array([[LOGK_Q]] * 6 + [[LOGK_KV]] * 2, dtype=np.float32)
    t["kk_col"] = np.array([[256.0]] * 6 + [[1024.0]] * 2, dtype=np.float32)
    return t


def _tiles_cm(a):
    d_in, o = a.shape
    assert d_in % 128 == 0
    return np.ascontiguousarray(
        a.reshape(d_in // 128, 128, o).transpose(1, 0, 2).reshape(128, -1).astype(np.float32))


def _colv(b):
    return np.ascontiguousarray(np.asarray(b).reshape(-1, 128).T.astype(np.float32))


def _prep_weights(W):
    ft = np.float32
    H = {}

    def tT(a):
        return np.ascontiguousarray(np.asarray(a).T.astype(ft))

    H["wqT"] = _tiles_cm(tT(W["L0_attn_Wq"]))
    H["wkT"] = _tiles_cm(tT(W["L0_attn_Wk"]))
    H["wvT"] = _tiles_cm(tT(W["L0_attn_Wv"]))
    H["bqc"] = _colv(W["L0_attn_bq"])
    H["bkc"] = _colv(W["L0_attn_bk"])
    H["bv_row"] = np.asarray(W["L0_attn_bv"], ft)[None, :]
    H["wqkvT"] = _tiles_cm(tT(W["L0_ffn_w_qkv"]))
    H["wloT"] = _tiles_cm(tT(W["L0_ffn_w_lo"]))
    H["rtT"] = _tiles_cm(np.concatenate([tT(W["L0_ffn_rt_q"]), tT(W["L0_ffn_rt_kv"])], axis=1))
    H["lngc"] = _colv(W["L0_ffn_ln_g"])
    H["lnbc"] = _colv(W["L0_ffn_ln_b"])
    H["qngc"] = np.concatenate([_colv(np.asarray(W["L0_ffn_qnorm_g"])[g]) for g in range(G)], axis=1)
    H["cngc"] = np.concatenate([_colv(np.asarray(W["L0_ffn_cnorm_g"])[g]) for g in range(G)], axis=1)
    H["nullk"] = np.stack([_colv(np.asarray(W["L0_ffn_null_kv"])[0, g].reshape(-1)) for g in range(G)])
    H["nullv_row"] = np.stack([np.concatenate(
        [np.asarray(W["L0_ffn_null_kv"])[1, g].reshape(16, 64).astype(ft),
         np.ones((16, 1), dtype=ft)], axis=1).reshape(-1) for g in range(G)])
    H["wgqT"] = np.stack([_tiles_cm(tT(np.asarray(W["L0_ffn_w_q"])[g])) for g in range(G)])
    H["wgkvT"] = np.stack([_tiles_cm(tT(np.asarray(W["L0_ffn_w_kv"])[g])) for g in range(G)])
    H["wgoT"] = np.stack([_tiles_cm(tT(np.asarray(W["L0_ffn_w_out"])[g])) for g in range(G)])
    perm = np.concatenate([np.arange(0, DH, 2), np.arange(1, DH, 2)])
    qcols = np.concatenate([h * DH + perm for h in range(HEADS)])
    kcols = np.concatenate([h * DH + perm for h in range(KVH)])
    H["l1wq"] = _tiles_cm(np.asarray(W["L1_attn_Wq"], ft)[:, qcols])
    H["l1wk"] = _tiles_cm(np.asarray(W["L1_attn_Wk"], ft)[:, kcols])
    H["l1wv"] = _tiles_cm(np.asarray(W["L1_attn_Wv"], ft))
    H["l1wo"] = _tiles_cm(np.asarray(W["L1_attn_Wo"], ft))
    H["phase_row"] = np.asarray(W["L1_attn_q_phase_angle"], ft)[None, :]
    H["als_row"] = np.asarray(W["L1_attn_alibi_log_slopes"], ft)[None, :]
    H["f_gamc"] = _colv(W["L1_ffn_ln_gamma"])
    H["f_betc"] = _colv(W["L1_ffn_ln_beta"])
    H["f_gam_row"] = np.asarray(W["L1_ffn_ln_gamma"], ft)[None, :]
    H["f_bet_row"] = np.asarray(W["L1_ffn_ln_beta"], ft)[None, :]
    H["f_mu"] = _tiles_cm(np.asarray(W["L1_ffn_mu"], ft).reshape(D, E_EXP * S_SLOT))
    H["f_scale"] = np.asarray(W["L1_ffn_scale"], ft).reshape(1, 1)
    H["f_w1"] = np.stack([_tiles_cm(np.asarray(W["L1_ffn_w1"])[e].astype(ft)) for e in range(E_EXP)])
    H["f_b1c"] = np.concatenate([_colv(np.asarray(W["L1_ffn_b1"])[e]) for e in range(E_EXP)], axis=1)
    w2b = np.stack([np.concatenate([np.asarray(W["L1_ffn_w2"])[e].astype(ft),
                                    np.asarray(W["L1_ffn_b2"])[e].astype(ft)[None, :]], axis=0)
                    for e in range(E_EXP)])
    H["f_w2a"] = np.ascontiguousarray(
        w2b[:, :256].reshape(E_EXP, 2, 128, D).transpose(0, 2, 1, 3).reshape(E_EXP, 128, 2 * D))
    H["f_w2b"] = np.ascontiguousarray(w2b[:, 256:257])
    return H


DRAM_SPECS = [
    ("x", (N, D), F32),
    ("ident_f32", (128, 128), F32),
    ("ident_bf", (128, 128), BF16),
    ("ones_row_bf", (1, 2048), BF16),
    ("ones_row_f32", (1, 128), F32),
    ("ones_col_bf", (128, 1), BF16),
    ("pos_row", (1, N), F32),
    ("satkey_row", (1, 1536), F32), ("zerokey_row", (1, 1536), F32),
    ("logk_col", (8, 1), F32), ("kk_col", (8, 1), F32),
    ("j128_col", (128, 1), F32),
    ("cmask", (128, 4 * 512), BF16),
    ("bmask", (128, 8 * 512), BF16),
    ("wqT", (128, 8 * 1024), F32), ("wkT", (128, 8 * 1024), F32), ("wvT", (128, 8 * 1024), F32),
    ("bqc", (128, 8), F32), ("bkc", (128, 8), F32), ("bv_row", (1, 1024), F32),
    ("wqkvT", (128, 8 * 3072), F32), ("wloT", (128, 8 * 1024), F32), ("rtT", (128, 8 * 4), F32),
    ("lngc", (128, 8), F32), ("lnbc", (128, 8), F32),
    ("qngc", (128, 16), F32), ("cngc", (128, 16), F32),
    ("nullk", (2, 128, 8), F32), ("nullv_row", (2, 1040), F32),
    ("wgqT", (2, 128, 8 * 1024), F32), ("wgkvT", (2, 128, 8 * 2048), F32), ("wgoT", (2, 128, 8 * 1024), F32),
    ("l1wq", (128, 4 * 1024), F32), ("l1wk", (128, 4 * 256), F32), ("l1wv", (128, 4 * 256), F32),
    ("l1wo", (128, 8 * 1024), F32),
    ("phase_row", (1, 16), F32), ("als_row", (1, 16), F32),
    ("f_gamc", (128, 8), F32), ("f_betc", (128, 8), F32),
    ("f_gam_row", (1, 1024), F32), ("f_bet_row", (1, 1024), F32),
    ("f_mu", (128, 8 * 256), F32), ("f_scale", (1, 1), F32),
    ("f_w1", (16, 128, 8 * 256), F32), ("f_b1c", (128, 32), F32),
    ("f_w2a", (16, 128, 2 * 1024), F32), ("f_w2b", (16, 1, 1024), F32),
]

HEAP_COLS = 49280



def build_program():
    nc = bass.Bass("TRN2", target_bir_lowering=False)
    dt = {}
    for nm, sh, ty in DRAM_SPECS:
        dt[nm] = nc.dram_tensor(nm, list(sh), ty, kind="ExternalInput")
    out_d = nc.dram_tensor("out", [N, D], F32, kind="ExternalOutput")
    xres = nc.dram_tensor("xres", [N, D], F32, kind="Internal")
    cmt_d = nc.dram_tensor("cmt_d", [128, 4096], BF16, kind="Internal")
    dt["cmt_d"] = cmt_d

    with contextlib.ExitStack() as es:
        def sbuf(name, shape, dtype):
            return es.enter_context(nc.sbuf_tensor(name, list(shape), dtype))

        sems = {e: es.enter_context(nc.semaphore("se_" + e)) for e in Sched.ENGS}
        dma_sems = {e: es.enter_context(nc.semaphore("sd_" + e)) for e in ("sync", "gpsimd", "scalar")}

        M = {}
        M["xnT"] = sbuf("xnT", (128, NDT * N), BF16)
        M["stats"] = sbuf("stats", (128, 128), F32)
        M["cm"] = sbuf("cm", (128, 4 * 512), BF16)
        M["idf"] = sbuf("idf", (128, 128), F32)
        M["idb"] = sbuf("idb", (128, 128), BF16)
        M["ones_row"] = sbuf("ones_row", (1, 2048), BF16)
        M["ones_rf"] = sbuf("ones_rf", (1, 128), F32)
        M["ones_col"] = sbuf("ones_col", (128, 1), BF16)
        M["pos"] = sbuf("pos", (1, N), F32)
        M["rowf"] = sbuf("rowf", (8, 2564), F32)
        M["routf"] = sbuf("routf", (8, 7680), F32)
        M["selcol"] = sbuf("selcol", (128, 128), F32)
        M["hf"] = sbuf("hf", (128, 4224), F32)
        M["hb"] = sbuf("hb", (128, HEAP_COLS), BF16)
        M["qa"] = sbuf("qa", (128, 512), BF16)
        M["qx"] = sbuf("qx", (8, 512), BF16)
        M["rtb"] = sbuf("rtb", (128, 80), BF16)
        M["ki"] = sbuf("ki", (128, 512), mybir.dt.int32)
        M["B"] = [es.enter_context(nc.psum_tensor(f"pb{i}", [128, 512], F32)) for i in range(7)]
        M["PTB"] = es.enter_context(nc.psum_tensor("ptb", [128, 1024], BF16))

        S = Sched()
        _build(S, dt, out_d, xres, M)
        with nc.Block() as blk:
            st = S.finalize(blk, sems, dma_sems)
        build_program.stats = st
    return nc


def _build(S, dt, out_d, xres, M):
    xnT, stats = M["xnT"], M["stats"]
    hb, hf = M["hb"], M["hf"]
    rowf, routf = M["rowf"], M["routf"]
    B = M["B"]

    def PB(b):
        return R(f"pb{b}", 0, 128, 0, 512)

    for nm, dst in [("cmask", M["cm"]), ("ident_f32", M["idf"]), ("ident_bf", M["idb"]),
                    ("ones_row_bf", M["ones_row"]),
                    ("ones_row_f32", M["ones_rf"]), ("ones_col_bf", M["ones_col"]), ("pos_row", M["pos"])]:
        rows = dst.shape[0]
        cols = dst.shape[1]
        S.dma("sync", lambda e, nm=nm, dst=dst: e.dma_start(dst[:, :], dt[nm].ap()),
              writes=[R({"ident_f32": "idf", "ident_bf": "idb"}.get(nm, nm), 0, rows, 0, cols)], tag="ld_" + nm)
    CONST = {"cmask": R("cmask", 0, 128, 0, 2048), "idf": R("idf", 0, 128, 0, 128),
             "idb": R("idb", 0, 128, 0, 128), "bmask": R("bmask", 0, 128, 0, 4096),
             "ones_row_bf": R("ones_row_bf", 0, 1, 0, 2048), "ones_row_f32": R("ones_row_f32", 0, 1, 0, 128),
             "ones_col_bf": R("ones_col_bf", 0, 128, 0, 1), "pos_row": R("pos_row", 0, 1, 0, N)}

    _stage_ln(S, dt, M, PB, CONST)
    if EMIT_ROUTERS >= 1:
        _stage_routers(S, dt, M, PB, CONST)
    if BUILD_UPTO >= 1:
        _l0_attn(S, dt, xres, M, PB, CONST)
    if BUILD_UPTO >= 2:
        _l0_local(S, dt, xres, M, PB, CONST)
    if BUILD_UPTO >= 3:
        _l0_grouped(S, dt, xres, M, PB, CONST)
    if BUILD_UPTO >= 4:
        _bv_fold(S, dt, xres, M, PB, CONST)
        _stage_ln(S, dt, M, PB, CONST, src=xres, with_lo=False)
        if BUILD_UPTO >= 5:
            _l1_ffn_pre(S, dt, xres, M, PB, CONST)
        _l1_attn(S, dt, xres, M, PB, CONST)
        if BUILD_UPTO >= 5:
            _l1_ffn_out(S, dt, out_d, xres, M, PB, CONST)
    _finish(S, dt, out_d, xres, M, PB, CONST)



XL = HEAP_COLS - 16384


def _stage_ln(S, dt, M, PB, CONST, src=None, with_lo=True):
    xnT, stats, hf, hb = M["xnT"], M["stats"], M["hf"], M["hb"]

    def st(c0, c1):
        return R("stats", 0, 128, c0, c1)

    for i in range(NT):
        r = 1024 * (i % 2)
        xr = R("hf", 0, 128, r, r + D)
        xn = R("hf", 0, 128, 2048, 3072)
        sq = R("hf", 0, 128, 3072, 4096)
        if src is None:
            S.dma("sync", lambda e, i=i, r=r: e.dma_start(hf[:, r:r + D],
                                                          dt["x"].ap()[128 * i:128 * (i + 1), :]),
                  reads=[xr], writes=[xr], tag="ld_xt")
        else:
            S.dma("sync", lambda e, i=i, r=r: e.dma_start(hf[:, r:r + D],
                                                          src.ap()[128 * i:128 * (i + 1), :]),
                  reads=[xr, R("xres", i, i + 1, 0, 1)], writes=[xr], tag="ld_xt1")
        S.op("vector", lambda e, i=i, r=r: e.reduce_sum(stats[:, i:i + 1], hf[:, r:r + D], axis=AX.X),
             reads=[xr], writes=[st(i, i + 1)], tag="lnsum")
        S.op("vector", lambda e, i=i: e.tensor_scalar(stats[:, i:i + 1], stats[:, i:i + 1], 1.0 / D, None,
                                                      op0=OP.mult),
             reads=[st(i, i + 1)], writes=[st(i, i + 1)], tag="mu")
        S.op("vector", lambda e, i=i, r=r: e.tensor_scalar(hf[:, 2048:3072], hf[:, r:r + D], stats[:, i:i + 1],
                                                           None, op0=OP.subtract),
             reads=[xr, st(i, i + 1)], writes=[xn], tag="xc")
        S.op("scalar", lambda e: e.activation(hf[:, 3072:4096], hf[:, 2048:3072], AF.Square),
             reads=[xn], writes=[sq], tag="sq")
        S.op("vector", lambda e, i=i: e.reduce_sum(stats[:, 16 + i:17 + i], hf[:, 3072:4096], axis=AX.X),
             reads=[sq], writes=[st(16 + i, 17 + i)], tag="var")
        S.op("vector", lambda e, i=i: e.tensor_scalar(stats[:, 16 + i:17 + i], stats[:, 16 + i:17 + i],
                                                      1.0 / D, 1e-5, op0=OP.mult, op1=OP.add),
             reads=[st(16 + i, 17 + i)], writes=[st(16 + i, 17 + i)], tag="vs")
        S.op("vector", lambda e, i=i: e.reciprocal(stats[:, 32 + i:33 + i], stats[:, 16 + i:17 + i]),
             reads=[st(16 + i, 17 + i)], writes=[st(32 + i, 33 + i)], tag="rcp")
        S.op("scalar", lambda e, i=i: e.activation(stats[:, 32 + i:33 + i], stats[:, 32 + i:33 + i], AF.Sqrt),
             reads=[st(32 + i, 33 + i)], writes=[st(32 + i, 33 + i)], tag="sqrt")
        for _ in range(2):
            S.op("vector", lambda e, i=i: e.tensor_tensor(stats[:, 48 + i:49 + i], stats[:, 32 + i:33 + i],
                                                          stats[:, 32 + i:33 + i], op=OP.mult),
                 reads=[st(32 + i, 33 + i)], writes=[st(48 + i, 49 + i)], tag="n1")
            S.op("vector", lambda e, i=i: e.tensor_tensor(stats[:, 48 + i:49 + i], stats[:, 48 + i:49 + i],
                                                          stats[:, 16 + i:17 + i], op=OP.mult),
                 reads=[st(48 + i, 49 + i), st(16 + i, 17 + i)], writes=[st(48 + i, 49 + i)], tag="n2")
            S.op("vector", lambda e, i=i: e.tensor_scalar(stats[:, 48 + i:49 + i], stats[:, 48 + i:49 + i],
                                                          -0.5, 1.5, op0=OP.mult, op1=OP.add),
                 reads=[st(48 + i, 49 + i)], writes=[st(48 + i, 49 + i)], tag="n3")
            S.op("vector", lambda e, i=i: e.tensor_tensor(stats[:, 32 + i:33 + i], stats[:, 32 + i:33 + i],
                                                          stats[:, 48 + i:49 + i], op=OP.mult),
                 reads=[st(32 + i, 33 + i), st(48 + i, 49 + i)], writes=[st(32 + i, 33 + i)], tag="n4")
        S.op("vector", lambda e, i=i: e.tensor_scalar(hf[:, 2048:3072], hf[:, 2048:3072],
                                                      stats[:, 32 + i:33 + i], None, op0=OP.mult),
             reads=[xn, st(32 + i, 33 + i)], writes=[xn], tag="xnmul")
        for j in range(NDT):
            b = 5 + (j % 2)
            xnsl = R("hf", 0, 128, 2048 + 128 * j, 2048 + 128 * (j + 1))
            hiT = R("xnT", 0, 128, N * j + 128 * i, N * j + 128 * (i + 1))
            loT = R("hb", 0, 128, XL + N * j + 128 * i, XL + N * j + 128 * (i + 1))
            S.op("tensor", lambda e, j=j, b=b: e.transpose(M["B"][b][:, 0:128],
                                                           hf[:, 2048 + 128 * j:2048 + 128 * (j + 1)],
                                                           M["idf"][:, :]),
                 reads=[xnsl, CONST["idf"]], writes=[R(f"pb{b}", 0, 128, 0, 128)], tag="trx")
            S.op("scalar", lambda e, i=i, j=j, b=b: e.activation(
                xnT[:, N * j + 128 * i:N * j + 128 * (i + 1)], M["B"][b][:, 0:128], AF.Copy),
                reads=[R(f"pb{b}", 0, 128, 0, 128)], writes=[R(f"pb{b}", 0, 128, 0, 128), hiT], tag="xnh")
            if with_lo:
                S.op("vector", lambda e, i=i, j=j, b=b: e.tensor_tensor(
                    hb[:, XL + N * j + 128 * i:XL + N * j + 128 * (i + 1)], M["B"][b][:, 0:128],
                    xnT[:, N * j + 128 * i:N * j + 128 * (i + 1)], op=OP.subtract),
                    reads=[R(f"pb{b}", 0, 128, 0, 128), hiT],
                    writes=[R(f"pb{b}", 0, 128, 0, 128), loT], tag="xnl")

    S.op("vector", lambda e: e.tensor_tensor(stats[:, 48:64], stats[:, 32:48], stats[:, 32:48], op=OP.mult),
         reads=[st(32, 48)], writes=[st(48, 64)], tag="rs")
    S.op("vector", lambda e: e.tensor_scalar(stats[:, 64:80], stats[:, 48:64], -1e-5, 1.0,
                                             op0=OP.mult, op1=OP.add),
         reads=[st(48, 64)], writes=[st(64, 80)], tag="var2")

    def rsqrt_cols(dst0, eps):
        S.op("vector", lambda e: e.tensor_scalar(stats[:, dst0:dst0 + 16], stats[:, 64:80], 1.0, eps,
                                                 op0=OP.mult, op1=OP.add),
             reads=[st(64, 80)], writes=[st(dst0, dst0 + 16)], tag="rq0")
        S.op("vector", lambda e: e.reciprocal(stats[:, 112:128], stats[:, dst0:dst0 + 16]),
             reads=[st(dst0, dst0 + 16)], writes=[st(112, 128)], tag="rq1")
        S.op("scalar", lambda e: e.activation(stats[:, 112:128], stats[:, 112:128], AF.Sqrt),
             reads=[st(112, 128)], writes=[st(112, 128)], tag="rq2")
        for _ in range(2):
            S.op("vector", lambda e: e.tensor_tensor(stats[:, 48:64], stats[:, 112:128], stats[:, 112:128],
                                                     op=OP.mult),
                 reads=[st(112, 128)], writes=[st(48, 64)], tag="rq3")
            S.op("vector", lambda e: e.tensor_tensor(stats[:, 48:64], stats[:, 48:64], stats[:, dst0:dst0 + 16],
                                                     op=OP.mult),
                 reads=[st(48, 64), st(dst0, dst0 + 16)], writes=[st(48, 64)], tag="rq4")
            S.op("vector", lambda e: e.tensor_scalar(stats[:, 48:64], stats[:, 48:64], -0.5, 1.5,
                                                     op0=OP.mult, op1=OP.add),
                 reads=[st(48, 64)], writes=[st(48, 64)], tag="rq5")
            S.op("vector", lambda e: e.tensor_tensor(stats[:, 112:128], stats[:, 112:128], stats[:, 48:64],
                                                     op=OP.mult),
                 reads=[st(112, 128), st(48, 64)], writes=[st(112, 128)], tag="rq6")
        S.op("vector", lambda e: e.tensor_copy(stats[:, dst0:dst0 + 16], stats[:, 112:128]),
             reads=[st(112, 128)], writes=[st(dst0, dst0 + 16)], tag="rq7")

    rsqrt_cols(80, 1e-5)
    rsqrt_cols(96, 0.0)



def _stage_routers(S, dt, M, PB, CONST):
    xnT, hb, hf = M["xnT"], M["hb"], M["hf"]
    rowf, routf, rtb = M["rowf"], M["routf"], M["rtb"]
    B = M["B"]
    n = 1536

    def RW(r0, r1, c0, c1):
        return R("routf", r0, r1, c0, c1)

    def RF(r0, r1, c0, c1):
        return R("rowf", r0, r1, c0, c1)

    xnT_all = R("xnT", 0, 128, 0, NDT * N)
    xnl_all = R("hb", 0, 128, XL, XL + NDT * N)

    S.dma("sync", lambda e: e.dma_start(hf[:, 4096:4128], dt["rtT"].ap()),
          reads=[R("hf", 0, 128, 4096, 4128)], writes=[R("hf", 0, 128, 4096, 4128)], tag="ld_rt")
    S.op("scalar", lambda e: e.activation(rtb[:, 0:32], hf[:, 4096:4128], AF.Copy),
         reads=[R("hf", 0, 128, 4096, 4128)], writes=[R("rtb", 0, 128, 0, 32)], tag="rth")
    S.op("vector", lambda e: e.tensor_tensor(rtb[:, 32:64], hf[:, 4096:4128], rtb[:, 0:32], op=OP.subtract),
         reads=[R("hf", 0, 128, 4096, 4128), R("rtb", 0, 128, 0, 32)], writes=[R("rtb", 0, 128, 32, 64)], tag="rtl")
    for ch in range(4):
        def mmrt(e, ch=ch):
            last = None
            for hi in (0, 32):
                for lo in (0, 1):
                    for j in range(NDT):
                        if lo == 0:
                            rhs = xnT[:, N * j + 512 * ch:N * j + 512 * (ch + 1)]
                        else:
                            rhs = hb[:, XL + N * j + 512 * ch:XL + N * j + 512 * (ch + 1)]
                        last = e.matmul(B[5][0:4, 0:512], rtb[:, hi + 4 * j:hi + 4 * (j + 1)], rhs,
                                        start=(hi == 0 and lo == 0 and j == 0),
                                        stop=(hi == 32 and lo == 1 and j == NDT - 1))
            return last
        S.op("tensor", mmrt, reads=[R("rtb", 0, 128, 0, 64), xnT_all, xnl_all],
             writes=[R("pb5", 0, 4, 0, 512)], tag="s_mm")
        S.op("vector", lambda e, ch=ch: e.tensor_copy(hf[0:4, 512 * ch:512 * (ch + 1)], B[5][0:4, 0:512]),
             reads=[R("pb5", 0, 4, 0, 512)],
             writes=[R("pb5", 0, 4, 0, 512), R("hf", 0, 4, 512 * ch, 512 * (ch + 1))], tag="s_ev")
    for w in range(3):
        S.dma("sync", lambda e, w=w: e.dma_start(routf[2 * w:2 * w + 2, 0:512],
                                                 hf[0:2, 512 * (w + 1):512 * (w + 2)]),
              reads=[R("hf", 0, 2, 512 * (w + 1), 512 * (w + 2))],
              writes=[R("routf", 2 * w, 2 * w + 2, 0, 512)], tag="s_scq")
    S.dma("sync", lambda e: e.dma_start(routf[6:8, 0:1536], hf[2:4, 0:1536]),
          reads=[R("hf", 2, 4, 0, 1536)], writes=[R("routf", 6, 8, 0, 1536)], tag="s_sck")
    S.op("vector", lambda e: e.memset(routf[0:6, 512:1536], NEG30),
         writes=[R("routf", 0, 6, 512, 1536)], tag="spad")

    if EMIT_ROUTERS < 2:
        return
    n = 1536
    sA = R("routf", 0, 8, 0, 1536)
    uA = R("routf", 0, 8, 1536, 3072)
    gA = R("routf", 0, 8, 3072, 4608)
    tA = R("routf", 0, 8, 4608, 6144)
    zA = R("routf", 0, 8, 6144, 7680)
    dA = R("hf", 0, 8, 2048, 3584)
    CC = lambda c: RF(0, 8, c, c + 1)
    S.dma("sync", lambda e: e.dma_start(rowf[0:8, 10:11], dt["logk_col"].ap()),
          writes=[CC(10)], tag="ld_logk")
    S.dma("sync", lambda e: e.dma_start(rowf[0:8, 11:12], dt["kk_col"].ap()),
          writes=[CC(11)], tag="ld_kk")
    S.dma("sync", lambda e: e.dma_start(rowf[0:1, 44:1580], dt["satkey_row"].ap()),
          writes=[RF(0, 1, 44, 1580)], tag="ld_sat")
    S.dma("sync", lambda e: e.dma_start(hf[0:1, 3584:5120][:, 0:1536] if False else hf[0:1, 2048:3584],
                                        dt["zerokey_row"].ap()),
          writes=[R("hf", 0, 1, 2048, 3584)], tag="ld_zero")
    for which, dst_c in ((0, 4608), (1, 6144)):
        for ch in range(3):
            def bkey(e, which=which, ch=ch):
                if which == 0:
                    rhs = rowf[0:1, 44 + 512 * ch:44 + 512 * (ch + 1)]
                else:
                    rhs = hf[0:1, 2048 + 512 * ch:2048 + 512 * (ch + 1)]
                return e.matmul(B[6][0:8, 0:512], M["ones_rf"][0:1, 0:8], rhs, start=True, stop=True)
            S.op("tensor", bkey,
                 reads=[RF(0, 1, 44, 1580), R("hf", 0, 1, 2048, 3584), CONST["ones_row_f32"]],
                 writes=[R("pb6", 0, 8, 0, 512)], tag="bkey")
            S.op("vector", lambda e, dst_c=dst_c, ch=ch: e.tensor_copy(
                routf[0:8, dst_c + 512 * ch:dst_c + 512 * (ch + 1)], B[6][0:8, 0:512]),
                reads=[R("pb6", 0, 8, 0, 512)],
                writes=[R("pb6", 0, 8, 0, 512), R("routf", 0, 8, dst_c + 512 * ch, dst_c + 512 * (ch + 1))],
                tag="bkey_ev")

    S.op("vector", lambda e: e.tensor_scalar(routf[0:8, 1536:3072], routf[0:8, 0:1536], -1e29, None,
                                             op0=OP.is_ge),
         reads=[sA], writes=[uA], tag="r_valid")
    S.op("vector", lambda e: e.tensor_tensor(routf[0:8, 1536:3072], routf[0:8, 1536:3072],
                                             routf[0:8, 0:1536], op=OP.mult),
         reads=[uA, sA], writes=[uA], tag="r_bn0")
    S.op("vector", lambda e: e.tensor_scalar(routf[0:8, 1536:3072], routf[0:8, 1536:3072], -1.0, None,
                                             op0=OP.mult),
         reads=[uA], writes=[uA], tag="r_bn1")

    for it, ce in enumerate(CE_SCHED):
        S.op("vector", lambda e: e.tensor_tensor(routf[0:8, 1536:3072], routf[0:8, 0:1536],
                                                 routf[0:8, 1536:3072], op=OP.add),
             reads=[sA, uA], writes=[uA], tag="r_u")
        S.op("vector", lambda e, ce=ce: e.tensor_scalar(routf[0:8, 1536:3072], routf[0:8, 1536:3072],
                                                        1.0 / ce, None, op0=OP.mult),
             reads=[uA], writes=[uA], tag="r_us")
        S.op("vector", lambda e: e.reduce_max(rowf[0:8, 0:1], routf[0:8, 1536:3072], axis=AX.X, negate=True),
             reads=[uA], writes=[CC(0)], tag="r_m")
        S.op("scalar", lambda e: e.activation(routf[0:8, 1536:3072], routf[0:8, 1536:3072], AF.Exp,
                                              bias=rowf[0:8, 0:1], scale=1.0),
             reads=[uA, CC(0)], writes=[uA], tag="r_e")
        S.op("vector", lambda e: e.reduce_sum(rowf[0:8, 1:2], routf[0:8, 1536:3072], axis=AX.X),
             reads=[uA], writes=[CC(1)], tag="r_s")
        S.op("scalar", lambda e: e.activation(rowf[0:8, 2:3], rowf[0:8, 1:2], AF.Ln),
             reads=[CC(1)], writes=[CC(2)], tag="r_ln")
        S.op("vector", lambda e: e.tensor_tensor(rowf[0:8, 2:3], rowf[0:8, 2:3], rowf[0:8, 0:1],
                                                 op=OP.subtract),
             reads=[CC(2), CC(0)], writes=[CC(2)], tag="r_lse")
        S.op("vector", lambda e: e.tensor_tensor(rowf[0:8, 9:10], rowf[0:8, 10:11], rowf[0:8, 2:3],
                                                 op=OP.subtract),
             reads=[CC(2), CC(10)], writes=[CC(9)], tag="r_lk")
        S.op("vector", lambda e, ce=ce: e.tensor_scalar(rowf[0:8, 3:4], rowf[0:8, 9:10], ce, None,
                                                        op0=OP.mult),
             reads=[CC(9)], writes=[CC(3)], tag="r_a")
        S.op("vector", lambda e: e.tensor_scalar(routf[0:8, 1536:3072], routf[0:8, 0:1536],
                                                 rowf[0:8, 3:4], None, op0=OP.add),
             reads=[sA, CC(3)], writes=[uA], tag="r_sa")
        if it < len(CE_SCHED) - 1:
            S.op("vector", lambda e: e.tensor_scalar(routf[0:8, 1536:3072], routf[0:8, 1536:3072],
                                                     0.0, -1.0, op0=OP.max, op1=OP.mult),
                 reads=[uA], writes=[uA], tag="r_bneg")

    if EMIT_ROUTERS < 3:
        return
    S.op("vector", lambda e: e.tensor_scalar(routf[0:8, 1536:3072], routf[0:8, 1536:3072], 0.0, -25.0,
                                             op0=OP.min, op1=OP.max),
         reads=[uA], writes=[uA], tag="k_u")
    S.op("scalar", lambda e: e.activation(routf[0:8, 3072:4608], routf[0:8, 1536:3072], AF.Exp,
                                          scale=1.0 / FE),
         reads=[uA], writes=[gA], tag="k_sc")
    S.op("vector", lambda e: e.memset(rowf[0:8, 12:13], 1e-30), writes=[CC(12)], tag="t_ms")
    S.op("vector", lambda e: e.tensor_scalar(rowf[0:8, 12:13], rowf[0:8, 12:13], 1e-30, None, op0=OP.mult),
         reads=[CC(12)], writes=[CC(12)], tag="t_u1")
    S.op("vector", lambda e: e.tensor_scalar(rowf[0:8, 12:13], rowf[0:8, 12:13], 0.0, None, op0=OP.is_le),
         reads=[CC(12)], writes=[CC(12)], tag="t_g")
    S.op("vector", lambda e: e.tensor_scalar(rowf[0:8, 12:13], rowf[0:8, 12:13], -1.0, 1.0,
                                             op0=OP.mult, op1=OP.add),
         reads=[CC(12)], writes=[CC(12)], tag="t_f")
    S.op("vector", lambda e: e.tensor_scalar(rowf[0:8, 13:14], rowf[0:8, 12:13], 2.0 ** -128, None, op0=OP.mult),
         reads=[CC(12)], writes=[CC(13)], tag="t_fb")
    S.op("vector", lambda e: e.tensor_scalar(rowf[0:8, 14:15], rowf[0:8, 12:13], -1.0, 1.0,
                                             op0=OP.mult, op1=OP.add),
         reads=[CC(12)], writes=[CC(14)], tag="t_1f")
    S.op("vector", lambda e: e.tensor_tensor(rowf[0:8, 13:14], rowf[0:8, 13:14], rowf[0:8, 14:15], op=OP.add),
         reads=[CC(13), CC(14)], writes=[CC(13)], tag="t_b")
    S.op("vector", lambda e: e.tensor_tensor(rowf[0:8, 14:15], rowf[0:8, 13:14], rowf[0:8, 13:14], op=OP.mult),
         reads=[CC(13)], writes=[CC(14)], tag="t_c")
    S.op("vector", lambda e: e.tensor_tensor(rowf[0:8, 15:16], rowf[0:8, 14:15], rowf[0:8, 14:15], op=OP.mult),
         reads=[CC(14)], writes=[CC(15)], tag="t_c2")
    S.op("vector", lambda e: e.tensor_scalar(rowf[0:8, 16:17], rowf[0:8, 13:14], 2.0 ** -126, None, op0=OP.mult),
         reads=[CC(13)], writes=[CC(16)], tag="t_ab")
    S.op("vector", lambda e: e.tensor_tensor(rowf[0:8, 16:17], rowf[0:8, 16:17], rowf[0:8, 14:15], op=OP.mult),
         reads=[CC(16), CC(14)], writes=[CC(16)], tag="t_ac")
    S.op("vector", lambda e: e.tensor_tensor(rowf[0:8, 16:17], rowf[0:8, 16:17], rowf[0:8, 15:16], op=OP.mult),
         reads=[CC(16), CC(15)], writes=[CC(16)], tag="t_T")
    S.op("vector", lambda e: e.tensor_scalar(routf[0:8, 0:1536], routf[0:8, 3072:4608], rowf[0:8, 16:17],
                                             None, op0=OP.is_ge),
         reads=[gA, CC(16)], writes=[sA], tag="k_mz")
    S.op("vector", lambda e: e.tensor_scalar(routf[0:8, 0:1536], routf[0:8, 0:1536], -1.0, 1.0,
                                             op0=OP.mult, op1=OP.add),
         reads=[sA], writes=[sA], tag="k_mzi")
    S.op("vector", lambda e: e.tensor_tensor(hf[0:8, 2048:3584], routf[0:8, 6144:7680],
                                             routf[0:8, 1536:3072], op=OP.subtract),
         reads=[zA, uA], writes=[dA], tag="k_zd")
    S.op("vector", lambda e: e.tensor_tensor(hf[0:8, 2048:3584], hf[0:8, 2048:3584],
                                             routf[0:8, 0:1536], op=OP.mult),
         reads=[dA, sA], writes=[dA], tag="k_zm")
    S.op("vector", lambda e: e.tensor_tensor(routf[0:8, 1536:3072], routf[0:8, 1536:3072],
                                             hf[0:8, 2048:3584], op=OP.add),
         reads=[uA, dA], writes=[uA], tag="k_za")
    S.op("vector", lambda e: e.tensor_scalar(routf[0:8, 0:1536], routf[0:8, 3072:4608], 1.0, None,
                                             op0=OP.is_ge),
         reads=[gA], writes=[sA], tag="k_ms")
    S.op("vector", lambda e: e.tensor_tensor(hf[0:8, 2048:3584], routf[0:8, 4608:6144],
                                             routf[0:8, 1536:3072], op=OP.subtract),
         reads=[tA, uA], writes=[dA], tag="k_sd")
    S.op("vector", lambda e: e.tensor_tensor(hf[0:8, 2048:3584], hf[0:8, 2048:3584],
                                             routf[0:8, 0:1536], op=OP.mult),
         reads=[dA, sA], writes=[dA], tag="k_sm")
    S.op("vector", lambda e: e.tensor_tensor(routf[0:8, 1536:3072], routf[0:8, 1536:3072],
                                             hf[0:8, 2048:3584], op=OP.add),
         reads=[uA, dA], writes=[uA], tag="k_sa")

    if EMIT_ROUTERS < 4:
        return
    S.op("vector", lambda e: e.memset(rowf[0:8, 4:5], -31.0), writes=[CC(4)], tag="b_lo")
    S.op("vector", lambda e: e.memset(rowf[0:8, 5:6], 2.0), writes=[CC(5)], tag="b_hi")
    for _ in range(BISECT_ITERS):
        S.op("vector", lambda e: e.tensor_tensor(rowf[0:8, 6:7], rowf[0:8, 4:5], rowf[0:8, 5:6], op=OP.add),
             reads=[CC(4), CC(5)], writes=[CC(6)], tag="b_m0")
        S.op("vector", lambda e: e.tensor_scalar(rowf[0:8, 6:7], rowf[0:8, 6:7], 0.5, None, op0=OP.mult),
             reads=[CC(6)], writes=[CC(6)], tag="b_m1")
        S.op("vector", lambda e: e.tensor_scalar(routf[0:8, 3072:4608], routf[0:8, 1536:3072],
                                                 rowf[0:8, 6:7], None, op0=OP.is_ge),
             reads=[uA, CC(6)], writes=[gA], tag="b_ge")
        S.op("vector", lambda e: e.reduce_sum(rowf[0:8, 7:8], routf[0:8, 3072:4608], axis=AX.X),
             reads=[gA], writes=[CC(7)], tag="b_cnt")
        S.op("vector", lambda e: e.tensor_tensor(rowf[0:8, 8:9], rowf[0:8, 7:8], rowf[0:8, 11:12],
                                                 op=OP.is_ge),
             reads=[CC(7), CC(11)], writes=[CC(8)], tag="b_cmp")
        S.op("vector", lambda e: e.tensor_tensor(rowf[0:8, 9:10], rowf[0:8, 6:7], rowf[0:8, 4:5],
                                                 op=OP.subtract),
             reads=[CC(6), CC(4)], writes=[CC(9)], tag="b_d1")
        S.op("vector", lambda e: e.tensor_tensor(rowf[0:8, 9:10], rowf[0:8, 9:10], rowf[0:8, 8:9],
                                                 op=OP.mult),
             reads=[CC(9), CC(8)], writes=[CC(9)], tag="b_m1b")
        S.op("vector", lambda e: e.tensor_tensor(rowf[0:8, 4:5], rowf[0:8, 4:5], rowf[0:8, 9:10],
                                                 op=OP.add),
             reads=[CC(4), CC(9)], writes=[CC(4)], tag="b_slo")
        S.op("vector", lambda e: e.tensor_tensor(rowf[0:8, 9:10], rowf[0:8, 5:6], rowf[0:8, 6:7],
                                                 op=OP.subtract),
             reads=[CC(5), CC(6)], writes=[CC(9)], tag="b_d2")
        S.op("vector", lambda e: e.tensor_tensor(rowf[0:8, 9:10], rowf[0:8, 9:10], rowf[0:8, 8:9],
                                                 op=OP.mult),
             reads=[CC(9), CC(8)], writes=[CC(9)], tag="b_m2b")
        S.op("vector", lambda e: e.tensor_tensor(rowf[0:8, 5:6], rowf[0:8, 6:7], rowf[0:8, 9:10],
                                                 op=OP.add),
             reads=[CC(6), CC(9)], writes=[CC(5)], tag="b_shi")
    S.op("vector", lambda e: e.tensor_scalar(routf[0:8, 3072:4608], routf[0:8, 1536:3072],
                                             rowf[0:8, 4:5], None, op0=OP.is_ge),
         reads=[uA, CC(4)], writes=[gA], tag="b_fin")

    if EMIT_ROUTERS < 5:
        return
    for c in range(12):
        S.op("tensor", lambda e, c=c: e.transpose(B[6][:, 0:8], routf[0:8, 3072 + 128 * c:3072 + 128 * (c + 1)],
                                                  M["idf"][0:8, 0:8]),
             reads=[gA, CONST["idf"]], writes=[R("pb6", 0, 128, 0, 128)], tag="selT")
        S.op("vector", lambda e, c=c: e.tensor_copy(M["selcol"][:, 8 * c:8 * c + 8], B[6][:, 0:8]),
             reads=[R("pb6", 0, 128, 0, 128)],
             writes=[R("pb6", 0, 128, 0, 128), R("selcol", 0, 128, 8 * c, 8 * c + 8)], tag="selTe")



def _l0_attn(S, dt, xres, M, PB, CONST):
    xnT, hb, hf, rowf, qa, qx = M["xnT"], M["hb"], M["hf"], M["rowf"], M["qa"], M["qx"]
    rtb = M["rtb"]
    B = M["B"]
    KT0, V0, WQ0, QC0, PT0 = 0, 16384, 32768, 40960, 45056
    VW = 1024
    WKS, WVS = 16384, 40960

    HB = lambda c0, c1: R("hb", 0, 128, c0, c1)
    HFc = lambda c0, c1: R("hf", 0, 128, c0, c1)
    RF = lambda r0, r1, c0, c1: R("rowf", r0, r1, c0, c1)

    S.dma("gpsimd", lambda e: e.dma_start(hb[:, WKS:WKS + 8192], dt["wkT"].ap()),
          reads=[HB(WKS, WKS + 8192)], writes=[HB(WKS, WKS + 8192)], tag="ld_wk")
    S.dma("gpsimd", lambda e: e.dma_start(hb[:, WVS:WVS + 8192], dt["wvT"].ap()),
          reads=[HB(WVS, WVS + 8192)], writes=[HB(WVS, WVS + 8192)], tag="ld_wv")
    S.dma("gpsimd", lambda e: e.dma_start(hb[:, WQ0:WQ0 + 8192], dt["wqT"].ap()),
          reads=[HB(WQ0, WQ0 + 8192)], writes=[HB(WQ0, WQ0 + 8192)], tag="ld_wq")
    S.dma("sync", lambda e: e.dma_start(hf[:, 4128:4136], dt["bqc"].ap()),
          writes=[HFc(4128, 4136)], tag="ld_bq")
    S.dma("sync", lambda e: e.dma_start(hf[:, 4136:4144], dt["bkc"].ap()),
          writes=[HFc(4136, 4144)], tag="ld_bk")

    S.op("vector", lambda e: e.tensor_scalar(hf[:, 4144:4152], hf[:, 4128:4136], 1.0 / 32.0, None, op0=OP.mult),
         reads=[HFc(4128, 4136)], writes=[HFc(4144, 4152)], tag="bq32")
    S.op("vector", lambda e: e.memset(hf[:, 4185:4186], 36.04365158081055),
         writes=[HFc(4185, 4186)], tag="ln2c")
    S.op("vector", lambda e: e.tensor_scalar(hf[:, 4185:4186], hf[:, 4185:4186], 1.0, 1.8083065924656694e-06,
                                             op0=OP.mult, op1=OP.add),
         reads=[HFc(4185, 4186)], writes=[HFc(4185, 4186)], tag="ln2c2")

    xnT_all = R("xnT", 0, 128, 0, NDT * N)

    for mt in range(NDT):
        for ch in range(4):
            b = (mt * 4 + ch) % 4
            def mmk(e, mt=mt, ch=ch, b=b):
                last = None
                for j in range(NDT):
                    last = e.matmul(B[b][:, 0:512],
                                    hb[:, WKS + 1024 * j + 128 * mt:WKS + 1024 * j + 128 * (mt + 1)],
                                    xnT[:, N * j + 512 * ch:N * j + 512 * (ch + 1)],
                                    start=(j == 0), stop=(j == NDT - 1))
                return last
            S.op("tensor", mmk, reads=[HB(WKS, WKS + 8192), xnT_all], writes=[PB(b)], tag="kmm")
            S.op("scalar", lambda e, mt=mt, ch=ch, b=b: e.activation(
                hb[:, KT0 + 2048 * mt + 512 * ch:KT0 + 2048 * mt + 512 * (ch + 1)], B[b][:, 0:512],
                AF.Identity, bias=hf[:, 4136 + mt:4137 + mt]),
                reads=[PB(b), HFc(4136, 4144)],
                writes=[PB(b), HB(KT0 + 2048 * mt + 512 * ch, KT0 + 2048 * mt + 512 * (ch + 1))], tag="kev")
    Krect = HB(KT0, KT0 + 16384)

    for i in range(NT):
        for ch in range(2):
            b = (i * 2 + ch) % 4
            def mmv(e, i=i, ch=ch, b=b):
                last = None
                for j in range(NDT):
                    last = e.matmul(B[b][:, 0:512], xnT[:, N * j + 128 * i:N * j + 128 * (i + 1)],
                                    hb[:, WVS + 1024 * j + 512 * ch:WVS + 1024 * j + 512 * (ch + 1)],
                                    start=(j == 0), stop=(j == NDT - 1))
                return last
            S.op("tensor", mmv, reads=[HB(WVS, WVS + 8192), xnT_all], writes=[PB(b)], tag="vmm")
            S.op("vector", lambda e, i=i, ch=ch, b=b: e.tensor_copy(
                hb[:, V0 + VW * i + 512 * ch:V0 + VW * i + 512 * (ch + 1)], B[b][:, 0:512]),
                reads=[PB(b)],
                writes=[PB(b), HB(V0 + VW * i + 512 * ch, V0 + VW * i + 512 * (ch + 1))], tag="vev")
    Vrect = HB(V0, V0 + NT * VW)

    for mt in range(NDT):
        S.op("vector", lambda e, mt=mt: e.reduce_max(hf[:, 4152 + mt:4153 + mt],
                                                     hb[:, KT0 + 2048 * mt:KT0 + 2048 * (mt + 1)], axis=AX.X),
             reads=[HB(KT0 + 2048 * mt, KT0 + 2048 * (mt + 1))], writes=[HFc(4152 + mt, 4153 + mt)], tag="kmx")
        S.op("vector", lambda e, mt=mt: e.tensor_reduce(hf[:, 4160 + mt:4161 + mt],
                                                        hb[:, KT0 + 2048 * mt:KT0 + 2048 * (mt + 1)],
                                                        axis=AX.X, op=OP.min),
             reads=[HB(KT0 + 2048 * mt, KT0 + 2048 * (mt + 1))], writes=[HFc(4160 + mt, 4161 + mt)], tag="kmn")
    S.op("vector", lambda e: e.tensor_tensor(hf[:, 4168:4176], hf[:, 4152:4160], hf[:, 4160:4168], op=OP.add),
         reads=[HFc(4152, 4168)], writes=[HFc(4168, 4176)], tag="c0")
    S.op("vector", lambda e: e.tensor_scalar(hf[:, 4168:4176], hf[:, 4168:4176], 0.5, None, op0=OP.mult),
         reads=[HFc(4168, 4176)], writes=[HFc(4168, 4176)], tag="c1")
    S.op("vector", lambda e: e.tensor_tensor(hf[:, 4176:4184], hf[:, 4152:4160], hf[:, 4160:4168], op=OP.subtract),
         reads=[HFc(4152, 4168)], writes=[HFc(4176, 4184)], tag="r0")
    S.op("vector", lambda e: e.tensor_scalar(hf[:, 4176:4184], hf[:, 4176:4184], 0.5, None, op0=OP.mult),
         reads=[HFc(4176, 4184)], writes=[HFc(4176, 4184)], tag="r1")
    S.op("scalar", lambda e: e.activation(rtb[:, 64:72], hf[:, 4168:4176], AF.Copy),
         reads=[HFc(4168, 4176)], writes=[R("rtb", 0, 128, 64, 72)], tag="crcb")
    S.op("scalar", lambda e: e.activation(rtb[:, 72:80], hf[:, 4176:4184], AF.Copy),
         reads=[HFc(4176, 4184)], writes=[R("rtb", 0, 128, 72, 80)], tag="crrb")
    for ch in range(4):
        for j in range(NDT):
            sl = 0
            S.op("scalar", lambda e, ch=ch, j=j, sl=sl: e.activation(
                qa[:, sl:sl + 512], hb[:, KT0 + 2048 * j + 512 * ch:KT0 + 2048 * j + 512 * (ch + 1)], AF.Square),
                reads=[HB(KT0 + 2048 * j + 512 * ch, KT0 + 2048 * j + 512 * (ch + 1))],
                writes=[R("qa", 0, 128, sl, sl + 512)], tag="k2")
            S.op("tensor", lambda e, j=j, sl=sl: e.matmul(B[4][0:1, 0:512], M["ones_col"][:, 0:1],
                                                          qa[:, sl:sl + 512], start=(j == 0), stop=(j == NDT - 1)),
                 reads=[R("qa", 0, 128, sl, sl + 512), CONST["ones_col_bf"]],
                 writes=[R("pb4", 0, 1, 0, 512)], tag="k2mm")
        S.op("vector", lambda e, ch=ch: e.tensor_copy(rowf[0:1, 512 * ch:512 * (ch + 1)], B[4][0:1, 0:512]),
             reads=[R("pb4", 0, 1, 0, 512)], writes=[R("pb4", 0, 1, 0, 512), RF(0, 1, 512 * ch, 512 * (ch + 1))],
             tag="k2ev")
    S.op("vector", lambda e: e.reduce_max(rowf[0:1, 2048:2049], rowf[0:1, 0:2048], axis=AX.X),
         reads=[RF(0, 1, 0, 2048)], writes=[RF(0, 1, 2048, 2049)], tag="knm")
    S.op("scalar", lambda e: e.activation(rowf[0:1, 2048:2049], rowf[0:1, 2048:2049], AF.Sqrt),
         reads=[RF(0, 1, 2048, 2049)], writes=[RF(0, 1, 2048, 2049)], tag="kns")

    for qc in range(L0A_CHUNKS):
        q0 = 256 * qc
        ktmax = (q0 + 256) // 128
        Qrect = HB(QC0, QC0 + 2048)
        for mt in range(NDT):
            b = mt % 2
            def mmq(e, mt=mt, b=b, q0=q0):
                last = None
                for j in range(NDT):
                    last = e.matmul(B[b][:, 0:256],
                                    hb[:, WQ0 + 1024 * j + 128 * mt:WQ0 + 1024 * j + 128 * (mt + 1)],
                                    xnT[:, N * j + q0:N * j + q0 + 256],
                                    start=(j == 0), stop=(j == NDT - 1))
                return last
            S.op("tensor", mmq, reads=[HB(WQ0, WQ0 + 8192), xnT_all], writes=[R(f"pb{b}", 0, 128, 0, 256)],
                 tag="qmm")
            S.op("scalar", lambda e, mt=mt, b=b: e.activation(
                hb[:, QC0 + 256 * mt:QC0 + 256 * (mt + 1)], B[b][:, 0:256], AF.Identity,
                bias=hf[:, 4144 + mt:4145 + mt], scale=1.0 / 32.0),
                reads=[R(f"pb{b}", 0, 128, 0, 256), HFc(4144, 4152)],
                writes=[R(f"pb{b}", 0, 128, 0, 256), HB(QC0 + 256 * mt, QC0 + 256 * (mt + 1))], tag="qev")
        for j in range(NDT):
            S.op("scalar", lambda e, j=j: e.activation(qa[:, 0:256], hb[:, QC0 + 256 * j:QC0 + 256 * (j + 1)],
                                                       AF.Abs),
                 reads=[HB(QC0 + 256 * j, QC0 + 256 * (j + 1))], writes=[R("qa", 0, 128, 0, 256)], tag="qabs")
            S.op("scalar", lambda e, j=j: e.activation(qa[:, 256:512], hb[:, QC0 + 256 * j:QC0 + 256 * (j + 1)],
                                                       AF.Square),
                 reads=[HB(QC0 + 256 * j, QC0 + 256 * (j + 1))], writes=[R("qa", 0, 128, 256, 512)], tag="qsq")
            S.op("tensor", lambda e, j=j: e.matmul(B[4][0:1, 0:256], rtb[:, 64 + j:65 + j],
                                                   hb[:, QC0 + 256 * j:QC0 + 256 * (j + 1)],
                                                   start=(j == 0), stop=False),
                 reads=[HB(QC0 + 256 * j, QC0 + 256 * (j + 1)), R("rtb", 0, 128, 64, 72)],
                 writes=[R("pb4", 0, 1, 0, 256)], tag="bc")
            S.op("tensor", lambda e, j=j: e.matmul(B[4][0:1, 0:256], rtb[:, 72 + j:73 + j], qa[:, 0:256],
                                                   start=False, stop=(j == NDT - 1)),
                 reads=[R("qa", 0, 128, 0, 256), R("rtb", 0, 128, 72, 80)],
                 writes=[R("pb4", 0, 1, 0, 256)], tag="br")
            S.op("tensor", lambda e, j=j: e.matmul(B[3][0:1, 0:256], M["ones_col"][:, 0:1], qa[:, 256:512],
                                                   start=(j == 0), stop=(j == NDT - 1)),
                 reads=[R("qa", 0, 128, 256, 512), CONST["ones_col_bf"]],
                 writes=[R("pb3", 0, 1, 0, 256)], tag="bq2")
        S.op("vector", lambda e: e.tensor_copy(rowf[0:1, 2052:2308], B[4][0:1, 0:256]),
             reads=[R("pb4", 0, 1, 0, 256)], writes=[R("pb4", 0, 1, 0, 256), RF(0, 1, 2052, 2308)], tag="bsum")
        S.op("scalar", lambda e: e.activation(rowf[0:1, 2308:2564], B[3][0:1, 0:256], AF.Sqrt),
             reads=[R("pb3", 0, 1, 0, 256)], writes=[R("pb3", 0, 1, 0, 256), RF(0, 1, 2308, 2564)], tag="bqn")
        S.op("vector", lambda e: e.tensor_scalar(rowf[0:1, 2308:2564], rowf[0:1, 2308:2564],
                                                 rowf[0:1, 2048:2049], None, op0=OP.mult),
             reads=[RF(0, 1, 2308, 2564), RF(0, 1, 2048, 2049)], writes=[RF(0, 1, 2308, 2564)], tag="bcs")
        S.op("vector", lambda e: e.tensor_tensor(rowf[0:1, 2052:2308], rowf[0:1, 2052:2308],
                                                 rowf[0:1, 2308:2564], op=OP.min),
             reads=[RF(0, 1, 2052, 2308), RF(0, 1, 2308, 2564)], writes=[RF(0, 1, 2052, 2308)], tag="bmin")
        S.op("vector", lambda e: e.tensor_scalar(qx[0:1, 0:256], rowf[0:1, 2052:2308], -1.0, None, op0=OP.mult),
             reads=[RF(0, 1, 2052, 2308)], writes=[R("qx", 0, 1, 0, 256)], tag="qex")

        for kt in range(ktmax):
            b = 4 + (kt % 2)
            pbr = R(f"pb{b}", 0, 128, 0, 256)
            def mms(e, kt=kt, b=b):
                for j in range(NDT):
                    e.matmul(B[b][:, 0:256], hb[:, KT0 + 2048 * j + 128 * kt:KT0 + 2048 * j + 128 * (kt + 1)],
                             hb[:, QC0 + 256 * j:QC0 + 256 * (j + 1)], start=(j == 0), stop=False)
                return e.matmul(B[b][:, 0:256], M["ones_row"][0:1, 0:128], qx[0:1, 0:256],
                                start=False, stop=True)
            S.op("tensor", mms, reads=[Qrect, R("qx", 0, 1, 0, 256), CONST["ones_row_bf"], Krect],
                 writes=[pbr], tag="smm")
            S.op("scalar", lambda e, kt=kt, b=b: e.activation(
                hb[:, PT0 + 256 * kt:PT0 + 256 * (kt + 1)], B[b][:, 0:256], AF.Exp),
                reads=[pbr], writes=[pbr, HB(PT0 + 256 * kt, PT0 + 256 * (kt + 1))], tag="sexp")
            if 128 * kt + 127 >= q0:
                r_ = kt - (q0 // 512) * 4
                off = q0 % 512
                S.op("vector", lambda e, kt=kt, r_=r_, off=off: e.tensor_tensor(
                    hb[:, PT0 + 256 * kt:PT0 + 256 * (kt + 1)], hb[:, PT0 + 256 * kt:PT0 + 256 * (kt + 1)],
                    M["cm"][:, 512 * r_ + off:512 * r_ + off + 256], op=OP.mult),
                    reads=[HB(PT0 + 256 * kt, PT0 + 256 * (kt + 1)), CONST["cmask"]],
                    writes=[HB(PT0 + 256 * kt, PT0 + 256 * (kt + 1))], tag="smask")
        PTrect = HB(PT0, PT0 + 256 * ktmax)

        for qs in range(2):
            i_tok = 2 * qc + qs
            def mk_mo(vlo, bnk):
                def mo(e, vlo=vlo, bnk=bnk, qs=qs, ktmax=ktmax):
                    last = None
                    for kt in range(ktmax):
                        last = e.matmul(B[bnk][:, 0:512],
                                        hb[:, PT0 + 256 * kt + 128 * qs:PT0 + 256 * kt + 128 * (qs + 1)],
                                        hb[:, V0 + VW * kt + vlo:V0 + VW * kt + vlo + 512],
                                        start=(kt == 0), stop=(kt == ktmax - 1))
                    return last
                return mo
            S.op("tensor", mk_mo(0, 1), reads=[PTrect, Vrect], writes=[PB(1)], tag="ommA")
            S.op("tensor", mk_mo(512, 2), reads=[PTrect, Vrect], writes=[PB(2)], tag="ommB")
            def moden(e, qs=qs, ktmax=ktmax):
                last = None
                for kt in range(ktmax):
                    last = e.matmul(B[3][:, 0:1],
                                    hb[:, PT0 + 256 * kt + 128 * qs:PT0 + 256 * kt + 128 * (qs + 1)],
                                    M["ones_col"][:, 0:1], start=(kt == 0), stop=(kt == ktmax - 1))
                return last
            S.op("tensor", moden, reads=[PTrect, CONST["ones_col_bf"]], writes=[R("pb3", 0, 128, 0, 1)],
                 tag="ommD")
            S.op("scalar", lambda e: e.activation(hf[:, 4184:4185], B[3][:, 0:1], AF.Ln, scale=2.0 ** 52),
                 reads=[R("pb3", 0, 128, 0, 1)], writes=[R("pb3", 0, 128, 0, 1), HFc(4184, 4185)], tag="ln")
            S.op("scalar", lambda e: e.activation(hf[:, 4184:4185], hf[:, 4184:4185], AF.Exp, scale=-1.0,
                                                  bias=hf[:, 4185:4186]),
                 reads=[HFc(4184, 4186)], writes=[HFc(4184, 4185)], tag="exn")
            r = 1024 * (i_tok % 2)
            xr = R("hf", 0, 128, r, r + D)
            S.dma("sync", lambda e, i_tok=i_tok, r=r: e.dma_start(
                hf[:, r:r + D], dt["x"].ap()[128 * i_tok:128 * (i_tok + 1), :]),
                reads=[xr], writes=[xr], tag="ld_xacc")
            S.op("vector", lambda e, r=r: e.scalar_tensor_tensor(
                hf[:, r:r + 512], B[1][:, 0:512], hf[:, 4184:4185], hf[:, r:r + 512],
                op0=OP.mult, op1=OP.add),
                reads=[PB(1), HFc(4184, 4185), R("hf", 0, 128, r, r + 512)],
                writes=[PB(1), R("hf", 0, 128, r, r + 512)], tag="oaccA")
            S.op("vector", lambda e, r=r: e.scalar_tensor_tensor(
                hf[:, r + 512:r + D], B[2][:, 0:512], hf[:, 4184:4185], hf[:, r + 512:r + D],
                op0=OP.mult, op1=OP.add),
                reads=[PB(2), HFc(4184, 4185), R("hf", 0, 128, r + 512, r + D)],
                writes=[PB(2), R("hf", 0, 128, r + 512, r + D)], tag="oaccB")
            S.dma("sync", lambda e, i_tok=i_tok, r=r: e.dma_start(
                xres.ap()[128 * i_tok:128 * (i_tok + 1), :], hf[:, r:r + D]),
                reads=[xr], writes=[R("xres", i_tok, i_tok + 1, 0, 1)], tag="st_xacc")



def _l0_local(S, dt, xres, M, PB, CONST):
    xnT, hb, hf, rowf, stats, qx, ki = M["xnT"], M["hb"], M["hf"], M["rowf"], M["stats"], M["qx"], M["ki"]
    B = M["B"]
    HB = lambda c0, c1: R("hb", 0, 128, c0, c1)
    HB2 = HB
    HFr = lambda r0, r1, c0, c1: R("hf", r0, r1, c0, c1)
    RF = lambda r0, r1, c0, c1: R("rowf", r0, r1, c0, c1)
    STT = lambda c0, c1: R("stats", 0, 128, c0, c1)

    BMK = 43168
    S.dma("sync", lambda e: e.dma_start(hb[:, 43168:43168 + 4096], dt["bmask"].ap()),
          reads=[HB2(43168, 47264)], writes=[HB2(43168, 47264)], tag="ld_bm")
    WST = 0
    QH = 4096
    KH = 12288
    VR0, VR1 = 20480, 22560
    XLW = 24736
    PTL = 28832
    ATT = 32928
    ATTT = 34976
    TABC = 37024
    TABS = 39072
    CTB = 41120
    PRT = PTL

    S.op("vector", lambda e: e.memset(qx[0:1, 0:512], 0.0), reads=[R("qx", 0, 1, 0, 512)],
         writes=[R("qx", 0, 1, 0, 512)], tag="qx0")
    for i in range(NT):
        if i < 4:
            S.op("vector", lambda e, i=i: e.memset(stats[:, 112 + i:113 + i], 1.0),
                 writes=[STT(112 + i, 113 + i)], tag="cnt1")
        else:
            w = i // 4 - 1
            c = i % 4
            S.op("vector", lambda e, i=i, c=c, w=w: e.tensor_tensor(
                stats[:, 112 + i:113 + i], M["selcol"][:, 8 * c + 2 * w:8 * c + 2 * w + 1],
                M["selcol"][:, 8 * c + 2 * w + 1:8 * c + 2 * w + 2], op=OP.add),
                reads=[R("selcol", 0, 128, 8 * c, 8 * c + 8)], writes=[STT(112 + i, 113 + i)], tag="cnt2")
            S.op("vector", lambda e, i=i: e.tensor_scalar(stats[:, 112 + i:113 + i],
                                                          stats[:, 112 + i:113 + i], 1.0, 1.0,
                                                          op0=OP.mult, op1=OP.add),
                 reads=[STT(112 + i, 113 + i)], writes=[STT(112 + i, 113 + i)], tag="cnt3")
            S.op("vector", lambda e, i=i: e.reciprocal(stats[:, 112 + i:113 + i], stats[:, 112 + i:113 + i]),
                 reads=[STT(112 + i, 113 + i)], writes=[STT(112 + i, 113 + i)], tag="cnt4")

    S.op("tensor", lambda e: e.transpose(B[6][0:16, 0:128], stats[:, 80:96], M["idf"][:, :]),
         reads=[STT(80, 96), CONST["idf"]], writes=[R("pb6", 0, 128, 0, 128)], tag="ctT")
    S.op("vector", lambda e: e.tensor_copy(hf[0:16, 3584:3712], B[6][0:16, 0:128]),
         reads=[R("pb6", 0, 128, 0, 16)], writes=[R("pb6", 0, 128, 0, 16), HFr(0, 16, 3584, 3712)], tag="ctTe")
    S.dma("sync", lambda e: e.dma_start(hf[0:1, 0:2048], hf[0:16, 3584:3712]),
          reads=[HFr(0, 16, 3584, 3712), HFr(0, 1, 0, 2048)], writes=[HFr(0, 1, 0, 2048)], tag="ct_sc")
    for ch in range(4):
        S.op("tensor", lambda e, ch=ch: e.matmul(B[6][:, 0:512], M["ones_rf"][0:1, 0:128],
                                                 hf[0:1, 512 * ch:512 * (ch + 1)], start=True, stop=True),
             reads=[HFr(0, 1, 0, 2048), CONST["ones_row_f32"]], writes=[PB(6)], tag="ctbc")
        S.op("scalar", lambda e, ch=ch: e.activation(hb[:, CTB + 512 * ch:CTB + 512 * (ch + 1)],
                                                     B[6][:, 0:512], AF.Copy),
             reads=[PB(6)], writes=[PB(6), HB(CTB + 512 * ch, CTB + 512 * (ch + 1))], tag="ctbe")

    c13a = float(np.float32(13 * math.log(2)))
    c13b = float(np.float32(13 * math.log(2) - np.float32(13 * math.log(2))))
    S.op("vector", lambda e: e.memset(hf[:, 3600:3601], 1.220703125), writes=[HFr(0, 128, 3600, 3601)], tag="tb0")
    S.op("scalar", lambda e: e.activation(hf[:, 3600:3601], hf[:, 3600:3601], AF.Ln),
         reads=[HFr(0, 128, 3600, 3601)], writes=[HFr(0, 128, 3600, 3601)], tag="tb1")
    S.op("vector", lambda e: e.tensor_scalar(hf[:, 3600:3601], hf[:, 3600:3601], c13a, c13b,
                                             op0=OP.add, op1=OP.add),
         reads=[HFr(0, 128, 3600, 3601)], writes=[HFr(0, 128, 3600, 3601)], tag="tb2")
    S.dma("sync", lambda e: e.dma_start(hf[:, 3601:3602], dt["j128_col"].ap()),
          writes=[HFr(0, 128, 3601, 3602)], tag="tb3")
    S.op("vector", lambda e: e.tensor_tensor(hf[:, 3602:3603], hf[:, 3601:3602], hf[:, 3600:3601], op=OP.mult),
         reads=[HFr(0, 128, 3600, 3602)], writes=[HFr(0, 128, 3602, 3603)], tag="tb4")
    S.op("scalar", lambda e: e.activation(hf[:, 3602:3603], hf[:, 3602:3603], AF.Exp, scale=-1.0),
         reads=[HFr(0, 128, 3602, 3603)], writes=[HFr(0, 128, 3602, 3603)], tag="tb5")
    C1 = float(np.float32(2 * math.pi))
    DLT = float(np.float32(np.float64(np.float32(2 * math.pi)) - 2 * math.pi))
    INVC1 = float(np.float32(1.0 / (2 * math.pi)))
    HALF = float(np.float32(math.pi))
    PLO = float(np.nextafter(np.float32(math.pi), np.float32(0)))
    QPI = float(np.float32(math.pi / 2))
    for tgt, tcol, shift in ((TABC, 0, QPI), (TABS, 1, 0.0)):
        for ch in range(4):
            S.op("tensor", lambda e, ch=ch: e.matmul(B[6][:, 0:512], M["ones_rf"][0:1, 0:128],
                                                     M["pos"][0:1, 512 * ch:512 * (ch + 1)], start=True, stop=True),
                 reads=[CONST["pos_row"], CONST["ones_row_f32"]], writes=[PB(6)], tag="ph0")
            S.op("vector", lambda e, shift=shift: e.tensor_scalar(hf[:, 2048:2560], B[6][:, 0:512],
                                                                  hf[:, 3602:3603], shift,
                                                                  op0=OP.mult, op1=OP.add),
                 reads=[PB(6), HFr(0, 128, 3602, 3603)], writes=[PB(6), HFr(0, 128, 2048, 2560)], tag="ph1")
            if shift != 0.0:
                S.op("vector", lambda e: e.tensor_scalar(hf[:, 2048:2560], hf[:, 2048:2560],
                                                         -4.371139000186241e-08, None, op0=OP.add),
                     reads=[HFr(0, 128, 2048, 2560)], writes=[HFr(0, 128, 2048, 2560)], tag="ph1b")
            S.op("vector", lambda e: e.tensor_scalar(hf[:, 2560:3072], hf[:, 2048:2560], INVC1, None,
                                                     op0=OP.mult),
                 reads=[HFr(0, 128, 2048, 2560)], writes=[HFr(0, 128, 2560, 3072)], tag="ph2")
            S.op("vector", lambda e: e.tensor_copy(ki[:, 0:512], hf[:, 2560:3072]),
                 reads=[HFr(0, 128, 2560, 3072)], writes=[R("ki", 0, 128, 0, 512)], tag="ph3")
            S.op("vector", lambda e: e.tensor_copy(hf[:, 2560:3072], ki[:, 0:512]),
                 reads=[R("ki", 0, 128, 0, 512)], writes=[HFr(0, 128, 2560, 3072)], tag="ph4")
            S.op("vector", lambda e: e.tensor_scalar(hf[:, 3072:3584], hf[:, 2560:3072], C1, None,
                                                     op0=OP.mult),
                 reads=[HFr(0, 128, 2560, 3072)], writes=[HFr(0, 128, 3072, 3584)], tag="ph5")
            S.op("vector", lambda e: e.tensor_tensor(hf[:, 2048:2560], hf[:, 2048:2560], hf[:, 3072:3584],
                                                     op=OP.subtract),
                 reads=[HFr(0, 128, 2048, 2560), HFr(0, 128, 3072, 3584)],
                 writes=[HFr(0, 128, 2048, 2560)], tag="ph6")
            S.op("vector", lambda e: e.tensor_scalar(hf[:, 2560:3072], hf[:, 2560:3072], DLT, None,
                                                     op0=OP.mult),
                 reads=[HFr(0, 128, 2560, 3072)], writes=[HFr(0, 128, 2560, 3072)], tag="ph7")
            S.op("vector", lambda e: e.tensor_tensor(hf[:, 2048:2560], hf[:, 2048:2560], hf[:, 2560:3072],
                                                     op=OP.add),
                 reads=[HFr(0, 128, 2048, 2560), HFr(0, 128, 2560, 3072)],
                 writes=[HFr(0, 128, 2048, 2560)], tag="ph8")
            S.op("vector", lambda e: e.tensor_scalar(hf[:, 2560:3072], hf[:, 2048:2560], HALF, C1,
                                                     op0=OP.is_ge, op1=OP.mult),
                 reads=[HFr(0, 128, 2048, 2560)], writes=[HFr(0, 128, 2560, 3072)], tag="ph9")
            S.op("vector", lambda e: e.tensor_tensor(hf[:, 2048:2560], hf[:, 2048:2560], hf[:, 2560:3072],
                                                     op=OP.subtract),
                 reads=[HFr(0, 128, 2048, 2560), HFr(0, 128, 2560, 3072)],
                 writes=[HFr(0, 128, 2048, 2560)], tag="ph10")
            S.op("vector", lambda e: e.tensor_scalar(hf[:, 2048:2560], hf[:, 2048:2560], PLO, -PLO,
                                                     op0=OP.min, op1=OP.max),
                 reads=[HFr(0, 128, 2048, 2560)], writes=[HFr(0, 128, 2048, 2560)], tag="ph11")
            S.op("scalar", lambda e, tgt=tgt, ch=ch: e.activation(
                hb[:, tgt + 512 * ch:tgt + 512 * (ch + 1)], hf[:, 2048:2560], AF.Sin),
                reads=[HFr(0, 128, 2048, 2560)], writes=[HB(tgt + 512 * ch, tgt + 512 * (ch + 1))], tag="ph12")
    for r0 in (0, 64):
        S.op("vector", lambda e, r0=r0: e.tensor_scalar(hb[r0:r0 + 32, TABS:TABS + 2048],
                                                        hb[r0:r0 + 32, TABS:TABS + 2048], -1.0, None,
                                                        op0=OP.mult),
             reads=[HB(TABS, TABS + 2048)], writes=[HB(TABS, TABS + 2048)], tag="sneg")

    xnT_all = R("xnT", 0, 128, 0, NDT * N)

    def build_xlw(w):
        for j in range(NDT):
            S.op("vector", lambda e, w=w, j=j: e.tensor_tensor(
                hb[:, XLW + 512 * j:XLW + 512 * (j + 1)], xnT[:, N * j + 512 * w:N * j + 512 * (w + 1)],
                hb[:, CTB + 512 * w:CTB + 512 * (w + 1)], op=OP.mult),
                reads=[xnT_all, HB(CTB, CTB + 2048)], writes=[HB(XLW + 512 * j, XLW + 512 * (j + 1))],
                tag="xl0")
            S.op("scalar", lambda e, j=j: e.activation(
                hb[:, XLW + 512 * j:XLW + 512 * (j + 1)], hb[:, XLW + 512 * j:XLW + 512 * (j + 1)],
                AF.Identity, bias=hf[:, 4200 + 8 + j:4200 + 9 + j], scale=hf[:, 4200 + j:4200 + 1 + j]),
                reads=[HB(XLW + 512 * j, XLW + 512 * (j + 1)), HFr(0, 128, 4200, 4216)],
                writes=[HB(XLW + 512 * j, XLW + 512 * (j + 1))], tag="xl1")

    S.dma("sync", lambda e: e.dma_start(hf[:, 4200:4208], dt["lngc"].ap()),
          writes=[HFr(0, 128, 4200, 4208)], tag="ld_lng")
    S.dma("sync", lambda e: e.dma_start(hf[:, 4208:4216], dt["lnbc"].ap()),
          writes=[HFr(0, 128, 4208, 4216)], tag="ld_lnb")

    wq_ap = dt["wqkvT"].ap().rearrange("p (j o) -> p j o", j=8)

    for half in range(LOC_HALVES):
        hbase = 8 * half
        for kind, dst in (("q", QH), ("k", KH)):
            ocol = (0 if kind == "q" else 1024) + 512 * half
            S.dma("gpsimd", lambda e, ocol=ocol: e.dma_start(
                hb[:, WST:WST + 4096].rearrange("p (j o) -> p j o", j=8),
                wq_ap[:, :, ocol:ocol + 512]),
                reads=[HB(WST, WST + 4096)], writes=[HB(WST, WST + 4096)], tag="ld_wl")
            for w in range(NW):
                build_xlw(w)
                for mt in range(4):
                    b = mt % 2
                    def mmp(e, w=w, mt=mt, b=b):
                        last = None
                        for j in range(NDT):
                            last = e.matmul(B[b][:, 0:512],
                                            hb[:, WST + 512 * j + 128 * mt:WST + 512 * j + 128 * (mt + 1)],
                                            hb[:, XLW + 512 * j:XLW + 512 * (j + 1)],
                                            start=(j == 0), stop=(j == NDT - 1))
                        return last
                    S.op("tensor", mmp, reads=[HB(WST, WST + 4096), HB(XLW, XLW + 4096)],
                         writes=[PB(b)], tag="lqk_mm")
                    S.op("scalar", lambda e, dst=dst, mt=mt, w=w, b=b: e.activation(
                        hb[:, dst + 2048 * mt + 512 * w:dst + 2048 * mt + 512 * (w + 1)], B[b][:, 0:512],
                        AF.Copy),
                        reads=[PB(b)],
                        writes=[PB(b), HB(dst + 2048 * mt + 512 * w, dst + 2048 * mt + 512 * (w + 1))],
                        tag="lqk_ev")
            for sl in range(4):
                r0 = 32 * sl
                pr = 32 * (sl + 1 if sl % 2 == 0 else sl - 1)
                S.dma("sync", lambda e, dst=dst, r0=r0, pr=pr: e.dma_start(
                    hb[pr:pr + 32, PRT:PRT + 8192], hb[r0:r0 + 32, dst:dst + 8192]),
                    reads=[HB(dst, dst + 8192), R("hb", pr, pr + 32, PRT, PRT + 8192)],
                    writes=[R("hb", pr, pr + 32, PRT, PRT + 8192)], tag="rp_sw")
            for mt in range(4):
                sc = 0.125 if kind == "q" else 1.0
                S.op("vector", lambda e, dst=dst, mt=mt: e.tensor_tensor(
                    hb[:, PRT + 2048 * mt:PRT + 2048 * (mt + 1)], hb[:, PRT + 2048 * mt:PRT + 2048 * (mt + 1)],
                    hb[:, TABS:TABS + 2048], op=OP.mult),
                    reads=[R("hb", 0, 128, PRT + 2048 * mt, PRT + 2048 * (mt + 1)), HB(TABS, TABS + 2048)],
                    writes=[R("hb", 0, 128, PRT + 2048 * mt, PRT + 2048 * (mt + 1))], tag="rp_m1")
                S.op("vector", lambda e, dst=dst, mt=mt: e.tensor_tensor(
                    hb[:, dst + 2048 * mt:dst + 2048 * (mt + 1)], hb[:, dst + 2048 * mt:dst + 2048 * (mt + 1)],
                    hb[:, TABC:TABC + 2048], op=OP.mult),
                    reads=[HB(dst + 2048 * mt, dst + 2048 * (mt + 1)), HB(TABC, TABC + 2048)],
                    writes=[HB(dst + 2048 * mt, dst + 2048 * (mt + 1))], tag="rp_m2")
                def rp3(e, dst=dst, mt=mt, sc=sc):
                    if sc == 1.0:
                        return e.tensor_tensor(hb[:, dst + 2048 * mt:dst + 2048 * (mt + 1)],
                                               hb[:, dst + 2048 * mt:dst + 2048 * (mt + 1)],
                                               hb[:, PRT + 2048 * mt:PRT + 2048 * (mt + 1)], op=OP.add)
                    return e.scalar_tensor_tensor(hb[:, dst + 2048 * mt:dst + 2048 * (mt + 1)],
                                                  hb[:, dst + 2048 * mt:dst + 2048 * (mt + 1)], 1.0,
                                                  hb[:, PRT + 2048 * mt:PRT + 2048 * (mt + 1)],
                                                  op0=OP.mult, op1=OP.add)
                S.op("vector", rp3, reads=[HB(dst + 2048 * mt, dst + 2048 * (mt + 1)),
                                           R("hb", 0, 128, PRT + 2048 * mt, PRT + 2048 * (mt + 1))],
                     writes=[HB(dst + 2048 * mt, dst + 2048 * (mt + 1))], tag="rp_a")
                if sc != 1.0:
                    S.op("vector", lambda e, dst=dst, mt=mt: e.tensor_scalar(
                        hb[:, dst + 2048 * mt:dst + 2048 * (mt + 1)],
                        hb[:, dst + 2048 * mt:dst + 2048 * (mt + 1)], 0.125, None, op0=OP.mult),
                        reads=[HB(dst + 2048 * mt, dst + 2048 * (mt + 1))],
                        writes=[HB(dst + 2048 * mt, dst + 2048 * (mt + 1))], tag="rp_s")

        S.dma("gpsimd", lambda e, half=half: e.dma_start(
            hb[:, WST:WST + 4096].rearrange("p (j o) -> p j o", j=8),
            wq_ap[:, :, 2048 + 512 * half:2048 + 512 * (half + 1)]),
            reads=[HB(WST, WST + 4096)], writes=[HB(WST, WST + 4096)], tag="ld_wv2")

        def head_rows(h):
            mt = (h % 8) // 2
            r0 = 64 * (h % 2)
            return mt, r0

        for w in range(LOC_WINDOWS):
            vdst = VR0 if w % 2 == 0 else VR1
            build_xlw(w)
            for tt in range(4):
                b = tt % 2
                def mmv(e, tt=tt, b=b, w=w):
                    last = None
                    for j in range(NDT):
                        last = e.matmul(B[b][:, 0:512],
                                        hb[:, XLW + 512 * j + 128 * tt:XLW + 512 * j + 128 * tt + 128],
                                        hb[:, WST + 512 * j:WST + 512 * (j + 1)],
                                        start=(j == 0), stop=(j == NDT - 1))
                    return last
                S.op("tensor", mmv, reads=[HB(WST, WST + 4096), HB(XLW, XLW + 4096)],
                     writes=[PB(b)], tag="lv_mm")
                def evv(e, vdst=vdst, tt=tt, b=b):
                    return e.tensor_copy(
                        hb[:, vdst + 520 * tt:vdst + 520 * tt + 520].rearrange("p (h c) -> p h c", h=8)[:, :, 0:64],
                        B[b][:, 0:512].rearrange("p (h c) -> p h c", h=8))
                S.op("vector", evv, reads=[PB(b)],
                     writes=[PB(b), HB(vdst + 520 * tt, vdst + 520 * tt + 520)], tag="lv_ev")
                S.op("vector", lambda e, vdst=vdst, tt=tt: e.memset(
                    hb[:, vdst + 520 * tt:vdst + 520 * tt + 520].rearrange("p (h c) -> p h c", h=8)[:, :, 64:65], 1.0),
                    reads=[HB(vdst + 520 * tt, vdst + 520 * tt + 520)],
                    writes=[HB(vdst + 520 * tt, vdst + 520 * tt + 520)], tag="lv_on")

            for h8 in range(8):
                h = hbase + h8
                mt, r0 = head_rows(h8 + 0)
                qsl = QH + 2048 * mt + 512 * w
                ksl = KH + 2048 * mt + 512 * w
                S.op("scalar", lambda e, qsl=qsl, r0=r0: e.activation(
                    M["qa"][r0:r0 + 64, 0:512], hb[r0:r0 + 64, qsl:qsl + 512], AF.Square),
                    reads=[R("hb", r0, r0 + 64, qsl, qsl + 512)], writes=[R("qa", r0, r0 + 64, 0, 512)],
                    tag="lb_q2")
                S.op("tensor", lambda e, r0=r0: e.matmul(B[4][0:1, 0:512], M["ones_col"][r0:r0 + 64, 0:1],
                                                         M["qa"][r0:r0 + 64, 0:512], start=True, stop=True),
                     reads=[R("qa", r0, r0 + 64, 0, 512), CONST["ones_col_bf"]],
                     writes=[R("pb4", 0, 1, 0, 512)], tag="lb_qm")
                S.op("vector", lambda e, w=w, h8=h8: e.reduce_max(
                    rowf[0:1, 1100 + 32 * w + h8:1101 + 32 * w + h8], B[4][0:1, 0:512], axis=AX.X),
                    reads=[R("pb4", 0, 1, 0, 512)],
                    writes=[R("pb4", 0, 1, 0, 512), RF(0, 1, 1100 + 32 * w + h8, 1101 + 32 * w + h8)], tag="lb_qx")
                S.op("scalar", lambda e, ksl=ksl, r0=r0: e.activation(
                    M["qa"][r0:r0 + 64, 0:512], hb[r0:r0 + 64, ksl:ksl + 512], AF.Square),
                    reads=[R("hb", r0, r0 + 64, ksl, ksl + 512)], writes=[R("qa", r0, r0 + 64, 0, 512)],
                    tag="lb_k2")
                S.op("tensor", lambda e, r0=r0: e.matmul(B[4][0:1, 0:512], M["ones_col"][r0:r0 + 64, 0:1],
                                                         M["qa"][r0:r0 + 64, 0:512], start=True, stop=True),
                     reads=[R("qa", r0, r0 + 64, 0, 512), CONST["ones_col_bf"]],
                     writes=[R("pb4", 0, 1, 0, 512)], tag="lb_km")
                S.op("vector", lambda e, w=w, h8=h8: e.reduce_max(
                    rowf[0:1, 1100 + 16 + 32 * w + h8:1101 + 16 + 32 * w + h8], B[4][0:1, 0:512], axis=AX.X),
                    reads=[R("pb4", 0, 1, 0, 512)],
                    writes=[R("pb4", 0, 1, 0, 512), RF(0, 1, 1116 + 32 * w + h8, 1117 + 32 * w + h8)], tag="lb_kx")

            nkt = 4 if w == 0 else 8
            for h8 in range(8):
                mt, r0 = head_rows(h8)
                qsl = QH + 2048 * mt + 512 * w
                c0 = 1230 + 32 * w + h8
                def mkM(e, w=w, h8=h8, c0=c0):
                    return e.tensor_tensor(rowf[0:1, c0:c0 + 1], rowf[0:1, 1116 + 32 * w + h8:1117 + 32 * w + h8],
                                           rowf[0:1, 1116 + 32 * (w - 1) + h8:1117 + 32 * (w - 1) + h8]
                                           if w >= 1 else rowf[0:1, 1116 + 32 * w + h8:1117 + 32 * w + h8],
                                           op=OP.max)
                S.op("vector", mkM, reads=[RF(0, 1, 1100, 1230)], writes=[RF(0, 1, c0, c0 + 1)], tag="lM0")
                S.op("vector", lambda e, w=w, h8=h8, c0=c0: e.tensor_tensor(
                    rowf[0:1, c0:c0 + 1], rowf[0:1, c0:c0 + 1],
                    rowf[0:1, 1100 + 32 * w + h8:1101 + 32 * w + h8], op=OP.mult),
                    reads=[RF(0, 1, c0, c0 + 1), RF(0, 1, 1100, 1230)], writes=[RF(0, 1, c0, c0 + 1)], tag="lM1")
                S.op("scalar", lambda e, c0=c0: e.activation(rowf[0:1, c0:c0 + 1], rowf[0:1, c0:c0 + 1], AF.Sqrt),
                     reads=[RF(0, 1, c0, c0 + 1)], writes=[RF(0, 1, c0, c0 + 1)], tag="lM2")
                S.op("vector", lambda e, c0=c0: e.tensor_scalar(qx[0:1, 0:512], qx[0:1, 0:512], 0.0,
                                                                rowf[0:1, c0:c0 + 1],
                                                                op0=OP.mult, op1=OP.subtract),
                     reads=[R("qx", 0, 1, 0, 512), RF(0, 1, c0, c0 + 1)], writes=[R("qx", 0, 1, 0, 512)],
                     tag="lqex")
                for kt in range(nkt):
                    b = 4 + (kt % 2)
                    kw = w - 1 + kt // 4 if w >= 1 else w
                    kk_t = kt % 4
                    ksl = KH + 2048 * mt + 512 * kw + 128 * kk_t
                    def mms(e, ksl=ksl, qsl=qsl, r0=r0, b=b):
                        e.matmul(B[b][:, 0:512], hb[r0:r0 + 64, ksl:ksl + 128], hb[r0:r0 + 64, qsl:qsl + 512],
                                 start=True, stop=False)
                        return e.matmul(B[b][:, 0:512], M["ones_row"][0:1, 0:128], qx[0:1, 0:512],
                                        start=False, stop=True)
                    S.op("tensor", mms, reads=[HB(KH, KH + 8192), HB(QH, QH + 8192),
                                               R("qx", 0, 1, 0, 512), CONST["ones_row_bf"]],
                         writes=[PB(b)], tag="ls_mm")
                    S.op("scalar", lambda e, kt=kt, b=b: e.activation(
                        hb[:, PTL + 512 * kt:PTL + 512 * (kt + 1)], B[b][:, 0:512], AF.Exp),
                        reads=[PB(b)], writes=[PB(b), HB(PTL + 512 * kt, PTL + 512 * (kt + 1))], tag="ls_ex")
                    def mkmask(e, kt=kt, w=w):
                        if w == 0:
                            msl = M["cm"][:, 512 * kt:512 * (kt + 1)]
                        else:
                            msl = hb[:, BMK + 512 * kt:BMK + 512 * (kt + 1)]
                        return e.tensor_tensor(hb[:, PTL + 512 * kt:PTL + 512 * (kt + 1)],
                                               hb[:, PTL + 512 * kt:PTL + 512 * (kt + 1)], msl, op=OP.mult)
                    S.op("vector", mkmask,
                         reads=[HB(PTL + 512 * kt, PTL + 512 * (kt + 1)), CONST["cmask"], HB(BMK, BMK + 4096)],
                         writes=[HB(PTL + 512 * kt, PTL + 512 * (kt + 1))], tag="ls_mk")
                for qs in range(4):
                    def mo(e, qs=qs, nkt=nkt, w=w, h8=h8):
                        last = None
                        for kt in range(nkt):
                            kw = w - 1 + kt // 4 if w >= 1 else w
                            vdst = VR0 if kw % 2 == 0 else VR1
                            last = e.matmul(B[2][:, 0:65],
                                            hb[:, PTL + 512 * kt + 128 * qs:PTL + 512 * kt + 128 * (qs + 1)],
                                            hb[:, vdst + 520 * (kt % 4) + 65 * h8:vdst + 520 * (kt % 4) + 65 * (h8 + 1)],
                                            start=(kt == 0), stop=(kt == nkt - 1))
                        return last
                    S.op("tensor", mo, reads=[HB(PTL, PTL + 4096), HB(VR0, VR0 + 4160)],
                         writes=[R("pb2", 0, 128, 0, 65)], tag="lo_mm")
                    S.op("scalar", lambda e: e.activation(hf[:, 4184:4185], B[2][:, 64:65], AF.Ln,
                                                          scale=2.0 ** 52),
                         reads=[R("pb2", 0, 128, 0, 65)], writes=[HFr(0, 128, 4184, 4185)], tag="lo_ln")
                    S.op("scalar", lambda e: e.activation(hf[:, 4184:4185], hf[:, 4184:4185], AF.Exp,
                                                          scale=-1.0, bias=hf[:, 4185:4186]),
                         reads=[HFr(0, 128, 4184, 4186)], writes=[HFr(0, 128, 4184, 4185)], tag="lo_ex")
                    S.op("scalar", lambda e, qs=qs, h8=h8: e.activation(
                        hb[:, ATT + 512 * qs + 64 * h8:ATT + 512 * qs + 64 * (h8 + 1)], B[2][:, 0:64],
                        AF.Identity, scale=hf[:, 4184:4185]),
                        reads=[R("pb2", 0, 128, 0, 65), HFr(0, 128, 4184, 4185)],
                        writes=[R("pb2", 0, 128, 0, 65),
                                HB(ATT + 512 * qs + 64 * h8, ATT + 512 * qs + 64 * (h8 + 1))], tag="lo_ev")
            for tt in range(4):
                for jj in range(4):
                    po = 512 * (jj % 2)
                    S.op("tensor", lambda e, tt=tt, jj=jj, po=po: e.transpose(
                        M["PTB"][:, po:po + 128], hb[:, ATT + 512 * tt + 128 * jj:ATT + 512 * tt + 128 * (jj + 1)],
                        M["idb"][:, :]),
                        reads=[HB(ATT, ATT + 2048), CONST["idb"]], writes=[R("ptb", 0, 128, 0, 1024)],
                        tag="lt_T")
                    S.op("vector", lambda e, tt=tt, jj=jj, po=po: e.tensor_copy(
                        hb[:, ATTT + 512 * jj + 128 * tt:ATTT + 512 * jj + 128 * (tt + 1)], M["PTB"][:, po:po + 128]),
                        reads=[R("ptb", 0, 128, 0, 1024)],
                        writes=[R("ptb", 0, 128, 0, 1024),
                                HB(ATTT + 512 * jj + 128 * tt, ATTT + 512 * jj + 128 * (tt + 1))], tag="lt_e")
            if w == 0 and half == 0:
                pass
            S.dma("gpsimd", lambda e, half=half: e.dma_start(
                hb[:, XLW:XLW + 4096].rearrange("p (j o) -> p j o", j=4),
                dt["wloT"].ap().rearrange("p (j o) -> p j o", j=8)[:, 4 * half:4 * half + 4, :]),
                reads=[HB(XLW, XLW + 4096)], writes=[HB(XLW, XLW + 4096)], tag="ld_wlo")
            for tt in range(4):
                i_tok = 4 * w + tt
                r = 1024 * (i_tok % 2)
                xr = R("hf", 0, 128, r, r + D)
                for chv in range(2):
                    b = 2 + chv
                    def mmo2(e, tt=tt, chv=chv, b=b):
                        last = None
                        for jj in range(4):
                            last = e.matmul(B[b][:, 0:512],
                                            hb[:, ATTT + 512 * jj + 128 * tt:ATTT + 512 * jj + 128 * (tt + 1)],
                                            hb[:, XLW + 1024 * jj + 512 * chv:XLW + 1024 * jj + 512 * (chv + 1)],
                                            start=(jj == 0), stop=(jj == 3))
                        return last
                    S.op("tensor", mmo2, reads=[HB(ATTT, ATTT + 2048), HB(XLW, XLW + 4096)],
                         writes=[PB(b)], tag="lp_mm")
                src_x = dt["x"] if False else xres
                S.dma("sync", lambda e, i_tok=i_tok, r=r: e.dma_start(
                    hf[:, r:r + D], xres.ap()[128 * i_tok:128 * (i_tok + 1), :]),
                    reads=[xr, R("xres", i_tok, i_tok + 1, 0, 1)], writes=[xr], tag="lx_ld")
                for chv in range(2):
                    b = 2 + chv
                    S.op("vector", lambda e, r=r, chv=chv, b=b, i_tok=i_tok: e.scalar_tensor_tensor(
                        hf[:, r + 512 * chv:r + 512 * (chv + 1)], B[b][:, 0:512],
                        stats[:, 112 + i_tok:113 + i_tok], hf[:, r + 512 * chv:r + 512 * (chv + 1)],
                        op0=OP.mult, op1=OP.add),
                        reads=[PB(b), STT(112 + i_tok, 113 + i_tok), R("hf", 0, 128, r + 512 * chv, r + 512 * (chv + 1))],
                        writes=[PB(b), R("hf", 0, 128, r + 512 * chv, r + 512 * (chv + 1))], tag="lx_acc")
                S.dma("sync", lambda e, i_tok=i_tok, r=r: e.dma_start(
                    xres.ap()[128 * i_tok:128 * (i_tok + 1), :], hf[:, r:r + D]),
                    reads=[xr], writes=[R("xres", i_tok, i_tok + 1, 0, 1)], tag="lx_st")


def _l0_grouped(S, dt, xres, M, PB, CONST):
    xnT, hb, hf, rowf, stats, qa, qx = (M["xnT"], M["hb"], M["hf"], M["rowf"], M["stats"],
                                        M["qa"], M["qx"])
    B = M["B"]
    HB = lambda c0, c1: R("hb", 0, 128, c0, c1)
    HFr = lambda r0, r1, c0, c1: R("hf", r0, r1, c0, c1)
    RF = lambda r0, r1, c0, c1: R("rowf", r0, r1, c0, c1)
    STT = lambda c0, c1: R("stats", 0, 128, c0, c1)

    GK, GV, GQ = 0, 12288, 24768
    PT3 = 28864
    ATT3 = 35008
    WS3 = 39104
    WS3B = GQ
    PN3 = 43200
    NK3 = 43712
    NV3 = 43720
    IRMB = 44760

    for i in range(4, NT):
        w = i // 4 - 1
        for g in range(2):
            c = 3072 + 2 * (i - 4) + g
            S.op("vector", lambda e, i=i, w=w, g=g, c=c: e.tensor_tensor(
                hf[:, c:c + 1], M["selcol"][:, 8 * (i % 4) + 2 * w + g:8 * (i % 4) + 2 * w + g + 1],
                stats[:, 112 + i:113 + i], op=OP.mult),
                reads=[R("selcol", 0, 128, 8 * (i % 4), 8 * (i % 4) + 8), STT(112 + i, 113 + i)],
                writes=[HFr(0, 128, c, c + 1)], tag="gqw")

    S.op("tensor", lambda e: e.transpose(B[6][0:16, 0:128], stats[:, 96:112], M["idf"][:, :]),
         reads=[STT(96, 112), CONST["idf"]], writes=[R("pb6", 0, 128, 0, 128)], tag="irT")
    S.op("vector", lambda e: e.tensor_copy(hf[0:16, 3584:3712], B[6][0:16, 0:128]),
         reads=[R("pb6", 0, 128, 0, 128)],
         writes=[R("pb6", 0, 128, 0, 128), HFr(0, 16, 3584, 3712)], tag="irTe")
    S.dma("sync", lambda e: e.dma_start(hf[0:1, 0:2048], hf[0:16, 3584:3712]),
          reads=[HFr(0, 16, 3584, 3712), HFr(0, 1, 0, 2048)], writes=[HFr(0, 1, 0, 2048)], tag="ir_sc")
    for ch in range(4):
        S.op("tensor", lambda e, ch=ch: e.matmul(B[6][:, 0:512], M["ones_rf"][0:1, 0:128],
                                                 hf[0:1, 512 * ch:512 * (ch + 1)], start=True, stop=True),
             reads=[HFr(0, 1, 0, 2048), CONST["ones_row_f32"]], writes=[PB(6)], tag="irbc")
        S.op("scalar", lambda e, ch=ch: e.activation(hb[:, IRMB + 512 * ch:IRMB + 512 * (ch + 1)],
                                                     B[6][:, 0:512], AF.Copy),
             reads=[PB(6)], writes=[PB(6), HB(IRMB + 512 * ch, IRMB + 512 * (ch + 1))], tag="irbe")
    for j in range(NDT):
        for c in range(4):
            S.op("vector", lambda e, j=j, c=c: e.tensor_tensor(
                xnT[:, N * j + 512 * c:N * j + 512 * (c + 1)],
                xnT[:, N * j + 512 * c:N * j + 512 * (c + 1)],
                hb[:, IRMB + 512 * c:IRMB + 512 * (c + 1)], op=OP.mult),
                reads=[R("xnT", 0, 128, N * j + 512 * c, N * j + 512 * (c + 1)),
                       HB(IRMB + 512 * c, IRMB + 512 * (c + 1))],
                writes=[R("xnT", 0, 128, N * j + 512 * c, N * j + 512 * (c + 1))], tag="gxr")

    wkv_ap = dt["wgkvT"].ap().rearrange("g p (j o) -> g p j o", j=8)
    wq_ap = dt["wgqT"].ap().rearrange("g p (j o) -> g p j o", j=8)
    wo_ap = dt["wgoT"].ap().rearrange("g p (j o) -> g p j o", j=8)

    def load_ws(dst, src_ap, tag):
        S.dma("gpsimd", lambda e: e.dma_start(
            hb[:, dst:dst + 4096].rearrange("p (j o) -> p j o", j=8), src_ap),
            reads=[HB(dst, dst + 4096)], writes=[HB(dst, dst + 4096)], tag=tag)

    S.dma("sync", lambda e: e.dma_start(hf[:, 3232:3248], dt["qngc"].ap()),
          reads=[HFr(0, 128, 3232, 3248)], writes=[HFr(0, 128, 3232, 3248)], tag="ld_qng")
    S.dma("sync", lambda e: e.dma_start(hf[:, 3248:3264], dt["cngc"].ap()),
          reads=[HFr(0, 128, 3248, 3264)], writes=[HFr(0, 128, 3248, 3264)], tag="ld_cng")

    def gain_ws(gc0, g, tag):
        for j in range(NDT):
            c = gc0 + 8 * g + j
            S.op("scalar", lambda e, j=j, c=c: e.activation(
                hb[:, WS3 + 512 * j:WS3 + 512 * (j + 1)], hb[:, WS3 + 512 * j:WS3 + 512 * (j + 1)],
                AF.Identity, scale=hf[:, c:c + 1]),
                reads=[HB(WS3 + 512 * j, WS3 + 512 * (j + 1)), HFr(0, 128, c, c + 1)],
                writes=[HB(WS3 + 512 * j, WS3 + 512 * (j + 1))], tag=tag)

    for g in range(2):
        S.dma("gpsimd", lambda e, g=g: e.dma_start(hb[:, NK3:NK3 + 8], dt["nullk"].ap()[g]),
              reads=[HB(NK3, NK3 + 8)], writes=[HB(NK3, NK3 + 8)], tag="ld_nk")
        S.dma("gpsimd", lambda e, g=g: e.dma_start(hb[0:1, NV3:NV3 + 1040], dt["nullv_row"].ap()[g:g + 1, :]),
              reads=[R("hb", 0, 1, NV3, NV3 + 1040)], writes=[R("hb", 0, 1, NV3, NV3 + 1040)], tag="ld_nv")

        for kc in range(2):
            load_ws(WS3, wkv_ap[g][:, :, 512 * kc:512 * (kc + 1)], "ld_gkv")
            gain_ws(3248, g, "ggc")
            for m4 in range(4):
                mt = 4 * kc + m4
                for ch in range(3):
                    b = (mt + ch) % 2
                    def mmk(e, m4=m4, ch=ch, b=b):
                        last = None
                        for j in range(NDT):
                            last = e.matmul(B[b][:, 0:512],
                                            hb[:, WS3 + 512 * j + 128 * m4:WS3 + 512 * j + 128 * (m4 + 1)],
                                            xnT[:, N * j + 512 * ch:N * j + 512 * (ch + 1)],
                                            start=(j == 0), stop=(j == NDT - 1))
                        return last
                    S.op("tensor", mmk, reads=[HB(WS3, WS3 + 4096), R("xnT", 0, 128, 0, NDT * N)],
                         writes=[PB(b)], tag="gk_mm")
                    S.op("scalar", lambda e, mt=mt, ch=ch, b=b: e.activation(
                        hb[:, GK + 1536 * mt + 512 * ch:GK + 1536 * mt + 512 * (ch + 1)], B[b][:, 0:512],
                        AF.Copy),
                        reads=[PB(b)],
                        writes=[PB(b), HB(GK + 1536 * mt + 512 * ch, GK + 1536 * mt + 512 * (ch + 1))],
                        tag="gk_ev")
        for vc in range(2):
            load_ws(WS3, wkv_ap[g][:, :, 1024 + 512 * vc:1024 + 512 * (vc + 1)], "ld_gvw")
            gain_ws(3248, g, "ggv")
            for t in range(12):
                b = (t + vc) % 2
                def mmv3(e, t=t, b=b):
                    last = None
                    for j in range(NDT):
                        last = e.matmul(B[b][:, 0:512],
                                        xnT[:, N * j + 128 * t:N * j + 128 * (t + 1)],
                                        hb[:, WS3 + 512 * j:WS3 + 512 * (j + 1)],
                                        start=(j == 0), stop=(j == NDT - 1))
                    return last
                S.op("tensor", mmv3, reads=[HB(WS3, WS3 + 4096), R("xnT", 0, 128, 0, NDT * N)],
                     writes=[PB(b)], tag="gv_mm")
                def evv3(e, t=t, vc=vc, b=b):
                    return e.tensor_copy(
                        hb[:, GV + 1040 * t + 520 * vc:GV + 1040 * t + 520 * (vc + 1)]
                        .rearrange("p (h c) -> p h c", h=8)[:, :, 0:64],
                        B[b][:, 0:512].rearrange("p (h c) -> p h c", h=8))
                S.op("vector", evv3, reads=[PB(b)],
                     writes=[PB(b), HB(GV + 1040 * t + 520 * vc, GV + 1040 * t + 520 * (vc + 1))],
                     tag="gv_ev")
        for t in range(12):
            S.op("vector", lambda e, t=t: e.memset(
                hb[:, GV + 1040 * t:GV + 1040 * (t + 1)].rearrange("p (h c) -> p h c", h=16)[:, :, 64:65],
                1.0),
                reads=[HB(GV + 1040 * t, GV + 1040 * (t + 1))],
                writes=[HB(GV + 1040 * t, GV + 1040 * (t + 1))], tag="gv_on")

        for mt in range(NDT):
            for ch in range(3):
                S.op("scalar", lambda e, mt=mt, ch=ch: e.activation(
                    qa[:, 0:512], hb[:, GK + 1536 * mt + 512 * ch:GK + 1536 * mt + 512 * (ch + 1)],
                    AF.Square),
                    reads=[HB(GK + 1536 * mt + 512 * ch, GK + 1536 * mt + 512 * (ch + 1))],
                    writes=[R("qa", 0, 128, 0, 512)], tag="gkb_sq")
                for par in range(2):
                    r0 = 64 * par
                    h = 2 * mt + par
                    S.op("tensor", lambda e, r0=r0: e.matmul(
                        B[3][0:1, 0:512], M["ones_col"][r0:r0 + 64, 0:1], qa[r0:r0 + 64, 0:512],
                        start=True, stop=True),
                        reads=[R("qa", 0, 128, 0, 512), CONST["ones_col_bf"]],
                        writes=[R("pb3", 0, 1, 0, 512)], tag="gkb_mm")
                    S.op("vector", lambda e, h=h, ch=ch: e.reduce_max(
                        rowf[0:1, 1408 + 3 * h + ch:1409 + 3 * h + ch], B[3][0:1, 0:512], axis=AX.X),
                        reads=[R("pb3", 0, 1, 0, 512)],
                        writes=[R("pb3", 0, 1, 0, 512), RF(0, 1, 1408 + 3 * h + ch, 1409 + 3 * h + ch)],
                        tag="gkb_mx")
        S.op("scalar", lambda e: e.activation(qa[:, 0:8], hb[:, NK3:NK3 + 8], AF.Square),
             reads=[HB(NK3, NK3 + 8)], writes=[R("qa", 0, 128, 0, 8)], tag="gnk_sq")
        for par in range(2):
            r0 = 64 * par
            S.op("tensor", lambda e, r0=r0: e.matmul(B[3][0:1, 0:8], M["ones_col"][r0:r0 + 64, 0:1],
                                                     qa[r0:r0 + 64, 0:8], start=True, stop=True),
                 reads=[R("qa", 0, 128, 0, 8), CONST["ones_col_bf"]],
                 writes=[R("pb3", 0, 1, 0, 512)], tag="gnk_mm")
            S.op("vector", lambda e, par=par: e.tensor_copy(rowf[0:1, 1456 + 8 * par:1464 + 8 * par],
                                                            B[3][0:1, 0:8]),
                 reads=[R("pb3", 0, 1, 0, 512)],
                 writes=[R("pb3", 0, 1, 0, 512), RF(0, 1, 1456 + 8 * par, 1464 + 8 * par)], tag="gnk_ev")
        for h in range(16):
            nk2c = 1456 + 8 * (h % 2) + h // 2
            S.op("vector", lambda e, h=h, nk2c=nk2c: e.tensor_tensor(
                rowf[0:1, 1472 + 3 * h:1473 + 3 * h], rowf[0:1, 1408 + 3 * h:1409 + 3 * h],
                rowf[0:1, nk2c:nk2c + 1], op=OP.max),
                reads=[RF(0, 1, 1408, 1472)], writes=[RF(0, 1, 1472 + 3 * h, 1473 + 3 * h)], tag="gkm0")
            for wi in (1, 2):
                S.op("vector", lambda e, h=h, wi=wi: e.tensor_tensor(
                    rowf[0:1, 1472 + 3 * h + wi:1473 + 3 * h + wi],
                    rowf[0:1, 1472 + 3 * h + wi - 1:1473 + 3 * h + wi - 1],
                    rowf[0:1, 1408 + 3 * h + wi:1409 + 3 * h + wi], op=OP.max),
                    reads=[RF(0, 1, 1408, 1520)],
                    writes=[RF(0, 1, 1472 + 3 * h + wi, 1473 + 3 * h + wi)], tag="gkm1")

        for w in (1, 2, 3):
            if w == 3:
                for t in range(12):
                    S.op("vector", lambda e, t=t, g=g: e.tensor_scalar(
                        hb[:, GV + 1040 * t:GV + 1040 * (t + 1)], hb[:, GV + 1040 * t:GV + 1040 * (t + 1)],
                        M["selcol"][:, 8 * t + 6 + g:8 * t + 7 + g], None, op0=OP.mult),
                        reads=[HB(GV + 1040 * t, GV + 1040 * (t + 1)),
                               R("selcol", 0, 128, 8 * t, 8 * t + 8)],
                        writes=[HB(GV + 1040 * t, GV + 1040 * (t + 1))], tag="gv_mk")
            for qc in range(2):
                load_ws(WS3, wq_ap[g][:, :, 512 * qc:512 * (qc + 1)], "ld_gq")
                gain_ws(3232, g, "ggq")
                for m4 in range(4):
                    mt = 4 * qc + m4
                    b = (mt) % 2
                    def mmq(e, m4=m4, w=w, b=b):
                        last = None
                        for j in range(NDT):
                            last = e.matmul(B[b][:, 0:512],
                                            hb[:, WS3 + 512 * j + 128 * m4:WS3 + 512 * j + 128 * (m4 + 1)],
                                            xnT[:, N * j + 512 * w:N * j + 512 * (w + 1)],
                                            start=(j == 0), stop=(j == NDT - 1))
                        return last
                    S.op("tensor", mmq, reads=[HB(WS3, WS3 + 4096), R("xnT", 0, 128, 0, NDT * N)],
                         writes=[PB(b)], tag="gq_mm")
                    S.op("scalar", lambda e, mt=mt, b=b: e.activation(
                        hb[:, GQ + 512 * mt:GQ + 512 * (mt + 1)], B[b][:, 0:512], AF.Copy, scale=0.125),
                        reads=[PB(b)], writes=[PB(b), HB(GQ + 512 * mt, GQ + 512 * (mt + 1))], tag="gq_ev")
            for mt in range(NDT):
                S.op("scalar", lambda e, mt=mt: e.activation(qa[:, 0:512],
                                                             hb[:, GQ + 512 * mt:GQ + 512 * (mt + 1)],
                                                             AF.Square),
                     reads=[HB(GQ + 512 * mt, GQ + 512 * (mt + 1))], writes=[R("qa", 0, 128, 0, 512)],
                     tag="gqb_sq")
                for par in range(2):
                    r0 = 64 * par
                    h = 2 * mt + par
                    S.op("tensor", lambda e, r0=r0: e.matmul(
                        B[3][0:1, 0:512], M["ones_col"][r0:r0 + 64, 0:1], qa[r0:r0 + 64, 0:512],
                        start=True, stop=True),
                        reads=[R("qa", 0, 128, 0, 512), CONST["ones_col_bf"]],
                        writes=[R("pb3", 0, 1, 0, 512)], tag="gqb_mm")
                    S.op("vector", lambda e, h=h: e.reduce_max(
                        rowf[0:1, 1520 + h:1521 + h], B[3][0:1, 0:512], axis=AX.X),
                        reads=[R("pb3", 0, 1, 0, 512)],
                        writes=[R("pb3", 0, 1, 0, 512), RF(0, 1, 1520 + h, 1521 + h)], tag="gqb_mx")
            for h in range(16):
                S.op("vector", lambda e, h=h, w=w: e.tensor_tensor(
                    rowf[0:1, 1536 + h:1537 + h], rowf[0:1, 1520 + h:1521 + h],
                    rowf[0:1, 1472 + 3 * h + w - 1:1472 + 3 * h + w], op=OP.mult),
                    reads=[RF(0, 1, 1472, 1536)], writes=[RF(0, 1, 1536 + h, 1537 + h)], tag="gM0")
                S.op("scalar", lambda e, h=h: e.activation(rowf[0:1, 1536 + h:1537 + h],
                                                           rowf[0:1, 1536 + h:1537 + h], AF.Sqrt),
                     reads=[RF(0, 1, 1536 + h, 1537 + h)], writes=[RF(0, 1, 1536 + h, 1537 + h)],
                     tag="gM1")

            nkt = 4 * w
            for h in range(16):
                mt, r0 = h // 2, 64 * (h % 2)
                S.op("vector", lambda e, h=h: e.tensor_scalar(qx[0:1, 0:512], qx[0:1, 0:512], 0.0,
                                                              rowf[0:1, 1536 + h:1537 + h],
                                                              op0=OP.mult, op1=OP.subtract),
                     reads=[R("qx", 0, 1, 0, 512), RF(0, 1, 1536 + h, 1537 + h)],
                     writes=[R("qx", 0, 1, 0, 512)], tag="gqex")
                for kt in range(nkt):
                    b = 4 + (kt % 2)
                    def mms3(e, mt=mt, r0=r0, kt=kt, b=b):
                        e.matmul(B[b][:, 0:512], hb[r0:r0 + 64, GK + 1536 * mt + 128 * kt:GK + 1536 * mt + 128 * (kt + 1)],
                                 hb[r0:r0 + 64, GQ + 512 * mt:GQ + 512 * (mt + 1)], start=True, stop=False)
                        return e.matmul(B[b][:, 0:512], M["ones_row"][0:1, 0:128], qx[0:1, 0:512],
                                        start=False, stop=True)
                    S.op("tensor", mms3, reads=[HB(GK, GK + 12288), HB(GQ, GQ + 4096),
                                                R("qx", 0, 1, 0, 512), CONST["ones_row_bf"]],
                         writes=[PB(b)], tag="gs_mm")
                    S.op("scalar", lambda e, kt=kt, b=b: e.activation(
                        hb[:, PT3 + 512 * kt:PT3 + 512 * (kt + 1)], B[b][:, 0:512], AF.Exp),
                        reads=[PB(b)], writes=[PB(b), HB(PT3 + 512 * kt, PT3 + 512 * (kt + 1))],
                        tag="gs_ex")
                def mmn(e, mt=mt, r0=r0):
                    e.matmul(B[6][0:1, 0:512], hb[r0:r0 + 64, NK3 + mt:NK3 + mt + 1],
                             hb[r0:r0 + 64, GQ + 512 * mt:GQ + 512 * (mt + 1)], start=True, stop=False)
                    return e.matmul(B[6][0:1, 0:512], M["ones_row"][0:1, 0:1], qx[0:1, 0:512],
                                    start=False, stop=True)
                S.op("tensor", mmn, reads=[HB(NK3, NK3 + 8), HB(GQ, GQ + 4096),
                                           R("qx", 0, 1, 0, 512), CONST["ones_row_bf"]],
                     writes=[R("pb6", 0, 1, 0, 512)], tag="gn_mm")
                S.op("scalar", lambda e: e.activation(hb[0:1, PN3:PN3 + 512], B[6][0:1, 0:512], AF.Exp),
                     reads=[R("pb6", 0, 1, 0, 512)],
                     writes=[R("pb6", 0, 1, 0, 512), R("hb", 0, 1, PN3, PN3 + 512)], tag="gn_ex")
                for qs in range(4):
                    b = 2 + (qs + h) % 2
                    def mo3(e, qs=qs, h=h, nkt=nkt, b=b):
                        last = None
                        for kt in range(nkt):
                            last = e.matmul(B[b][:, 0:65],
                                            hb[:, PT3 + 512 * kt + 128 * qs:PT3 + 512 * kt + 128 * (qs + 1)],
                                            hb[:, GV + 1040 * kt + 65 * h:GV + 1040 * kt + 65 * (h + 1)],
                                            start=(kt == 0), stop=False)
                        return e.matmul(B[b][:, 0:65], hb[0:1, PN3 + 128 * qs:PN3 + 128 * (qs + 1)],
                                        hb[0:1, NV3 + 65 * h:NV3 + 65 * (h + 1)], start=False, stop=True)
                    S.op("tensor", mo3, reads=[HB(PT3, PT3 + 6144), HB(GV, GV + 12480),
                                               R("hb", 0, 1, PN3, PN3 + 512), R("hb", 0, 1, NV3, NV3 + 1040)],
                         writes=[R(f"pb{b}", 0, 128, 0, 512)], tag="go_mm")
                    dnc = 3104 + 16 * qs + h
                    S.op("vector", lambda e, dnc=dnc, b=b: e.tensor_copy(hf[:, dnc:dnc + 1],
                                                                         B[b][:, 64:65]),
                         reads=[R(f"pb{b}", 0, 128, 0, 512)],
                         writes=[R(f"pb{b}", 0, 128, 0, 512), HFr(0, 128, dnc, dnc + 1)], tag="go_dn")
                    S.op("scalar", lambda e, qs=qs, h=h, b=b: e.activation(
                        hb[:, ATT3 + 1024 * qs + 64 * h:ATT3 + 1024 * qs + 64 * (h + 1)], B[b][:, 0:64],
                        AF.Copy),
                        reads=[R(f"pb{b}", 0, 128, 0, 512)],
                        writes=[R(f"pb{b}", 0, 128, 0, 512),
                                HB(ATT3 + 1024 * qs + 64 * h, ATT3 + 1024 * qs + 64 * (h + 1))], tag="go_ev")
            S.op("scalar", lambda e: e.activation(hf[:, 3168:3232], hf[:, 3104:3168], AF.Ln,
                                                  scale=2.0 ** 52),
                 reads=[HFr(0, 128, 3104, 3168)], writes=[HFr(0, 128, 3168, 3232)], tag="gr_ln")
            S.op("scalar", lambda e: e.activation(hf[:, 3168:3232], hf[:, 3168:3232], AF.Exp, scale=-1.0,
                                                  bias=hf[:, 4185:4186]),
                 reads=[HFr(0, 128, 3168, 3232), HFr(0, 128, 4185, 4186)],
                 writes=[HFr(0, 128, 3168, 3232)], tag="gr_ex")
            for qs in range(4):
                for h in range(16):
                    rc = 3168 + 16 * qs + h
                    S.op("vector", lambda e, qs=qs, h=h, rc=rc: e.tensor_scalar(
                        hb[:, ATT3 + 1024 * qs + 64 * h:ATT3 + 1024 * qs + 64 * (h + 1)],
                        hb[:, ATT3 + 1024 * qs + 64 * h:ATT3 + 1024 * qs + 64 * (h + 1)],
                        hf[:, rc:rc + 1], None, op0=OP.mult),
                        reads=[HB(ATT3 + 1024 * qs + 64 * h, ATT3 + 1024 * qs + 64 * (h + 1)),
                               HFr(0, 128, rc, rc + 1)],
                        writes=[HB(ATT3 + 1024 * qs + 64 * h, ATT3 + 1024 * qs + 64 * (h + 1))],
                        tag="gr_sc")
            for qs in range(4):
                for ot in range(8):
                    b = (qs + ot) % 2
                    S.op("tensor", lambda e, qs=qs, ot=ot, b=b: e.matmul(
                        B[b][:, 0:128], hb[:, ATT3 + 1024 * qs + 128 * ot:ATT3 + 1024 * qs + 128 * (ot + 1)],
                        M["idb"][:, :], start=True, stop=True),
                        reads=[HB(ATT3, ATT3 + 4096), CONST["idb"]], writes=[PB(b)], tag="gt_T")
                    S.op("vector", lambda e, qs=qs, ot=ot, b=b: e.tensor_copy(
                        hb[:, PT3 + 512 * ot + 128 * qs:PT3 + 512 * ot + 128 * (qs + 1)], B[b][:, 0:128]),
                        reads=[PB(b)],
                        writes=[PB(b), HB(PT3 + 512 * ot + 128 * qs, PT3 + 512 * ot + 128 * (qs + 1))],
                        tag="gt_e")
            load_ws(WS3, wo_ap[g][:, :, 0:512], "ld_go0")
            load_ws(WS3B, wo_ap[g][:, :, 512:1024], "ld_go1")
            for tt in range(4):
                i_tok = 4 * w + tt
                r = 1024 * (i_tok % 2)
                xr = R("hf", 0, 128, r, r + D)
                for dc in range(2):
                    ws = WS3 if dc == 0 else WS3B
                    def mmo3(e, tt=tt, ws=ws, dc=dc):
                        last = None
                        for jj in range(NDT):
                            last = e.matmul(B[dc][:, 0:512],
                                            hb[:, PT3 + 512 * jj + 128 * tt:PT3 + 512 * jj + 128 * (tt + 1)],
                                            hb[:, ws + 512 * jj:ws + 512 * (jj + 1)],
                                            start=(jj == 0), stop=(jj == NDT - 1))
                        return last
                    S.op("tensor", mmo3, reads=[HB(PT3, PT3 + 6144), HB(WS3, WS3 + 4096),
                                                HB(WS3B, WS3B + 4096)],
                         writes=[PB(dc)], tag="gp_mm")
                S.dma("sync", lambda e, i_tok=i_tok, r=r: e.dma_start(
                    hf[:, r:r + D], xres.ap()[128 * i_tok:128 * (i_tok + 1), :]),
                    reads=[xr, R("xres", i_tok, i_tok + 1, 0, 1)], writes=[xr], tag="gx_ld")
                qwc = 3072 + 2 * (i_tok - 4) + g
                for dc in range(2):
                    S.op("vector", lambda e, r=r, dc=dc, qwc=qwc: e.scalar_tensor_tensor(
                        hf[:, r + 512 * dc:r + 512 * (dc + 1)], B[dc][:, 0:512],
                        hf[:, qwc:qwc + 1], hf[:, r + 512 * dc:r + 512 * (dc + 1)],
                        op0=OP.mult, op1=OP.add),
                        reads=[PB(dc), HFr(0, 128, qwc, qwc + 1),
                               R("hf", 0, 128, r + 512 * dc, r + 512 * (dc + 1))],
                        writes=[PB(dc), R("hf", 0, 128, r + 512 * dc, r + 512 * (dc + 1))], tag="gx_acc")
                S.dma("sync", lambda e, i_tok=i_tok, r=r: e.dma_start(
                    xres.ap()[128 * i_tok:128 * (i_tok + 1), :], hf[:, r:r + D]),
                    reads=[xr], writes=[R("xres", i_tok, i_tok + 1, 0, 1)], tag="gx_st")


def _bv_fold(S, dt, xres, M, PB, CONST):
    hf, rowf = M["hf"], M["rowf"]
    B = M["B"]
    RF = lambda r0, r1, c0, c1: R("rowf", r0, r1, c0, c1)
    S.dma("sync", lambda e: e.dma_start(rowf[0:1, 0:1024], dt["bv_row"].ap()),
          reads=[RF(0, 1, 0, 1024)], writes=[RF(0, 1, 0, 1024)], tag="ld_bv4")
    for half in range(2):
        S.op("tensor", lambda e, half=half: e.matmul(
            B[6][:, 0:512], M["ones_rf"][0:1, 0:128],
            rowf[0:1, 512 * half:512 * (half + 1)], start=True, stop=True),
            reads=[RF(0, 1, 0, 1024), CONST["ones_row_f32"]], writes=[PB(6)], tag="bvmm4")
        S.op("vector", lambda e, half=half: e.tensor_copy(
            hf[:, 2048 + 512 * half:2048 + 512 * (half + 1)], B[6][:, 0:512]),
            reads=[PB(6)], writes=[PB(6), R("hf", 0, 128, 2048 + 512 * half, 2048 + 512 * (half + 1))],
            tag="bvev4")
    for i in range(NT):
        r = 1024 * (i % 2)
        xr = R("hf", 0, 128, r, r + D)
        S.dma("sync", lambda e, i=i, r=r: e.dma_start(hf[:, r:r + D], xres.ap()[128 * i:128 * (i + 1), :]),
              reads=[xr, R("xres", i, i + 1, 0, 1)], writes=[xr], tag="bv_ld")
        S.op("vector", lambda e, r=r: e.tensor_tensor(hf[:, r:r + D], hf[:, r:r + D], hf[:, 2048:3072],
                                                      op=OP.add),
             reads=[xr, R("hf", 0, 128, 2048, 3072)], writes=[xr], tag="bv_add")
        S.dma("sync", lambda e, i=i, r=r: e.dma_start(xres.ap()[128 * i:128 * (i + 1), :], hf[:, r:r + D]),
              reads=[xr], writes=[R("xres", i, i + 1, 0, 1)], tag="bv_st")


def _l1_attn(S, dt, xres, M, PB, CONST):
    xnT, hb, hf, rowf, stats, qa = M["xnT"], M["hb"], M["hf"], M["rowf"], M["stats"], M["qa"]
    B = M["B"]
    HB = lambda c0, c1: R("hb", 0, 128, c0, c1)
    HBr = lambda r0, r1, c0, c1: R("hb", r0, r1, c0, c1)
    HFr = lambda r0, r1, c0, c1: R("hf", r0, r1, c0, c1)
    RF = lambda r0, r1, c0, c1: R("rowf", r0, r1, c0, c1)

    KP4 = 0
    V4 = 8192
    QP4 = 12352
    PT4 = 16448
    AT4 = 24640
    WL4 = 41024
    WKV4 = 45120
    SCR = 47168

    S.dma("sync", lambda e: e.dma_start(hf[0:1, 3300:3316], dt["phase_row"].ap()),
          reads=[HFr(0, 1, 3300, 3316)], writes=[HFr(0, 1, 3300, 3316)], tag="ld_ph")
    S.dma("sync", lambda e: e.dma_start(hf[0:1, 3316:3332], dt["als_row"].ap()),
          reads=[HFr(0, 1, 3316, 3332)], writes=[HFr(0, 1, 3316, 3332)], tag="ld_als")
    S.op("vector", lambda e: e.tensor_scalar(hf[0:1, 3300:3316], hf[0:1, 3300:3316], 1.5, -1.5,
                                             op0=OP.min, op1=OP.max),
         reads=[HFr(0, 1, 3300, 3316)], writes=[HFr(0, 1, 3300, 3316)], tag="phclamp")
    S.op("scalar", lambda e: e.activation(hf[0:1, 3332:3348], hf[0:1, 3300:3316], AF.Sin),
         reads=[HFr(0, 1, 3300, 3316)], writes=[HFr(0, 1, 3332, 3348)], tag="srow")
    S.op("vector", lambda e: e.memset(hf[0:1, 3364:3365], 1.5707963705062866),
         writes=[HFr(0, 1, 3364, 3365)], tag="hpi")
    S.op("vector", lambda e: e.tensor_scalar(hf[0:1, 3364:3365], hf[0:1, 3364:3365],
                                             1.0, -4.371139000186241e-08, op0=OP.mult, op1=OP.add),
         reads=[HFr(0, 1, 3364, 3365)], writes=[HFr(0, 1, 3364, 3365)], tag="hpib")
    S.op("scalar", lambda e: e.activation(hf[0:1, 3348:3364], hf[0:1, 3300:3316], AF.Sin,
                                          bias=hf[0:1, 3364:3365]),
         reads=[HFr(0, 1, 3300, 3316), HFr(0, 1, 3364, 3365)], writes=[HFr(0, 1, 3348, 3364)], tag="crow")
    S.op("scalar", lambda e: e.activation(hf[0:1, 3365:3381], hf[0:1, 3316:3332], AF.Exp),
         reads=[HFr(0, 1, 3316, 3332)], writes=[HFr(0, 1, 3365, 3381)], tag="slrow")
    S.op("vector", lambda e: e.tensor_scalar(hf[0:1, 3381:3397], hf[0:1, 3365:3381], -1.0, None,
                                             op0=OP.mult),
         reads=[HFr(0, 1, 3365, 3381)], writes=[HFr(0, 1, 3381, 3397)], tag="nslrow")
    S.op("vector", lambda e: e.tensor_scalar(hf[0:1, 3397:3413], hf[0:1, 3332:3348], -1.0, None,
                                             op0=OP.mult),
         reads=[HFr(0, 1, 3332, 3348)], writes=[HFr(0, 1, 3397, 3413)], tag="nsrow")
    for h in range(16):
        S.op("tensor", lambda e, h=h: e.matmul(B[6][:, 0:1], M["ones_rf"][0:1, 0:128],
                                               hf[0:1, 3348 + h:3349 + h], start=True, stop=True),
             reads=[HFr(0, 1, 3348, 3364), CONST["ones_row_f32"]], writes=[PB(6)], tag="cAm")
        S.op("vector", lambda e, h=h: e.tensor_copy(hf[:, 3420 + h:3421 + h], B[6][:, 0:1]),
             reads=[PB(6)], writes=[PB(6), HFr(0, 128, 3420 + h, 3421 + h)], tag="cAe")
        def cBm(e, h=h):
            e.matmul(B[6][0:32, 0:1], M["ones_rf"][0:1, 0:32], hf[0:1, 3397 + h:3398 + h],
                     start=True, stop=True)
            return e.matmul(B[6][32:64, 0:1], M["ones_rf"][0:1, 0:32], hf[0:1, 3332 + h:3333 + h],
                            start=True, stop=True)
        S.op("tensor", cBm, reads=[HFr(0, 1, 3332, 3348), HFr(0, 1, 3397, 3413), CONST["ones_row_f32"]],
             writes=[PB(6)], tag="cBm")
        S.op("vector", lambda e, h=h: e.tensor_copy(hf[0:64, 3440 + h:3441 + h], B[6][0:64, 0:1]),
             reads=[PB(6)], writes=[PB(6), HFr(0, 64, 3440 + h, 3441 + h)], tag="cBe")

    S.op("vector", lambda e: e.tensor_scalar(hf[:, 1160:1672], M["cm"][:, 0:512], 1e30, -1e30,
                                             op0=OP.mult, op1=OP.add),
         reads=[CONST["cmask"]], writes=[HFr(0, 128, 1160, 1672)], tag="amask")

    S.dma("gpsimd", lambda e: e.dma_start(hb[:, WL4:WL4 + 4096], dt["l1wq"].ap()),
          reads=[HB(WL4, WL4 + 4096)], writes=[HB(WL4, WL4 + 4096)], tag="ld_l1wq")
    S.dma("gpsimd", lambda e: e.dma_start(hb[:, WKV4:WKV4 + 1024], dt["l1wk"].ap()),
          reads=[HB(WKV4, WKV4 + 1024)], writes=[HB(WKV4, WKV4 + 1024)], tag="ld_l1wk")
    S.dma("gpsimd", lambda e: e.dma_start(hb[:, WKV4 + 1024:WKV4 + 2048], dt["l1wv"].ap()),
          reads=[HB(WKV4 + 1024, WKV4 + 2048)], writes=[HB(WKV4 + 1024, WKV4 + 2048)], tag="ld_l1wv")

    for t in range(NT):
        b = t % 2
        def mmv4(e, t=t, b=b):
            last = None
            for j in range(4):
                last = e.matmul(B[b][:, 0:256], xnT[:, N * j + 128 * t:N * j + 128 * (t + 1)],
                                hb[:, WKV4 + 1024 + 256 * j:WKV4 + 1024 + 256 * (j + 1)],
                                start=(j == 0), stop=(j == 3))
            return last
        S.op("tensor", mmv4, reads=[HB(WKV4 + 1024, WKV4 + 2048), R("xnT", 0, 128, 0, NDT * N)],
             writes=[PB(b)], tag="v4_mm")
        def evv4(e, t=t, b=b):
            return e.tensor_copy(
                hb[:, V4 + 260 * t:V4 + 260 * (t + 1)].rearrange("p (h c) -> p h c", h=4)[:, :, 0:64],
                B[b][:, 0:256].rearrange("p (h c) -> p h c", h=4))
        S.op("vector", evv4, reads=[PB(b)], writes=[PB(b), HB(V4 + 260 * t, V4 + 260 * (t + 1))],
             tag="v4_ev")
        S.op("vector", lambda e, t=t: e.memset(
            hb[:, V4 + 260 * t:V4 + 260 * (t + 1)].rearrange("p (h c) -> p h c", h=4)[:, :, 64:65], 1.0),
            reads=[HB(V4 + 260 * t, V4 + 260 * (t + 1))],
            writes=[HB(V4 + 260 * t, V4 + 260 * (t + 1))], tag="v4_on")

    S.op("vector", lambda e: e.memset(hb[0:1, SCR:SCR + 2048], 1.0),
         writes=[HBr(0, 1, SCR, SCR + 2048)], tag="onerow")
    for kvh in range(4):
        kp = KP4 + 2048 * kvh
        for c in range(4):
            b = c % 2
            def mmk4(e, kvh=kvh, c=c, b=b):
                last = None
                for j in range(4):
                    last = e.matmul(B[b][0:64, 0:512],
                                    hb[:, WKV4 + 256 * j + 64 * kvh:WKV4 + 256 * j + 64 * (kvh + 1)],
                                    xnT[:, N * (4 + j) + 512 * c:N * (4 + j) + 512 * (c + 1)],
                                    start=(j == 0), stop=(j == 3))
                return last
            S.op("tensor", mmk4, reads=[HB(WKV4, WKV4 + 1024), R("xnT", 0, 128, 0, NDT * N)],
                 writes=[PB(b)], tag="k4_mm")
            S.op("scalar", lambda e, kp=kp, c=c, b=b: e.activation(
                hb[0:64, kp + 512 * c:kp + 512 * (c + 1)], B[b][0:64, 0:512], AF.Copy),
                reads=[PB(b)], writes=[PB(b), HBr(0, 64, kp + 512 * c, kp + 512 * (c + 1))], tag="k4_ev")
        for rr in range(3):
            S.dma("sync", lambda e, kp=kp, rr=rr: e.dma_start(hb[64 + rr:65 + rr, kp:kp + 2048],
                                                              hb[0:1, SCR:SCR + 2048]),
                  reads=[HBr(0, 1, SCR, SCR + 2048), HBr(64 + rr, 65 + rr, kp, kp + 2048)],
                  writes=[HBr(64 + rr, 65 + rr, kp, kp + 2048)], tag="k4_on")
    for rr in range(3):
        S.dma("sync", lambda e, rr=rr: e.dma_start(hb[67 + rr:68 + rr, QP4:QP4 + 2048],
                                                   hb[0:1, SCR:SCR + 2048]),
              reads=[HBr(0, 1, SCR, SCR + 2048), HBr(67 + rr, 68 + rr, QP4, QP4 + 2048)],
              writes=[HBr(67 + rr, 68 + rr, QP4, QP4 + 2048)], tag="q4_on")

    def split3(vrect_fn, vcols, dstc, width, tag):
        vr = HFr(0, 1, vcols, vcols + width)
        t1 = HFr(0, 1, 2184, 2184 + width)
        for k in range(3):
            dr = HBr(0, 1, dstc + width * k, dstc + width * (k + 1))
            srcr = vr if k == 0 else t1
            src = (lambda e: hf[0:1, vcols:vcols + width]) if k == 0 else \
                  (lambda e: hf[0:1, 2184:2184 + width])
            S.op("vector", lambda e, k=k, src=src: e.tensor_copy(
                hb[0:1, dstc + width * k:dstc + width * (k + 1)], src(e)),
                reads=[srcr], writes=[dr], tag=tag + "h")
            if k < 2:
                S.op("vector", lambda e, k=k, src=src: e.tensor_tensor(
                    hf[0:1, 2184:2184 + width], src(e),
                    hb[0:1, dstc + width * k:dstc + width * (k + 1)], op=OP.subtract),
                    reads=[srcr, dr], writes=[t1], tag=tag + "l")

    nqc = 4
    for h in range(16):
        kvh = h // 4
        kp = KP4 + 2048 * kvh
        for ch in range(4):
            S.op("vector", lambda e, h=h, ch=ch: e.tensor_scalar(
                hf[0:1, 0:512], M["pos"][0:1, 512 * ch:512 * (ch + 1)],
                hf[0:1, 3381 + h:3382 + h], None, op0=OP.mult),
                reads=[CONST["pos_row"], HFr(0, 1, 3381, 3397)], writes=[HFr(0, 1, 0, 512)], tag="ckv")
            split3(None, 0, SCR, 512, "ck")
            S.dma("sync", lambda e, kp=kp, ch=ch: e.dma_start(
                hb[67:70, kp + 512 * ch:kp + 512 * (ch + 1)], hb[0:1, SCR:SCR + 1536]),
                reads=[HBr(0, 1, SCR, SCR + 1536), HBr(67, 70, kp + 512 * ch, kp + 512 * (ch + 1))],
                writes=[HBr(67, 70, kp + 512 * ch, kp + 512 * (ch + 1))], tag="ck_dma")
        for c in range(4):
            b = c % 2
            def mmq4(e, h=h, c=c, b=b):
                last = None
                for j in range(4):
                    last = e.matmul(B[b][0:64, 0:512],
                                    hb[:, WL4 + 1024 * j + 64 * h:WL4 + 1024 * j + 64 * (h + 1)],
                                    xnT[:, N * (4 + j) + 512 * c:N * (4 + j) + 512 * (c + 1)],
                                    start=(j == 0), stop=(j == 3))
                return last
            S.op("tensor", mmq4, reads=[HB(WL4, WL4 + 4096), R("xnT", 0, 128, 0, NDT * N)],
                 writes=[PB(b)], tag="q4_mm")
            S.op("scalar", lambda e, c=c, b=b: e.activation(
                hb[0:64, QP4 + 512 * c:QP4 + 512 * (c + 1)], B[b][0:64, 0:512], AF.Copy, scale=0.125),
                reads=[PB(b)], writes=[PB(b), HBr(0, 64, QP4 + 512 * c, QP4 + 512 * (c + 1))],
                tag="q4_ev")
        for sl in range(2):
            r0, pr = 32 * sl, 32 * (1 - sl)
            S.dma("sync", lambda e, r0=r0, pr=pr: e.dma_start(
                hb[pr:pr + 32, PT4:PT4 + 2048], hb[r0:r0 + 32, QP4:QP4 + 2048]),
                reads=[HBr(0, 64, QP4, QP4 + 2048), HBr(pr, pr + 32, PT4, PT4 + 2048)],
                writes=[HBr(pr, pr + 32, PT4, PT4 + 2048)], tag="q4_sw")
        S.op("vector", lambda e, h=h: e.tensor_scalar(
            hb[0:64, QP4:QP4 + 2048], hb[0:64, QP4:QP4 + 2048], hf[0:64, 3420 + h:3421 + h], None,
            op0=OP.mult),
            reads=[HBr(0, 64, QP4, QP4 + 2048), HFr(0, 64, 3420 + h, 3421 + h)],
            writes=[HBr(0, 64, QP4, QP4 + 2048)], tag="q4_rA")
        S.op("vector", lambda e, h=h: e.scalar_tensor_tensor(
            hb[0:64, QP4:QP4 + 2048], hb[0:64, PT4:PT4 + 2048], hf[0:64, 3440 + h:3441 + h],
            hb[0:64, QP4:QP4 + 2048], op0=OP.mult, op1=OP.add),
            reads=[HBr(0, 64, PT4, PT4 + 2048), HFr(0, 64, 3440 + h, 3441 + h),
                   HBr(0, 64, QP4, QP4 + 2048)],
            writes=[HBr(0, 64, QP4, QP4 + 2048)], tag="q4_rB")

        for qc in range(nqc):
            S.op("vector", lambda e, h=h, qc=qc: e.tensor_scalar(
                hf[0:1, 0:512], M["pos"][0:1, 512 * qc:512 * (qc + 1)],
                hf[0:1, 3365 + h:3366 + h], None, op0=OP.mult),
                reads=[CONST["pos_row"], HFr(0, 1, 3365, 3381)], writes=[HFr(0, 1, 0, 512)], tag="bqa")
            split3(None, 0, SCR, 512, "bqs")
            S.dma("sync", lambda e, qc=qc: e.dma_start(
                hb[64:67, QP4 + 512 * qc:QP4 + 512 * (qc + 1)], hb[0:1, SCR:SCR + 1536]),
                reads=[HBr(0, 1, SCR, SCR + 1536), HBr(64, 67, QP4 + 512 * qc, QP4 + 512 * (qc + 1))],
                writes=[HBr(64, 67, QP4 + 512 * qc, QP4 + 512 * (qc + 1))], tag="bq_dma")
            S.op("tensor", lambda e, kp=kp, qc=qc: e.matmul(
                B[4][:, 0:512], hb[0:70, kp:kp + 128], hb[0:70, QP4 + 512 * qc:QP4 + 512 * (qc + 1)],
                start=True, stop=True),
                reads=[HBr(0, 70, kp, kp + 2048), HBr(0, 70, QP4, QP4 + 2048)], writes=[PB(4)],
                tag="pA_mm")
            S.op("vector", lambda e: e.tensor_copy(hf[:, 1672:2184], B[4][:, 0:512]),
                 reads=[PB(4)], writes=[PB(4), HFr(0, 128, 1672, 2184)], tag="pA_ev")
            if qc == 0:
                S.op("vector", lambda e: e.tensor_tensor(hf[:, 1672:2184], hf[:, 1672:2184],
                                                         hf[:, 1160:1672], op=OP.add),
                     reads=[HFr(0, 128, 1672, 2184), HFr(0, 128, 1160, 1672)],
                     writes=[HFr(0, 128, 1672, 2184)], tag="pA_msk")
            for sub in range(4):
                S.op("tensor", lambda e, sub=sub: e.matmul(
                    B[3][:, 0:128], hf[:, 1672 + 128 * sub:1672 + 128 * (sub + 1)], M["idf"][:, :],
                    start=True, stop=True),
                    reads=[HFr(0, 128, 1672, 2184), CONST["idf"]], writes=[PB(3)], tag="mT")
                S.op("vector", lambda e, sub=sub: e.reduce_max(hf[:, 516 + sub:517 + sub],
                                                               B[3][:, 0:128], axis=AX.X),
                     reads=[PB(3)], writes=[PB(3), HFr(0, 128, 516 + sub, 517 + sub)], tag="mR")
            S.op("tensor", lambda e: e.matmul(B[3][0:4, 0:128], hf[:, 516:520], M["idf"][:, :],
                                              start=True, stop=True),
                 reads=[HFr(0, 128, 516, 520), CONST["idf"]], writes=[PB(3)], tag="mT2")
            S.op("vector", lambda e: e.tensor_copy(hf[0:4, 520:648], B[3][0:4, 0:128]),
                 reads=[PB(3)], writes=[PB(3), HFr(0, 4, 520, 648)], tag="mT2e")
            S.dma("sync", lambda e: e.dma_start(hf[0:1, 648:1160], hf[0:4, 520:648]),
                  reads=[HFr(0, 4, 520, 648), HFr(0, 1, 648, 1160)], writes=[HFr(0, 1, 648, 1160)],
                  tag="m_sc")
            S.op("vector", lambda e: e.tensor_tensor(hf[0:1, 0:512], hf[0:1, 0:512],
                                                     hf[0:1, 648:1160], op=OP.subtract),
                 reads=[HFr(0, 1, 0, 512), HFr(0, 1, 648, 1160)], writes=[HFr(0, 1, 0, 512)], tag="bqm")
            split3(None, 0, SCR, 512, "bqf")
            S.dma("sync", lambda e, qc=qc: e.dma_start(
                hb[64:67, QP4 + 512 * qc:QP4 + 512 * (qc + 1)], hb[0:1, SCR:SCR + 1536]),
                reads=[HBr(0, 1, SCR, SCR + 1536), HBr(64, 67, QP4 + 512 * qc, QP4 + 512 * (qc + 1))],
                writes=[HBr(64, 67, QP4 + 512 * qc, QP4 + 512 * (qc + 1))], tag="bq_dma2")
            for kt in range(4 * qc + 4):
                b = 4 + (kt % 2)
                S.op("tensor", lambda e, kp=kp, kt=kt, qc=qc, b=b: e.matmul(
                    B[b][:, 0:512], hb[0:70, kp + 128 * kt:kp + 128 * (kt + 1)],
                    hb[0:70, QP4 + 512 * qc:QP4 + 512 * (qc + 1)], start=True, stop=True),
                    reads=[HBr(0, 70, kp, kp + 2048), HBr(0, 70, QP4, QP4 + 2048)], writes=[PB(b)],
                    tag="s4_mm")
                S.op("scalar", lambda e, kt=kt, b=b: e.activation(
                    hb[:, PT4 + 512 * kt:PT4 + 512 * (kt + 1)], B[b][:, 0:512], AF.Exp),
                    reads=[PB(b)], writes=[PB(b), HB(PT4 + 512 * kt, PT4 + 512 * (kt + 1))], tag="s4_ex")
                if kt >= 4 * qc:
                    S.op("vector", lambda e, kt=kt, qc=qc: e.tensor_tensor(
                        hb[:, PT4 + 512 * kt:PT4 + 512 * (kt + 1)],
                        hb[:, PT4 + 512 * kt:PT4 + 512 * (kt + 1)],
                        M["cm"][:, 512 * (kt - 4 * qc):512 * (kt - 4 * qc + 1)], op=OP.mult),
                        reads=[HB(PT4 + 512 * kt, PT4 + 512 * (kt + 1)), CONST["cmask"]],
                        writes=[HB(PT4 + 512 * kt, PT4 + 512 * (kt + 1))], tag="s4_mk")
            obanks = (2, 3, 6, 0)
            for qs in range(4):
                b = obanks[qs]
                def mo4(e, qs=qs, qc=qc, kvh=kvh, b=b):
                    last = None
                    for kt in range(4 * qc + 4):
                        last = e.matmul(B[b][:, 0:65],
                                        hb[:, PT4 + 512 * kt + 128 * qs:PT4 + 512 * kt + 128 * (qs + 1)],
                                        hb[:, V4 + 260 * kt + 65 * kvh:V4 + 260 * kt + 65 * (kvh + 1)],
                                        start=(kt == 0), stop=(kt == 4 * qc + 3))
                    return last
                S.op("tensor", mo4, reads=[HB(PT4, PT4 + 8192), HB(V4, V4 + 4160)],
                     writes=[PB(b)], tag="o4_mm")
                S.op("vector", lambda e, qs=qs, b=b: e.tensor_scalar(
                    hf[:, 2696 + qs:2697 + qs], B[b][:, 64:65], 1.0, 1.0, op0=OP.mult, op1=OP.add),
                    reads=[PB(b)], writes=[PB(b), HFr(0, 128, 2696 + qs, 2697 + qs)], tag="o4_dn")
            S.op("scalar", lambda e: e.activation(hf[:, 2700:2704], hf[:, 2696:2700], AF.Ln,
                                                  scale=2.0 ** 52),
                 reads=[HFr(0, 128, 2696, 2700)], writes=[HFr(0, 128, 2700, 2704)], tag="r4_ln")
            S.op("scalar", lambda e: e.activation(hf[:, 2700:2704], hf[:, 2700:2704], AF.Exp, scale=-1.0,
                                                  bias=hf[:, 4185:4186]),
                 reads=[HFr(0, 128, 2700, 2704), HFr(0, 128, 4185, 4186)],
                 writes=[HFr(0, 128, 2700, 2704)], tag="r4_ex")
            for qs in range(4):
                b = obanks[qs]
                t = 4 * qc + qs
                S.op("vector", lambda e, qs=qs, t=t, h=h, b=b: e.tensor_scalar(
                    hb[:, AT4 + 1024 * t + 64 * h:AT4 + 1024 * t + 64 * (h + 1)], B[b][:, 0:64],
                    hf[:, 2700 + qs:2701 + qs], None, op0=OP.mult),
                    reads=[PB(b), HFr(0, 128, 2700 + qs, 2701 + qs)],
                    writes=[PB(b), HB(AT4 + 1024 * t + 64 * h, AT4 + 1024 * t + 64 * (h + 1))],
                    tag="o4_ev")

    S.dma("gpsimd", lambda e: e.dma_start(
        hb[:, WL4:WL4 + 4096].rearrange("p (j o) -> p j o", j=8),
        dt["l1wo"].ap().rearrange("p (j o) -> p j o", j=8)[:, :, 0:512]),
        reads=[HB(WL4, WL4 + 4096)], writes=[HB(WL4, WL4 + 4096)], tag="ld_l1wo0")
    S.dma("gpsimd", lambda e: e.dma_start(
        hb[:, KP4:KP4 + 4096].rearrange("p (j o) -> p j o", j=8),
        dt["l1wo"].ap().rearrange("p (j o) -> p j o", j=8)[:, :, 512:1024]),
        reads=[HB(KP4, KP4 + 4096)], writes=[HB(KP4, KP4 + 4096)], tag="ld_l1wo1")
    for t in range(NT):
        for oj in range(8):
            b = 4 + (oj % 2)
            S.op("tensor", lambda e, t=t, oj=oj, b=b: e.matmul(
                B[b][:, 0:128], hb[:, AT4 + 1024 * t + 128 * oj:AT4 + 1024 * t + 128 * (oj + 1)],
                M["idb"][:, :], start=True, stop=True),
                reads=[HB(AT4 + 1024 * t, AT4 + 1024 * (t + 1)), CONST["idb"]], writes=[PB(b)],
                tag="t4_T")
            S.op("vector", lambda e, oj=oj, b=b: e.tensor_copy(
                hb[:, PT4 + 128 * oj:PT4 + 128 * (oj + 1)], B[b][:, 0:128]),
                reads=[PB(b)], writes=[PB(b), HB(PT4 + 128 * oj, PT4 + 128 * (oj + 1))], tag="t4_e")
        r = 1024 * (t % 2)
        xr = R("hf", 0, 128, r, r + D)
        for dc in range(2):
            ws = WL4 if dc == 0 else KP4
            def mmo4(e, ws=ws, dc=dc):
                last = None
                for jj in range(8):
                    last = e.matmul(B[dc][:, 0:512], hb[:, PT4 + 128 * jj:PT4 + 128 * (jj + 1)],
                                    hb[:, ws + 512 * jj:ws + 512 * (jj + 1)],
                                    start=(jj == 0), stop=(jj == 7))
                return last
            S.op("tensor", mmo4, reads=[HB(PT4, PT4 + 1024), HB(WL4, WL4 + 4096), HB(KP4, KP4 + 4096)],
                 writes=[PB(dc)], tag="p4_mm")
        S.dma("sync", lambda e, t=t, r=r: e.dma_start(hf[:, r:r + D], xres.ap()[128 * t:128 * (t + 1), :]),
              reads=[xr, R("xres", t, t + 1, 0, 1)], writes=[xr], tag="x4_ld")
        for dc in range(2):
            S.op("vector", lambda e, r=r, dc=dc: e.tensor_tensor(
                hf[:, r + 512 * dc:r + 512 * (dc + 1)], B[dc][:, 0:512],
                hf[:, r + 512 * dc:r + 512 * (dc + 1)], op=OP.add),
                reads=[PB(dc), R("hf", 0, 128, r + 512 * dc, r + 512 * (dc + 1))],
                writes=[PB(dc), R("hf", 0, 128, r + 512 * dc, r + 512 * (dc + 1))], tag="x4_acc")
        S.dma("sync", lambda e, t=t, r=r: e.dma_start(xres.ap()[128 * t:128 * (t + 1), :], hf[:, r:r + D]),
              reads=[xr], writes=[R("xres", t, t + 1, 0, 1)], tag="x4_st")


def _l1_ffn_pre(S, dt, xres, M, PB, CONST):
    xnT, hb, hf, rowf, stats, qa = M["xnT"], M["hb"], M["hf"], M["rowf"], M["stats"], M["qa"]
    B = M["B"]
    HB = lambda c0, c1: R("hb", 0, 128, c0, c1)
    HBr = lambda r0, r1, c0, c1: R("hb", r0, r1, c0, c1)
    HFr = lambda r0, r1, c0, c1: R("hf", r0, r1, c0, c1)
    RF = lambda r0, r1, c0, c1: R("rowf", r0, r1, c0, c1)
    STT = lambda c0, c1: R("stats", 0, 128, c0, c1)

    XC5 = 0
    SO5 = 14400
    ET5 = 16448
    DTM = 20544
    SIT = 24640
    F5MU = 26688
    SOT = 28736
    CMS = 30784
    CMT = 31040
    CF5 = 35136
    W5A = 37184
    W5B = 39232
    W5C = 41280
    HT5 = 42304
    G2B = 42336
    BETB = 42352

    for i in range(NT):
        S.op("vector", lambda e, i=i: e.tensor_tensor(stats[:, 48 + i:49 + i], stats[:, 32 + i:33 + i],
                                                      stats[:, 80 + i:81 + i], op=OP.mult),
             reads=[STT(32, 48), STT(80, 96)], writes=[STT(48 + i, 49 + i)], tag="f_isc")
    for i in range(NT):
        r = 1024 * (i % 2)
        xr = R("hf", 0, 128, r, r + D)
        S.dma("sync", lambda e, i=i, r=r: e.dma_start(hf[:, r:r + D], xres.ap()[128 * i:128 * (i + 1), :]),
              reads=[xr, R("xres", i, i + 1, 0, 1)], writes=[xr], tag="f_xld")
        S.op("vector", lambda e, i=i, r=r: e.tensor_scalar(
            hb[:, XC5 + 1024 * i:XC5 + 1024 * (i + 1)], hf[:, r:r + D],
            stats[:, i:i + 1], stats[:, 48 + i:49 + i], op0=OP.subtract, op1=OP.mult),
            reads=[xr, STT(0, 16), STT(48, 64)],
            writes=[HB(XC5 + 1024 * i, XC5 + 1024 * (i + 1))], tag="f_xc")

    S.dma("sync", lambda e: e.dma_start(hf[:, 4128:4136], dt["f_gamc"].ap()),
          reads=[HFr(0, 128, 4128, 4136)], writes=[HFr(0, 128, 4128, 4136)], tag="f_ldg")
    S.dma("sync", lambda e: e.dma_start(hf[:, 4136:4144], dt["f_betc"].ap()),
          reads=[HFr(0, 128, 4136, 4144)], writes=[HFr(0, 128, 4136, 4144)], tag="f_ldb")
    S.dma("sync", lambda e: e.dma_start(hf[:, 4096:4128], dt["f_b1c"].ap()),
          reads=[HFr(0, 128, 4096, 4128)], writes=[HFr(0, 128, 4096, 4128)], tag="f_ldb1")
    S.dma("sync", lambda e: e.dma_start(hf[0:1, 2700:2701], dt["f_scale"].ap()),
          reads=[HFr(0, 1, 2700, 2701)], writes=[HFr(0, 1, 2700, 2701)], tag="f_ldsc")
    S.op("vector", lambda e: e.tensor_tensor(hb[:, G2B:G2B + 8], hf[:, 4128:4136], hf[:, 4128:4136],
                                             op=OP.mult),
         reads=[HFr(0, 128, 4128, 4136)], writes=[HB(G2B, G2B + 8)], tag="f_g2")
    S.op("vector", lambda e: e.tensor_tensor(hb[:, G2B + 8:G2B + 16], hf[:, 4128:4136], hf[:, 4136:4144],
                                             op=OP.mult),
         reads=[HFr(0, 128, 4128, 4144)], writes=[HB(G2B + 8, G2B + 16)], tag="f_gb")
    S.op("vector", lambda e: e.tensor_copy(hb[:, BETB:BETB + 8], hf[:, 4136:4144]),
         reads=[HFr(0, 128, 4136, 4144)], writes=[HB(BETB, BETB + 8)], tag="f_bb")

    for ch in range(4):
        for j in range(NDT):
            S.op("scalar", lambda e, ch=ch, j=j: e.activation(
                qa[:, 0:512], xnT[:, N * j + 512 * ch:N * j + 512 * (ch + 1)], AF.Square),
                reads=[R("xnT", 0, 128, 0, NDT * N)], writes=[R("qa", 0, 128, 0, 512)], tag="f_sq")
            S.op("tensor", lambda e, j=j: e.matmul(B[3][0:1, 0:512], hb[:, G2B + j:G2B + j + 1],
                                                   qa[:, 0:512], start=(j == 0), stop=(j == NDT - 1)),
                 reads=[R("qa", 0, 128, 0, 512), HB(G2B, G2B + 8)],
                 writes=[R("pb3", 0, 1, 0, 512)], tag="f_s2mm")
        S.op("vector", lambda e, ch=ch: e.tensor_copy(hf[0:1, 512 * ch:512 * (ch + 1)], B[3][0:1, 0:512]),
             reads=[R("pb3", 0, 1, 0, 512)],
             writes=[R("pb3", 0, 1, 0, 512), HFr(0, 1, 512 * ch, 512 * (ch + 1))], tag="f_s2e")
        def s1mm(e, ch=ch):
            last = None
            for j in range(NDT):
                last = e.matmul(B[6][0:1, 0:512], hb[:, G2B + 8 + j:G2B + 9 + j],
                                xnT[:, N * j + 512 * ch:N * j + 512 * (ch + 1)],
                                start=(j == 0), stop=(j == NDT - 1))
            return last
        S.op("tensor", s1mm, reads=[R("xnT", 0, 128, 0, NDT * N), HB(G2B + 8, G2B + 16)],
             writes=[R("pb6", 0, 1, 0, 512)], tag="f_s1mm")
        S.op("vector", lambda e, ch=ch: e.tensor_copy(hf[0:1, 2048 + 512 * ch:2048 + 512 * (ch + 1)],
                                                      B[6][0:1, 0:512]),
             reads=[R("pb6", 0, 1, 0, 512)],
             writes=[R("pb6", 0, 1, 0, 512), HFr(0, 1, 2048 + 512 * ch, 2048 + 512 * (ch + 1))],
             tag="f_s1e")
    S.op("vector", lambda e: e.tensor_tensor(qa[:, 0:8], hf[:, 4136:4144], hf[:, 4136:4144], op=OP.mult),
         reads=[HFr(0, 128, 4136, 4144)], writes=[R("qa", 0, 128, 0, 8)], tag="f_b2sq")
    S.op("tensor", lambda e: e.matmul(B[3][0:1, 0:8], M["ones_col"][:, 0:1], qa[:, 0:8],
                                      start=True, stop=True),
         reads=[R("qa", 0, 128, 0, 8), CONST["ones_col_bf"]], writes=[R("pb3", 0, 1, 0, 512)],
         tag="f_b2mm")
    S.op("vector", lambda e: e.reduce_sum(rowf[0:1, 2200:2201], B[3][0:1, 0:8], axis=AX.X),
         reads=[R("pb3", 0, 1, 0, 512)], writes=[R("pb3", 0, 1, 0, 512), RF(0, 1, 2200, 2201)],
         tag="f_b2s")
    S.op("tensor", lambda e: e.transpose(B[6][0:16, 0:128], stats[:, 80:96], M["idf"][:, :]),
         reads=[STT(80, 96), CONST["idf"]], writes=[R("pb6", 0, 128, 0, 128)], tag="f_ctT")
    S.op("vector", lambda e: e.tensor_copy(hf[0:16, 3584:3712], B[6][0:16, 0:128]),
         reads=[R("pb6", 0, 128, 0, 128)],
         writes=[R("pb6", 0, 128, 0, 128), HFr(0, 16, 3584, 3712)], tag="f_ctTe")
    S.dma("sync", lambda e: e.dma_start(rowf[0:1, 0:2048], hf[0:16, 3584:3712]),
          reads=[HFr(0, 16, 3584, 3712), RF(0, 1, 0, 2048)], writes=[RF(0, 1, 0, 2048)], tag="f_ctsc")
    S.op("vector", lambda e: e.tensor_tensor(hf[0:1, 0:2048], hf[0:1, 0:2048], rowf[0:1, 0:2048],
                                             op=OP.mult),
         reads=[HFr(0, 1, 0, 2048), RF(0, 1, 0, 2048)], writes=[HFr(0, 1, 0, 2048)], tag="f_n0")
    S.op("vector", lambda e: e.scalar_tensor_tensor(hf[0:1, 0:2048], hf[0:1, 2048:4096], 2.0,
                                                    hf[0:1, 0:2048], op0=OP.mult, op1=OP.add),
         reads=[HFr(0, 1, 0, 4096)], writes=[HFr(0, 1, 0, 2048)], tag="f_n1")
    S.op("vector", lambda e: e.tensor_tensor(hf[0:1, 0:2048], hf[0:1, 0:2048], rowf[0:1, 0:2048],
                                             op=OP.mult),
         reads=[HFr(0, 1, 0, 2048), RF(0, 1, 0, 2048)], writes=[HFr(0, 1, 0, 2048)], tag="f_n2")
    S.op("vector", lambda e: e.tensor_scalar(hf[0:1, 0:2048], hf[0:1, 0:2048], rowf[0:1, 2200:2201],
                                             None, op0=OP.add),
         reads=[HFr(0, 1, 0, 2048), RF(0, 1, 2200, 2201)], writes=[HFr(0, 1, 0, 2048)], tag="f_n3")
    S.op("scalar", lambda e: e.activation(hf[0:1, 0:2048], hf[0:1, 0:2048], AF.Sqrt),
         reads=[HFr(0, 1, 0, 2048)], writes=[HFr(0, 1, 0, 2048)], tag="f_n4")
    S.op("vector", lambda e: e.tensor_scalar(hf[0:1, 0:2048], hf[0:1, 0:2048], 1e-12, None, op0=OP.max),
         reads=[HFr(0, 1, 0, 2048)], writes=[HFr(0, 1, 0, 2048)], tag="f_n5")
    S.op("vector", lambda e: e.reciprocal(hf[0:1, 0:2048], hf[0:1, 0:2048]),
         reads=[HFr(0, 1, 0, 2048)], writes=[HFr(0, 1, 0, 2048)], tag="f_n6")
    S.op("vector", lambda e: e.tensor_scalar(hf[0:1, 0:2048], hf[0:1, 0:2048], hf[0:1, 2700:2701],
                                             None, op0=OP.mult),
         reads=[HFr(0, 1, 0, 2048), HFr(0, 1, 2700, 2701)], writes=[HFr(0, 1, 0, 2048)], tag="f_n8")
    for ch in range(4):
        S.op("tensor", lambda e, ch=ch: e.matmul(B[6][:, 0:512], M["ones_rf"][0:1, 0:128],
                                                 hf[0:1, 512 * ch:512 * (ch + 1)], start=True, stop=True),
             reads=[HFr(0, 1, 0, 2048), CONST["ones_row_f32"]], writes=[PB(6)], tag="f_cfm")
        S.op("scalar", lambda e, ch=ch: e.activation(hb[:, CF5 + 512 * ch:CF5 + 512 * (ch + 1)],
                                                     B[6][:, 0:512], AF.Copy),
             reads=[PB(6)], writes=[PB(6), HB(CF5 + 512 * ch, CF5 + 512 * (ch + 1))], tag="f_cfe")
        S.op("tensor", lambda e, ch=ch: e.matmul(B[6][:, 0:512], M["ones_rf"][0:1, 0:128],
                                                 rowf[0:1, 512 * ch:512 * (ch + 1)], start=True, stop=True),
             reads=[RF(0, 1, 0, 2048), CONST["ones_row_f32"]], writes=[PB(6)], tag="f_ctm")
        S.op("scalar", lambda e, ch=ch: e.activation(hb[:, W5A + 512 * ch:W5A + 512 * (ch + 1)],
                                                     B[6][:, 0:512], AF.Copy),
             reads=[PB(6)], writes=[PB(6), HB(W5A + 512 * ch, W5A + 512 * (ch + 1))], tag="f_cte")

    S.dma("gpsimd", lambda e: e.dma_start(hb[:, F5MU:F5MU + 2048], dt["f_mu"].ap()),
          reads=[HB(F5MU, F5MU + 2048)], writes=[HB(F5MU, F5MU + 2048)], tag="f_ldmu")
    for et in range(2):
        def mubm(e, et=et):
            last = None
            for j in range(NDT):
                last = e.matmul(B[3][:, 0:1], hb[:, F5MU + 256 * j + 128 * et:F5MU + 256 * j + 128 * (et + 1)],
                                hb[:, BETB + j:BETB + j + 1], start=(j == 0), stop=(j == NDT - 1))
            return last
        S.op("tensor", mubm, reads=[HB(F5MU, F5MU + 2048), HB(BETB, BETB + 8)],
             writes=[R("pb3", 0, 128, 0, 512)], tag="f_mbm")
        S.op("vector", lambda e, et=et: e.tensor_copy(hf[:, 4160 + et:4161 + et], B[3][:, 0:1]),
             reads=[R("pb3", 0, 128, 0, 512)],
             writes=[R("pb3", 0, 128, 0, 512), HFr(0, 128, 4160 + et, 4161 + et)], tag="f_mbe")
    for j in range(NDT):
        S.op("scalar", lambda e, j=j: e.activation(qa[:, 0:256], hb[:, F5MU + 256 * j:F5MU + 256 * (j + 1)],
                                                   AF.Square),
             reads=[HB(F5MU, F5MU + 2048)], writes=[R("qa", 0, 128, 0, 256)], tag="f_musq")
        S.op("tensor", lambda e, j=j: e.matmul(B[6][0:1, 0:256], M["ones_col"][:, 0:1], qa[:, 0:256],
                                               start=(j == 0), stop=(j == NDT - 1)),
             reads=[R("qa", 0, 128, 0, 256), CONST["ones_col_bf"]], writes=[R("pb6", 0, 1, 0, 512)],
             tag="f_mum")
    S.op("vector", lambda e: e.tensor_copy(rowf[0:1, 2204:2460], B[6][0:1, 0:256]),
         reads=[R("pb6", 0, 1, 0, 512)], writes=[R("pb6", 0, 1, 0, 512), RF(0, 1, 2204, 2460)],
         tag="f_mue")
    S.op("scalar", lambda e: e.activation(rowf[0:1, 2204:2460], rowf[0:1, 2204:2460], AF.Sqrt),
         reads=[RF(0, 1, 2204, 2460)], writes=[RF(0, 1, 2204, 2460)], tag="f_mus")
    S.op("vector", lambda e: e.tensor_scalar(rowf[0:1, 2204:2460], rowf[0:1, 2204:2460], 1e-12, None,
                                             op0=OP.max),
         reads=[RF(0, 1, 2204, 2460)], writes=[RF(0, 1, 2204, 2460)], tag="f_mux")
    S.op("vector", lambda e: e.reciprocal(rowf[0:1, 2204:2460], rowf[0:1, 2204:2460]),
         reads=[RF(0, 1, 2204, 2460)], writes=[RF(0, 1, 2204, 2460)], tag="f_mur")
    for et in range(2):
        S.op("tensor", lambda e, et=et: e.matmul(B[3][:, 0:1], rowf[0:1, 2204 + 128 * et:2204 + 128 * (et + 1)],
                                                 M["ones_rf"][0:1, 0:1], start=True, stop=True),
             reads=[RF(0, 1, 2204, 2460), CONST["ones_row_f32"]], writes=[R("pb3", 0, 128, 0, 512)],
             tag="f_mcm")
        S.op("vector", lambda e, et=et: e.tensor_copy(hf[:, 4162 + et:4163 + et], B[3][:, 0:1]),
             reads=[R("pb3", 0, 128, 0, 512)],
             writes=[R("pb3", 0, 128, 0, 512), HFr(0, 128, 4162 + et, 4163 + et)], tag="f_mce")
    for j in range(NDT):
        S.op("scalar", lambda e, j=j: e.activation(
            hb[:, F5MU + 256 * j:F5MU + 256 * (j + 1)], hb[:, F5MU + 256 * j:F5MU + 256 * (j + 1)],
            AF.Identity, scale=hf[:, 4128 + j:4129 + j]),
            reads=[HB(F5MU, F5MU + 2048), HFr(0, 128, 4128, 4136)],
            writes=[HB(F5MU, F5MU + 2048)], tag="f_mg")

    for et in range(2):
        for ch in range(4):
            b = (et * 4 + ch) % 2
            def gmm(e, et=et, ch=ch, b=b):
                last = None
                for j in range(NDT):
                    last = e.matmul(B[b][:, 0:512],
                                    hb[:, F5MU + 256 * j + 128 * et:F5MU + 256 * j + 128 * (et + 1)],
                                    xnT[:, N * j + 512 * ch:N * j + 512 * (ch + 1)],
                                    start=(j == 0), stop=(j == NDT - 1))
                return last
            S.op("tensor", gmm, reads=[HB(F5MU, F5MU + 2048), R("xnT", 0, 128, 0, NDT * N)],
                 writes=[PB(b)], tag="f_gmm")
            dst = 2048 * et + 512 * ch
            S.op("vector", lambda e, dst=dst, ch=ch, b=b: e.tensor_tensor(
                hf[:, dst:dst + 512], B[b][:, 0:512], hb[:, W5A + 512 * ch:W5A + 512 * (ch + 1)],
                op=OP.mult),
                reads=[PB(b), HB(W5A + 512 * ch, W5A + 512 * (ch + 1))],
                writes=[PB(b), HFr(0, 128, dst, dst + 512)], tag="f_gct")
            S.op("vector", lambda e, dst=dst, et=et, ch=ch: e.scalar_tensor_tensor(
                hf[:, dst:dst + 512], hf[:, dst:dst + 512], hf[:, 4160 + et:4161 + et],
                hb[:, CF5 + 512 * ch:CF5 + 512 * (ch + 1)], op0=OP.add, op1=OP.mult),
                reads=[HFr(0, 128, dst, dst + 512), HFr(0, 128, 4160 + et, 4161 + et),
                       HB(CF5 + 512 * ch, CF5 + 512 * (ch + 1))],
                writes=[HFr(0, 128, dst, dst + 512)], tag="f_gev")
            S.op("vector", lambda e, dst=dst, et=et: e.tensor_scalar(
                hf[:, dst:dst + 512], hf[:, dst:dst + 512], hf[:, 4162 + et:4163 + et], None,
                op0=OP.mult),
                reads=[HFr(0, 128, dst, dst + 512), HFr(0, 128, 4162 + et, 4163 + et)],
                writes=[HFr(0, 128, dst, dst + 512)], tag="f_gsc")

    for et in range(2):
        g0 = 2048 * et
        S.op("vector", lambda e, g0=g0, et=et: e.reduce_max(hf[:, 4164 + et:4165 + et],
                                                            hf[:, g0:g0 + 2048], axis=AX.X, negate=True),
             reads=[HFr(0, 128, g0, g0 + 2048)], writes=[HFr(0, 128, 4164 + et, 4165 + et)], tag="f_dmx")
        S.op("scalar", lambda e, g0=g0, et=et: e.activation(
            hb[:, ET5 + g0:ET5 + g0 + 2048], hf[:, g0:g0 + 2048], AF.Exp, bias=hf[:, 4164 + et:4165 + et]),
            reads=[HFr(0, 128, g0, g0 + 2048), HFr(0, 128, 4164 + et, 4165 + et)],
            writes=[HB(ET5 + g0, ET5 + g0 + 2048)], tag="f_dex")
        S.op("vector", lambda e, g0=g0, et=et: e.reduce_sum(hf[:, 4166 + et:4167 + et],
                                                            hb[:, ET5 + g0:ET5 + g0 + 2048], axis=AX.X),
             reads=[HB(ET5 + g0, ET5 + g0 + 2048)], writes=[HFr(0, 128, 4166 + et, 4167 + et)],
             tag="f_dsm")
        S.op("vector", lambda e, et=et: e.reciprocal(hf[:, 4166 + et:4167 + et], hf[:, 4166 + et:4167 + et]),
             reads=[HFr(0, 128, 4166 + et, 4167 + et)], writes=[HFr(0, 128, 4166 + et, 4167 + et)],
             tag="f_drc")
        S.op("vector", lambda e, g0=g0, et=et: e.tensor_scalar(
            hb[:, ET5 + g0:ET5 + g0 + 2048], hb[:, ET5 + g0:ET5 + g0 + 2048],
            hf[:, 4166 + et:4167 + et], None, op0=OP.mult),
            reads=[HB(ET5 + g0, ET5 + g0 + 2048), HFr(0, 128, 4166 + et, 4167 + et)],
            writes=[HB(ET5 + g0, ET5 + g0 + 2048)], tag="f_dnm")
        for t in range(NT):
            b = t % 2
            S.op("tensor", lambda e, g0=g0, t=t, b=b: e.matmul(
                B[b][:, 0:128], hb[:, ET5 + g0 + 128 * t:ET5 + g0 + 128 * (t + 1)], M["idb"][:, :],
                start=True, stop=True),
                reads=[HB(ET5 + g0, ET5 + g0 + 2048), CONST["idb"]], writes=[PB(b)], tag="f_dT")
            S.op("vector", lambda e, et=et, t=t, b=b: e.tensor_copy(
                hb[:, DTM + 256 * t + 128 * et:DTM + 256 * t + 128 * (et + 1)], B[b][:, 0:128]),
                reads=[PB(b)],
                writes=[PB(b), HB(DTM + 256 * t + 128 * et, DTM + 256 * t + 128 * (et + 1))], tag="f_dTe")

    for mt in range(NDT):
        b = mt % 2
        def simm(e, mt=mt, b=b):
            last = None
            for t in range(NT):
                last = e.matmul(B[b][:, 0:256], hb[:, XC5 + 1024 * t + 128 * mt:XC5 + 1024 * t + 128 * (mt + 1)],
                                hb[:, DTM + 256 * t:DTM + 256 * (t + 1)], start=(t == 0), stop=(t == NT - 1))
            return last
        S.op("tensor", simm, reads=[HB(XC5, XC5 + 16384), HB(DTM, DTM + 4096)], writes=[PB(b)],
             tag="f_simm")
        S.op("scalar", lambda e, mt=mt, b=b: e.activation(
            hb[:, SIT + 256 * mt:SIT + 256 * (mt + 1)], B[b][:, 0:256], AF.Identity,
            bias=hf[:, 4136 + mt:4137 + mt], scale=hf[:, 4128 + mt:4129 + mt]),
            reads=[PB(b), HFr(0, 128, 4128, 4144)],
            writes=[PB(b), HB(SIT + 256 * mt, SIT + 256 * (mt + 1))], tag="f_siev")

    for ex in range(16):
        S.dma("gpsimd", lambda e, ex=ex: e.dma_start(hb[:, W5A:W5A + 2048], dt["f_w1"].ap()[ex]),
              reads=[HB(W5A, W5A + 2048)], writes=[HB(W5A, W5A + 2048)], tag="f_lw1")
        S.dma("gpsimd", lambda e, ex=ex: e.dma_start(hb[:, W5B:W5B + 2048], dt["f_w2a"].ap()[ex]),
              reads=[HB(W5B, W5B + 2048)], writes=[HB(W5B, W5B + 2048)], tag="f_lw2")
        S.dma("gpsimd", lambda e, ex=ex: e.dma_start(hb[0:1, W5C:W5C + 1024], dt["f_w2b"].ap()[ex]),
              reads=[HBr(0, 1, W5C, W5C + 1024)], writes=[HBr(0, 1, W5C, W5C + 1024)], tag="f_lw2b")
        for ht in range(2):
            b = ht % 2
            def hmm(e, ex=ex, ht=ht, b=b):
                last = None
                for j in range(NDT):
                    last = e.matmul(B[b][:, 0:16],
                                    hb[:, W5A + 256 * j + 128 * ht:W5A + 256 * j + 128 * (ht + 1)],
                                    hb[:, SIT + 256 * j + 16 * ex:SIT + 256 * j + 16 * (ex + 1)],
                                    start=(j == 0), stop=(j == NDT - 1))
                return last
            S.op("tensor", hmm, reads=[HB(W5A, W5A + 2048), HB(SIT, SIT + 2048)], writes=[PB(b)],
                 tag="f_hmm")
            S.op("scalar", lambda e, ex=ex, ht=ht, b=b: e.activation(
                hb[:, HT5 + 16 * ht:HT5 + 16 * (ht + 1)], B[b][:, 0:16], AF.Gelu,
                bias=hf[:, 4096 + 2 * ex + ht:4097 + 2 * ex + ht]),
                reads=[PB(b), HFr(0, 128, 4096, 4128)],
                writes=[PB(b), HB(HT5 + 16 * ht, HT5 + 16 * (ht + 1))], tag="f_hev")
        for mt in range(NDT):
            b = mt % 2
            def somm(e, ex=ex, mt=mt, b=b):
                for ht in range(2):
                    e.matmul(B[b][:, 0:16], hb[:, W5B + 1024 * ht + 128 * mt:W5B + 1024 * ht + 128 * (mt + 1)],
                             hb[:, HT5 + 16 * ht:HT5 + 16 * (ht + 1)], start=(ht == 0), stop=False)
                return e.matmul(B[b][:, 0:16], hb[0:1, W5C + 128 * mt:W5C + 128 * (mt + 1)],
                                M["ones_row"][0:1, 0:16], start=False, stop=True)
            S.op("tensor", somm, reads=[HB(W5B, W5B + 2048), HB(HT5, HT5 + 32),
                                        HBr(0, 1, W5C, W5C + 1024), CONST["ones_row_bf"]],
                 writes=[PB(b)], tag="f_somm")
            S.op("vector", lambda e, ex=ex, mt=mt, b=b: e.tensor_copy(
                hb[:, SOT + 256 * mt + 16 * ex:SOT + 256 * mt + 16 * (ex + 1)], B[b][:, 0:16]),
                reads=[PB(b)], writes=[PB(b), HB(SOT + 256 * mt + 16 * ex, SOT + 256 * mt + 16 * (ex + 1))],
                tag="f_soev")

    for mt in range(NDT):
        for et in range(2):
            b = (mt + et) % 2
            S.op("tensor", lambda e, mt=mt, et=et, b=b: e.matmul(
                B[b][:, 0:128], hb[:, SOT + 256 * mt + 128 * et:SOT + 256 * mt + 128 * (et + 1)],
                M["idb"][:, :], start=True, stop=True),
                reads=[HB(SOT, SOT + 2048), CONST["idb"]], writes=[PB(b)], tag="f_sT")
            S.op("vector", lambda e, mt=mt, et=et, b=b: e.tensor_copy(
                hb[:, SO5 + 1024 * et + 128 * mt:SO5 + 1024 * et + 128 * (mt + 1)], B[b][:, 0:128]),
                reads=[PB(b)], writes=[PB(b), HB(SO5 + 1024 * et + 128 * mt, SO5 + 1024 * et + 128 * (mt + 1))],
                tag="f_sTe")

    for t in range(NT):
        for et in range(2):
            b = 2 + et
            S.op("tensor", lambda e, t=t, et=et, b=b: e.matmul(
                B[b][:, 0:128], hf[:, 2048 * et + 128 * t:2048 * et + 128 * (t + 1)], M["idf"][:, :],
                start=True, stop=True),
                reads=[HFr(0, 128, 0, 4096), CONST["idf"]], writes=[PB(b)], tag="f_cT")
        S.op("vector", lambda e: e.reduce_max(hf[:, 4168:4169], B[2][:, 0:128], axis=AX.X, negate=True),
             reads=[PB(2)], writes=[PB(2), HFr(0, 128, 4168, 4169)], tag="f_cm0")
        S.op("vector", lambda e: e.reduce_max(hf[:, 4169:4170], B[3][:, 0:128], axis=AX.X, negate=True),
             reads=[PB(3)], writes=[PB(3), HFr(0, 128, 4169, 4170)], tag="f_cm1")
        S.op("vector", lambda e: e.tensor_tensor(hf[:, 4168:4169], hf[:, 4168:4169], hf[:, 4169:4170],
                                                 op=OP.min),
             reads=[HFr(0, 128, 4168, 4170)], writes=[HFr(0, 128, 4168, 4169)], tag="f_cm2")
        for et in range(2):
            b = 2 + et
            S.op("scalar", lambda e, et=et, b=b: e.activation(
                hb[:, CMS + 128 * et:CMS + 128 * (et + 1)], B[b][:, 0:128], AF.Exp,
                bias=hf[:, 4168:4169]),
                reads=[PB(b), HFr(0, 128, 4168, 4169)],
                writes=[PB(b), HB(CMS + 128 * et, CMS + 128 * (et + 1))], tag="f_cex")
        S.op("vector", lambda e, t=t: e.reduce_sum(hf[:, 4096 + t:4097 + t], hb[:, CMS:CMS + 256],
                                                   axis=AX.X),
             reads=[HB(CMS, CMS + 256)], writes=[HFr(0, 128, 4096 + t, 4097 + t)], tag="f_csm")
        S.op("vector", lambda e, t=t: e.reciprocal(hf[:, 4096 + t:4097 + t], hf[:, 4096 + t:4097 + t]),
             reads=[HFr(0, 128, 4096 + t, 4097 + t)], writes=[HFr(0, 128, 4096 + t, 4097 + t)],
             tag="f_crc")
        for et in range(2):
            b = 4 + et
            S.op("tensor", lambda e, t=t, et=et, b=b: e.matmul(
                B[b][:, 0:128], hb[:, CMS + 128 * et:CMS + 128 * (et + 1)], M["idb"][:, :],
                start=True, stop=True),
                reads=[HB(CMS, CMS + 256), CONST["idb"]], writes=[PB(b)], tag="f_c2T")
            S.op("vector", lambda e, t=t, et=et, b=b: e.tensor_copy(
                hb[:, CMT + 2048 * et + 128 * t:CMT + 2048 * et + 128 * (t + 1)], B[b][:, 0:128]),
                reads=[PB(b)],
                writes=[PB(b), HB(CMT + 2048 * et + 128 * t, CMT + 2048 * et + 128 * (t + 1))],
                tag="f_c2e")
    S.dma("sync", lambda e: e.dma_start(dt["cmt_d"].ap(), hb[:, CMT:CMT + 4096]),
          reads=[HB(CMT, CMT + 4096)], writes=[R("cmt_d", 0, 128, 0, 4096)], tag="f_cst")


def _l1_ffn_out(S, dt, out_d, xres, M, PB, CONST):
    hb, hf = M["hb"], M["hf"]
    B = M["B"]
    HB = lambda c0, c1: R("hb", 0, 128, c0, c1)
    HFr = lambda r0, r1, c0, c1: R("hf", r0, r1, c0, c1)
    SO5 = 14400
    CMB = 0
    S.dma("sync", lambda e: e.dma_start(hb[:, CMB:CMB + 4096], dt["cmt_d"].ap()),
          reads=[R("cmt_d", 0, 128, 0, 4096), HB(CMB, CMB + 4096)], writes=[HB(CMB, CMB + 4096)],
          tag="f_cld")
    for t in range(NT):
        r = 1024 * (t % 2)
        xr = R("hf", 0, 128, r, r + D)
        for dc in range(2):
            def cmm(e, t=t, dc=dc):
                last = None
                for et in range(2):
                    last = e.matmul(B[dc][:, 0:512],
                                    hb[:, CMB + 2048 * et + 128 * t:CMB + 2048 * et + 128 * (t + 1)],
                                    hb[:, SO5 + 1024 * et + 512 * dc:SO5 + 1024 * et + 512 * (dc + 1)],
                                    start=(et == 0), stop=(et == 1))
                return last
            S.op("tensor", cmm, reads=[HB(CMB, CMB + 4096), HB(SO5, SO5 + 2048)], writes=[PB(dc)],
                 tag="f_omm")
        S.dma("sync", lambda e, t=t, r=r: e.dma_start(hf[:, r:r + D], xres.ap()[128 * t:128 * (t + 1), :]),
              reads=[xr, R("xres", t, t + 1, 0, 1)], writes=[xr], tag="f_oxld")
        for dc in range(2):
            S.op("vector", lambda e, t=t, r=r, dc=dc: e.scalar_tensor_tensor(
                hf[:, r + 512 * dc:r + 512 * (dc + 1)], B[dc][:, 0:512], hf[:, 4096 + t:4097 + t],
                hf[:, r + 512 * dc:r + 512 * (dc + 1)], op0=OP.mult, op1=OP.add),
                reads=[PB(dc), HFr(0, 128, 4096 + t, 4097 + t),
                       R("hf", 0, 128, r + 512 * dc, r + 512 * (dc + 1))],
                writes=[PB(dc), R("hf", 0, 128, r + 512 * dc, r + 512 * (dc + 1))], tag="f_oacc")
        S.dma("sync", lambda e, t=t, r=r: e.dma_start(out_d.ap()[128 * t:128 * (t + 1), :], hf[:, r:r + D]),
              reads=[xr], writes=[R("outd", t, t + 1, 0, 1)], tag="f_ost")


def _finish(S, dt, out_d, xres, M, PB, CONST):
    hf, rowf, qx = M["hf"], M["rowf"], M["qx"]
    B = M["B"]
    RF = lambda r0, r1, c0, c1: R("rowf", r0, r1, c0, c1)
    add_bv = BUILD_UPTO in (1, 2, 3)
    if add_bv:
        S.dma("sync", lambda e: e.dma_start(rowf[0:1, 0:1024], dt["bv_row"].ap()),
              reads=[RF(0, 1, 0, 1024)], writes=[RF(0, 1, 0, 1024)], tag="ld_bv")
        for half in range(2):
            S.op("tensor", lambda e, half=half: e.matmul(
                B[6][:, 0:512], M["ones_rf"][0:1, 0:128],
                rowf[0:1, 512 * half:512 * (half + 1)], start=True, stop=True),
                reads=[RF(0, 1, 0, 1024), CONST["ones_row_f32"]], writes=[PB(6)], tag="bvmm")
            S.op("vector", lambda e, half=half: e.tensor_copy(
                hf[:, 2048 + 512 * half:2048 + 512 * (half + 1)], B[6][:, 0:512]),
                reads=[PB(6)], writes=[PB(6), R("hf", 0, 128, 2048 + 512 * half, 2048 + 512 * (half + 1))],
                tag="bvev")
    for i in range(NT if BUILD_UPTO < 5 else 0):
        r = 1024 * (i % 2)
        xr = R("hf", 0, 128, r, r + D)
        S.dma("sync", lambda e, i=i, r=r: e.dma_start(hf[:, r:r + D], xres.ap()[128 * i:128 * (i + 1), :]),
              reads=[xr, R("xres", i, i + 1, 0, 1)], writes=[xr], tag="ld_xfin")
        if add_bv:
            S.op("vector", lambda e, r=r: e.tensor_tensor(hf[:, r:r + D], hf[:, r:r + D], hf[:, 2048:3072],
                                                          op=OP.add),
                 reads=[xr, R("hf", 0, 128, 2048, 3072)], writes=[xr], tag="bvadd")
        S.dma("sync", lambda e, i=i, r=r: e.dma_start(out_d.ap()[128 * i:128 * (i + 1), :], hf[:, r:r + D]),
              reads=[xr], writes=[R("outd", i, i + 1, 0, 1)], tag="st_out")
    S.op("sync", lambda e: e.nop(), reads=[R("outd", 0, NT, 0, 1)], tag="final_wait")



def kernel(**inputs):
    W = {k: np.asarray(v) for k, v in inputs.items()}
    H = _prep_weights(W)
    T = _host_tables()
    nc = build_program()
    base = {**H, **T}
    xs = np.ascontiguousarray(np.asarray(W["L0_attn_x"], np.float32))
    in_maps = []
    for c in range(8):
        m = dict(base)
        m["x"] = np.ascontiguousarray(xs[c])
        in_maps.append(m)
    res = run_bass_kernel_spmd(nc, in_maps, core_ids=list(range(8)))
    out = np.stack([np.asarray(r["out"]) for r in res.results]).astype(np.float32)
    return out


# version 03 (final)
# speedup vs baseline: 1.0023x; 1.0023x over previous
import contextlib
import math

import numpy as np

import concourse.bass as bass
import concourse.mybir as mybir
from concourse.bass_utils import run_bass_kernel_spmd

F32 = mybir.dt.float32
BF16 = mybir.dt.bfloat16
AF = mybir.ActivationFunctionType
OP = mybir.AluOpType
AX = mybir.AxisListType

N, D = 2048, 1024
NT, NDT = N // 128, D // 128
INNER, HEADS, DH = 1024, 16, 64
WIN, NW, NWINS, NCTX = 512, 4, 3, 1536
G, NQ, NKV = 2, 256, 1024
LOGK_Q = math.log(288.0)
LOGK_KV = math.log(1152.0)
CE_SCHED = [max(4.0 * 0.7 ** i, 0.03) for i in range(20)]
FE = 0.03
KVH = 4
E_EXP, S_SLOT = 16, 16
NEG30 = -1.0e30
BISECT_ITERS = 27

L0A_CHUNKS = 8
LOC_HALVES = 2
LOC_WINDOWS = 4
EMIT_ROUTERS = 5
BUILD_UPTO = 5



class Op:
    __slots__ = ("eng", "fn", "kind", "tag", "need_inc", "seq")

    def __init__(self, eng, fn, kind, tag, seq):
        self.eng, self.fn, self.kind, self.tag = eng, fn, kind, tag
        self.need_inc = False
        self.seq = seq


def _overlap(a, b):
    return a[1] > b[0] and b[1] > a[0] and a[3] > b[2] and b[3] > a[2]


class Sched:
    ENGS = ("tensor", "vector", "scalar", "gpsimd", "sync")

    def __init__(self):
        self.ops = []
        self.w_act = {}
        self.r_act = {}
        self.deps = {}

    def op(self, eng, fn, reads=(), writes=(), kind="op", tag=""):
        o = Op(eng, fn, kind, tag, len(self.ops))
        dep = set()
        for buf, rect in reads:
            for wr, wo in self.w_act.get(buf, ()):
                if _overlap(rect, wr):
                    dep.add(wo)
            self.r_act.setdefault(buf, []).append((rect, o))
        for buf, rect in writes:
            wl = self.w_act.setdefault(buf, [])
            keep = []
            for wr, wo in wl:
                if _overlap(rect, wr):
                    dep.add(wo)
                    keep.append((wr, wo))
                else:
                    keep.append((wr, wo))
            rl = self.r_act.get(buf, [])
            nrl = []
            for rr, ro in rl:
                if _overlap(rect, rr):
                    if ro is not o:
                        dep.add(ro)
                    if not (rect[0] <= rr[0] and rr[1] <= rect[1]
                            and rect[2] <= rr[2] and rr[3] <= rect[3]):
                        nrl.append((rr, ro))
                else:
                    nrl.append((rr, ro))
            self.r_act[buf] = nrl
            keep = [(wr, wo) for (wr, wo) in keep
                    if not (rect[0] <= wr[0] and wr[1] <= rect[1] and rect[2] <= wr[2] and wr[3] <= rect[3])]
            keep.append((rect, o))
            self.w_act[buf] = keep
        dep2 = set()
        for d in dep:
            if d is o:
                continue
            dep2.add(d)
        self.deps[o] = dep2
        self.ops.append(o)
        return o

    def dma(self, queue, fn, reads=(), writes=(), tag=""):
        return self.op(queue, fn, reads, writes, kind="dma", tag=tag)

    def finalize(self, blk, sems, dma_sems):
        for o in self.ops:
            for d in self.deps[o]:
                d.need_inc = True
        inc_idx = {}
        dmai = {e: 0 for e in self.ENGS}
        engi = {e: 0 for e in self.ENGS}
        for o in self.ops:
            if o.kind == "dma":
                dmai[o.eng] += 1
                inc_idx[o] = ("dma", o.eng, dmai[o.eng] * 16)
            elif o.need_inc:
                engi[o.eng] += 1
                inc_idx[o] = ("eng", o.eng, engi[o.eng])
        plans = {e: [] for e in self.ENGS}
        waited = {e: {} for e in self.ENGS}
        nwaits = 0
        for o in self.ops:
            waits = []
            for d in sorted(self.deps[o], key=lambda dd: dd.seq):
                ii = inc_idx.get(d)
                if ii is None:
                    continue
                kind, peng, val = ii
                key = (kind, peng)
                if val > waited[o.eng].get(key, 0):
                    waited[o.eng][key] = val
                    waits.append((kind, peng, val))
                    nwaits += 1
            plans[o.eng].append((waits, o))
        for e in self.ENGS:
            plan = plans[e]

            def section(eng, plan=plan):
                for waits, o in plan:
                    for kind, peng, val in waits:
                        h = dma_sems[peng] if kind == "dma" else sems[peng]
                        eng.wait_ge(h, val)
                    ii = inc_idx.get(o)
                    if ii is not None and ii[0] == "dma" and ii[2] > 16:
                        eng.wait_ge(dma_sems[ii[1]], ii[2] - 16)
                    instr = o.fn(eng)
                    if ii is not None:
                        if ii[0] == "dma":
                            instr.then_inc(dma_sems[ii[1]], 16)
                        else:
                            instr.then_inc(sems[ii[1]], 1)
            getattr(blk, e)(section)
        return {"ops": len(self.ops), "incs": sum(o.need_inc for o in self.ops), "waits": nwaits}


def _sel(e, out, mask, on_true, on_false):
    return e.select(out, mask, on_true, on_false, add_drain=True)


def R(buf, r0, r1, c0, c1):
    return (buf, (r0, r1, c0, c1))



def _host_tables():
    import ml_dtypes
    bf = ml_dtypes.bfloat16
    t = {}
    t["ident_f32"] = np.eye(128, dtype=np.float32)
    t["ident_bf"] = np.eye(128, dtype=bf)
    t["ones_row_bf"] = np.ones((1, 2048), dtype=bf)
    t["ones_row_f32"] = np.ones((1, 128), dtype=np.float32)
    t["ones_col_bf"] = np.ones((128, 1), dtype=bf)
    t["pos_row"] = np.arange(N, dtype=np.float32)[None, :]
    kk = np.arange(128)[:, None]
    qq = np.arange(512)[None, :]
    cm = np.zeros((4, 128, 512), dtype=np.float32)
    for r in range(4):
        cm[r] = np.where(128 * r + kk > qq, 0.0, 1.0)
    t["cmask"] = np.ascontiguousarray(cm.transpose(1, 0, 2).reshape(128, 4 * 512)).astype(bf)
    bm = np.zeros((8, 128, 512), dtype=np.float32)
    for kt in range(8):
        kr = 128 * kt + kk
        bm[kt] = np.where((kr < qq) | (kr > qq + 512), 0.0, 1.0)
    t["bmask"] = np.ascontiguousarray(bm.transpose(1, 0, 2).reshape(128, 8 * 512)).astype(bf)
    t["j128_col"] = np.tile(np.arange(32, dtype=np.float32) / 32.0, 4)[:, None]
    i1536 = np.arange(1536, dtype=np.float32)
    t["satkey_row"] = (1.0 + (1536.0 - i1536) * 1e-5).astype(np.float32)[None, :]
    t["zerokey_row"] = (-30.0 - i1536 * 1e-5).astype(np.float32)[None, :]
    t["logk_col"] = np.array([[LOGK_Q]] * 6 + [[LOGK_KV]] * 2, dtype=np.float32)
    t["kk_col"] = np.array([[256.0]] * 6 + [[1024.0]] * 2, dtype=np.float32)
    return t


def _tiles_cm(a):
    d_in, o = a.shape
    assert d_in % 128 == 0
    return np.ascontiguousarray(
        a.reshape(d_in // 128, 128, o).transpose(1, 0, 2).reshape(128, -1).astype(np.float32))


def _colv(b):
    return np.ascontiguousarray(np.asarray(b).reshape(-1, 128).T.astype(np.float32))


def _prep_weights(W):
    ft = np.float32
    H = {}

    def tT(a):
        return np.ascontiguousarray(np.asarray(a).T.astype(ft))

    H["wqT"] = _tiles_cm(tT(W["L0_attn_Wq"]))
    H["wkT"] = _tiles_cm(tT(W["L0_attn_Wk"]))
    H["wvT"] = _tiles_cm(tT(W["L0_attn_Wv"]))
    H["bqc"] = _colv(W["L0_attn_bq"])
    H["bkc"] = _colv(W["L0_attn_bk"])
    H["bv_row"] = np.asarray(W["L0_attn_bv"], ft)[None, :]
    H["wqkvT"] = _tiles_cm(tT(W["L0_ffn_w_qkv"]))
    H["wloT"] = _tiles_cm(tT(W["L0_ffn_w_lo"]))
    H["rtT"] = _tiles_cm(np.concatenate([tT(W["L0_ffn_rt_q"]), tT(W["L0_ffn_rt_kv"])], axis=1))
    H["lngc"] = _colv(W["L0_ffn_ln_g"])
    H["lnbc"] = _colv(W["L0_ffn_ln_b"])
    H["qngc"] = np.concatenate([_colv(np.asarray(W["L0_ffn_qnorm_g"])[g]) for g in range(G)], axis=1)
    H["cngc"] = np.concatenate([_colv(np.asarray(W["L0_ffn_cnorm_g"])[g]) for g in range(G)], axis=1)
    H["nullk"] = np.stack([_colv(np.asarray(W["L0_ffn_null_kv"])[0, g].reshape(-1)) for g in range(G)])
    H["nullv_row"] = np.stack([np.concatenate(
        [np.asarray(W["L0_ffn_null_kv"])[1, g].reshape(16, 64).astype(ft),
         np.ones((16, 1), dtype=ft)], axis=1).reshape(-1) for g in range(G)])
    H["wgqT"] = np.stack([_tiles_cm(tT(np.asarray(W["L0_ffn_w_q"])[g])) for g in range(G)])
    H["wgkvT"] = np.stack([_tiles_cm(tT(np.asarray(W["L0_ffn_w_kv"])[g])) for g in range(G)])
    H["wgoT"] = np.stack([_tiles_cm(tT(np.asarray(W["L0_ffn_w_out"])[g])) for g in range(G)])
    perm = np.concatenate([np.arange(0, DH, 2), np.arange(1, DH, 2)])
    qcols = np.concatenate([h * DH + perm for h in range(HEADS)])
    kcols = np.concatenate([h * DH + perm for h in range(KVH)])
    H["l1wq"] = _tiles_cm(np.asarray(W["L1_attn_Wq"], ft)[:, qcols])
    H["l1wk"] = _tiles_cm(np.asarray(W["L1_attn_Wk"], ft)[:, kcols])
    H["l1wv"] = _tiles_cm(np.asarray(W["L1_attn_Wv"], ft))
    H["l1wo"] = _tiles_cm(np.asarray(W["L1_attn_Wo"], ft))
    H["phase_row"] = np.asarray(W["L1_attn_q_phase_angle"], ft)[None, :]
    H["als_row"] = np.asarray(W["L1_attn_alibi_log_slopes"], ft)[None, :]
    H["f_gamc"] = _colv(W["L1_ffn_ln_gamma"])
    H["f_betc"] = _colv(W["L1_ffn_ln_beta"])
    H["f_gam_row"] = np.asarray(W["L1_ffn_ln_gamma"], ft)[None, :]
    H["f_bet_row"] = np.asarray(W["L1_ffn_ln_beta"], ft)[None, :]
    H["f_mu"] = _tiles_cm(np.asarray(W["L1_ffn_mu"], ft).reshape(D, E_EXP * S_SLOT))
    H["f_scale"] = np.asarray(W["L1_ffn_scale"], ft).reshape(1, 1)
    H["f_w1"] = np.stack([_tiles_cm(np.asarray(W["L1_ffn_w1"])[e].astype(ft)) for e in range(E_EXP)])
    H["f_b1c"] = np.concatenate([_colv(np.asarray(W["L1_ffn_b1"])[e]) for e in range(E_EXP)], axis=1)
    w2b = np.stack([np.concatenate([np.asarray(W["L1_ffn_w2"])[e].astype(ft),
                                    np.asarray(W["L1_ffn_b2"])[e].astype(ft)[None, :]], axis=0)
                    for e in range(E_EXP)])
    H["f_w2a"] = np.ascontiguousarray(
        w2b[:, :256].reshape(E_EXP, 2, 128, D).transpose(0, 2, 1, 3).reshape(E_EXP, 128, 2 * D))
    H["f_w2b"] = np.ascontiguousarray(w2b[:, 256:257])
    return H


DRAM_SPECS = [
    ("x", (N, D), F32),
    ("ident_f32", (128, 128), F32),
    ("ident_bf", (128, 128), BF16),
    ("ones_row_bf", (1, 2048), BF16),
    ("ones_row_f32", (1, 128), F32),
    ("ones_col_bf", (128, 1), BF16),
    ("pos_row", (1, N), F32),
    ("satkey_row", (1, 1536), F32), ("zerokey_row", (1, 1536), F32),
    ("logk_col", (8, 1), F32), ("kk_col", (8, 1), F32),
    ("j128_col", (128, 1), F32),
    ("cmask", (128, 4 * 512), BF16),
    ("bmask", (128, 8 * 512), BF16),
    ("wqT", (128, 8 * 1024), F32), ("wkT", (128, 8 * 1024), F32), ("wvT", (128, 8 * 1024), F32),
    ("bqc", (128, 8), F32), ("bkc", (128, 8), F32), ("bv_row", (1, 1024), F32),
    ("wqkvT", (128, 8 * 3072), F32), ("wloT", (128, 8 * 1024), F32), ("rtT", (128, 8 * 4), F32),
    ("lngc", (128, 8), F32), ("lnbc", (128, 8), F32),
    ("qngc", (128, 16), F32), ("cngc", (128, 16), F32),
    ("nullk", (2, 128, 8), F32), ("nullv_row", (2, 1040), F32),
    ("wgqT", (2, 128, 8 * 1024), F32), ("wgkvT", (2, 128, 8 * 2048), F32), ("wgoT", (2, 128, 8 * 1024), F32),
    ("l1wq", (128, 4 * 1024), F32), ("l1wk", (128, 4 * 256), F32), ("l1wv", (128, 4 * 256), F32),
    ("l1wo", (128, 8 * 1024), F32),
    ("phase_row", (1, 16), F32), ("als_row", (1, 16), F32),
    ("f_gamc", (128, 8), F32), ("f_betc", (128, 8), F32),
    ("f_gam_row", (1, 1024), F32), ("f_bet_row", (1, 1024), F32),
    ("f_mu", (128, 8 * 256), F32), ("f_scale", (1, 1), F32),
    ("f_w1", (16, 128, 8 * 256), F32), ("f_b1c", (128, 32), F32),
    ("f_w2a", (16, 128, 2 * 1024), F32), ("f_w2b", (16, 1, 1024), F32),
]

HEAP_COLS = 49280



def build_program():
    nc = bass.Bass("TRN2", target_bir_lowering=False)
    dt = {}
    for nm, sh, ty in DRAM_SPECS:
        dt[nm] = nc.dram_tensor(nm, list(sh), ty, kind="ExternalInput")
    out_d = nc.dram_tensor("out", [N, D], F32, kind="ExternalOutput")
    xres = nc.dram_tensor("xres", [N, D], F32, kind="Internal")
    cmt_d = nc.dram_tensor("cmt_d", [128, 4096], BF16, kind="Internal")
    dt["cmt_d"] = cmt_d

    with contextlib.ExitStack() as es:
        def sbuf(name, shape, dtype):
            return es.enter_context(nc.sbuf_tensor(name, list(shape), dtype))

        sems = {e: es.enter_context(nc.semaphore("se_" + e)) for e in Sched.ENGS}
        dma_sems = {e: es.enter_context(nc.semaphore("sd_" + e)) for e in ("sync", "gpsimd", "scalar")}

        M = {}
        M["xnT"] = sbuf("xnT", (128, NDT * N), BF16)
        M["stats"] = sbuf("stats", (128, 128), F32)
        M["cm"] = sbuf("cm", (128, 4 * 512), BF16)
        M["idf"] = sbuf("idf", (128, 128), F32)
        M["idb"] = sbuf("idb", (128, 128), BF16)
        M["ones_row"] = sbuf("ones_row", (1, 2048), BF16)
        M["ones_rf"] = sbuf("ones_rf", (1, 128), F32)
        M["ones_col"] = sbuf("ones_col", (128, 1), BF16)
        M["pos"] = sbuf("pos", (1, N), F32)
        M["rowf"] = sbuf("rowf", (8, 2564), F32)
        M["routf"] = sbuf("routf", (8, 7680), F32)
        M["selcol"] = sbuf("selcol", (128, 128), F32)
        M["hf"] = sbuf("hf", (128, 4224), F32)
        M["hb"] = sbuf("hb", (128, HEAP_COLS), BF16)
        M["qa"] = sbuf("qa", (128, 512), BF16)
        M["qx"] = sbuf("qx", (8, 512), BF16)
        M["rtb"] = sbuf("rtb", (128, 80), BF16)
        M["ki"] = sbuf("ki", (128, 512), mybir.dt.int32)
        M["B"] = [es.enter_context(nc.psum_tensor(f"pb{i}", [128, 512], F32)) for i in range(7)]
        M["PTB"] = es.enter_context(nc.psum_tensor("ptb", [128, 1024], BF16))

        S = Sched()
        _build(S, dt, out_d, xres, M)
        with nc.Block() as blk:
            st = S.finalize(blk, sems, dma_sems)
        build_program.stats = st
    return nc


def _build(S, dt, out_d, xres, M):
    xnT, stats = M["xnT"], M["stats"]
    hb, hf = M["hb"], M["hf"]
    rowf, routf = M["rowf"], M["routf"]
    B = M["B"]

    def PB(b):
        return R(f"pb{b}", 0, 128, 0, 512)

    for nm, dst in [("cmask", M["cm"]), ("ident_f32", M["idf"]), ("ident_bf", M["idb"]),
                    ("ones_row_bf", M["ones_row"]),
                    ("ones_row_f32", M["ones_rf"]), ("ones_col_bf", M["ones_col"]), ("pos_row", M["pos"])]:
        rows = dst.shape[0]
        cols = dst.shape[1]
        S.dma("sync", lambda e, nm=nm, dst=dst: e.dma_start(dst[:, :], dt[nm].ap()),
              writes=[R({"ident_f32": "idf", "ident_bf": "idb"}.get(nm, nm), 0, rows, 0, cols)], tag="ld_" + nm)
    CONST = {"cmask": R("cmask", 0, 128, 0, 2048), "idf": R("idf", 0, 128, 0, 128),
             "idb": R("idb", 0, 128, 0, 128), "bmask": R("bmask", 0, 128, 0, 4096),
             "ones_row_bf": R("ones_row_bf", 0, 1, 0, 2048), "ones_row_f32": R("ones_row_f32", 0, 1, 0, 128),
             "ones_col_bf": R("ones_col_bf", 0, 128, 0, 1), "pos_row": R("pos_row", 0, 1, 0, N)}

    _stage_ln(S, dt, M, PB, CONST)
    if EMIT_ROUTERS >= 1:
        _stage_routers(S, dt, M, PB, CONST)
    if BUILD_UPTO >= 1:
        _l0_attn(S, dt, xres, M, PB, CONST)
    if BUILD_UPTO >= 2:
        _l0_local(S, dt, xres, M, PB, CONST)
    if BUILD_UPTO >= 3:
        _l0_grouped(S, dt, xres, M, PB, CONST)
    if BUILD_UPTO >= 4:
        _bv_fold(S, dt, xres, M, PB, CONST)
        _stage_ln(S, dt, M, PB, CONST, src=xres, with_lo=False)
        if BUILD_UPTO >= 5:
            _l1_ffn_pre(S, dt, xres, M, PB, CONST)
        _l1_attn(S, dt, xres, M, PB, CONST)
        if BUILD_UPTO >= 5:
            _l1_ffn_out(S, dt, out_d, xres, M, PB, CONST)
    _finish(S, dt, out_d, xres, M, PB, CONST)



XL = HEAP_COLS - 16384


def _stage_ln(S, dt, M, PB, CONST, src=None, with_lo=True):
    xnT, stats, hf, hb = M["xnT"], M["stats"], M["hf"], M["hb"]

    def st(c0, c1):
        return R("stats", 0, 128, c0, c1)

    for i in range(NT):
        r = 1024 * (i % 2)
        xr = R("hf", 0, 128, r, r + D)
        xn = R("hf", 0, 128, 2048, 3072)
        sq = R("hf", 0, 128, 3072, 4096)
        if src is None:
            S.dma("sync", lambda e, i=i, r=r: e.dma_start(hf[:, r:r + D],
                                                          dt["x"].ap()[128 * i:128 * (i + 1), :]),
                  reads=[xr], writes=[xr], tag="ld_xt")
        else:
            S.dma("sync", lambda e, i=i, r=r: e.dma_start(hf[:, r:r + D],
                                                          src.ap()[128 * i:128 * (i + 1), :]),
                  reads=[xr, R("xres", i, i + 1, 0, 1)], writes=[xr], tag="ld_xt1")
        S.op("vector", lambda e, i=i, r=r: e.reduce_sum(stats[:, i:i + 1], hf[:, r:r + D], axis=AX.X),
             reads=[xr], writes=[st(i, i + 1)], tag="lnsum")
        S.op("vector", lambda e, i=i: e.tensor_scalar(stats[:, i:i + 1], stats[:, i:i + 1], 1.0 / D, None,
                                                      op0=OP.mult),
             reads=[st(i, i + 1)], writes=[st(i, i + 1)], tag="mu")
        S.op("vector", lambda e, i=i, r=r: e.tensor_scalar(hf[:, 2048:3072], hf[:, r:r + D], stats[:, i:i + 1],
                                                           None, op0=OP.subtract),
             reads=[xr, st(i, i + 1)], writes=[xn], tag="xc")
        S.op("scalar", lambda e: e.activation(hf[:, 3072:4096], hf[:, 2048:3072], AF.Square),
             reads=[xn], writes=[sq], tag="sq")
        S.op("vector", lambda e, i=i: e.reduce_sum(stats[:, 16 + i:17 + i], hf[:, 3072:4096], axis=AX.X),
             reads=[sq], writes=[st(16 + i, 17 + i)], tag="var")
        S.op("vector", lambda e, i=i: e.tensor_scalar(stats[:, 16 + i:17 + i], stats[:, 16 + i:17 + i],
                                                      1.0 / D, 1e-5, op0=OP.mult, op1=OP.add),
             reads=[st(16 + i, 17 + i)], writes=[st(16 + i, 17 + i)], tag="vs")
        S.op("vector", lambda e, i=i: e.reciprocal(stats[:, 32 + i:33 + i], stats[:, 16 + i:17 + i]),
             reads=[st(16 + i, 17 + i)], writes=[st(32 + i, 33 + i)], tag="rcp")
        S.op("scalar", lambda e, i=i: e.activation(stats[:, 32 + i:33 + i], stats[:, 32 + i:33 + i], AF.Sqrt),
             reads=[st(32 + i, 33 + i)], writes=[st(32 + i, 33 + i)], tag="sqrt")
        for _ in range(2):
            S.op("vector", lambda e, i=i: e.tensor_tensor(stats[:, 48 + i:49 + i], stats[:, 32 + i:33 + i],
                                                          stats[:, 32 + i:33 + i], op=OP.mult),
                 reads=[st(32 + i, 33 + i)], writes=[st(48 + i, 49 + i)], tag="n1")
            S.op("vector", lambda e, i=i: e.tensor_tensor(stats[:, 48 + i:49 + i], stats[:, 48 + i:49 + i],
                                                          stats[:, 16 + i:17 + i], op=OP.mult),
                 reads=[st(48 + i, 49 + i), st(16 + i, 17 + i)], writes=[st(48 + i, 49 + i)], tag="n2")
            S.op("vector", lambda e, i=i: e.tensor_scalar(stats[:, 48 + i:49 + i], stats[:, 48 + i:49 + i],
                                                          -0.5, 1.5, op0=OP.mult, op1=OP.add),
                 reads=[st(48 + i, 49 + i)], writes=[st(48 + i, 49 + i)], tag="n3")
            S.op("vector", lambda e, i=i: e.tensor_tensor(stats[:, 32 + i:33 + i], stats[:, 32 + i:33 + i],
                                                          stats[:, 48 + i:49 + i], op=OP.mult),
                 reads=[st(32 + i, 33 + i), st(48 + i, 49 + i)], writes=[st(32 + i, 33 + i)], tag="n4")
        S.op("vector", lambda e, i=i: e.tensor_scalar(hf[:, 2048:3072], hf[:, 2048:3072],
                                                      stats[:, 32 + i:33 + i], None, op0=OP.mult),
             reads=[xn, st(32 + i, 33 + i)], writes=[xn], tag="xnmul")
        for j in range(NDT):
            b = 5 + (j % 2)
            xnsl = R("hf", 0, 128, 2048 + 128 * j, 2048 + 128 * (j + 1))
            hiT = R("xnT", 0, 128, N * j + 128 * i, N * j + 128 * (i + 1))
            loT = R("hb", 0, 128, XL + N * j + 128 * i, XL + N * j + 128 * (i + 1))
            S.op("tensor", lambda e, j=j, b=b: e.transpose(M["B"][b][:, 0:128],
                                                           hf[:, 2048 + 128 * j:2048 + 128 * (j + 1)],
                                                           M["idf"][:, :]),
                 reads=[xnsl, CONST["idf"]], writes=[R(f"pb{b}", 0, 128, 0, 128)], tag="trx")
            S.op("scalar", lambda e, i=i, j=j, b=b: e.activation(
                xnT[:, N * j + 128 * i:N * j + 128 * (i + 1)], M["B"][b][:, 0:128], AF.Copy),
                reads=[R(f"pb{b}", 0, 128, 0, 128)], writes=[R(f"pb{b}", 0, 128, 0, 128), hiT], tag="xnh")
            if with_lo:
                S.op("vector", lambda e, i=i, j=j, b=b: e.tensor_tensor(
                    hb[:, XL + N * j + 128 * i:XL + N * j + 128 * (i + 1)], M["B"][b][:, 0:128],
                    xnT[:, N * j + 128 * i:N * j + 128 * (i + 1)], op=OP.subtract),
                    reads=[R(f"pb{b}", 0, 128, 0, 128), hiT],
                    writes=[R(f"pb{b}", 0, 128, 0, 128), loT], tag="xnl")

    S.op("vector", lambda e: e.tensor_tensor(stats[:, 48:64], stats[:, 32:48], stats[:, 32:48], op=OP.mult),
         reads=[st(32, 48)], writes=[st(48, 64)], tag="rs")
    S.op("vector", lambda e: e.tensor_scalar(stats[:, 64:80], stats[:, 48:64], -1e-5, 1.0,
                                             op0=OP.mult, op1=OP.add),
         reads=[st(48, 64)], writes=[st(64, 80)], tag="var2")

    def rsqrt_cols(dst0, eps):
        S.op("vector", lambda e: e.tensor_scalar(stats[:, dst0:dst0 + 16], stats[:, 64:80], 1.0, eps,
                                                 op0=OP.mult, op1=OP.add),
             reads=[st(64, 80)], writes=[st(dst0, dst0 + 16)], tag="rq0")
        S.op("vector", lambda e: e.reciprocal(stats[:, 112:128], stats[:, dst0:dst0 + 16]),
             reads=[st(dst0, dst0 + 16)], writes=[st(112, 128)], tag="rq1")
        S.op("scalar", lambda e: e.activation(stats[:, 112:128], stats[:, 112:128], AF.Sqrt),
             reads=[st(112, 128)], writes=[st(112, 128)], tag="rq2")
        for _ in range(2):
            S.op("vector", lambda e: e.tensor_tensor(stats[:, 48:64], stats[:, 112:128], stats[:, 112:128],
                                                     op=OP.mult),
                 reads=[st(112, 128)], writes=[st(48, 64)], tag="rq3")
            S.op("vector", lambda e: e.tensor_tensor(stats[:, 48:64], stats[:, 48:64], stats[:, dst0:dst0 + 16],
                                                     op=OP.mult),
                 reads=[st(48, 64), st(dst0, dst0 + 16)], writes=[st(48, 64)], tag="rq4")
            S.op("vector", lambda e: e.tensor_scalar(stats[:, 48:64], stats[:, 48:64], -0.5, 1.5,
                                                     op0=OP.mult, op1=OP.add),
                 reads=[st(48, 64)], writes=[st(48, 64)], tag="rq5")
            S.op("vector", lambda e: e.tensor_tensor(stats[:, 112:128], stats[:, 112:128], stats[:, 48:64],
                                                     op=OP.mult),
                 reads=[st(112, 128), st(48, 64)], writes=[st(112, 128)], tag="rq6")
        S.op("vector", lambda e: e.tensor_copy(stats[:, dst0:dst0 + 16], stats[:, 112:128]),
             reads=[st(112, 128)], writes=[st(dst0, dst0 + 16)], tag="rq7")

    rsqrt_cols(80, 1e-5)
    rsqrt_cols(96, 0.0)



def _stage_routers(S, dt, M, PB, CONST):
    xnT, hb, hf = M["xnT"], M["hb"], M["hf"]
    rowf, routf, rtb = M["rowf"], M["routf"], M["rtb"]
    B = M["B"]
    n = 1536

    def RW(r0, r1, c0, c1):
        return R("routf", r0, r1, c0, c1)

    def RF(r0, r1, c0, c1):
        return R("rowf", r0, r1, c0, c1)

    xnT_all = R("xnT", 0, 128, 0, NDT * N)
    xnl_all = R("hb", 0, 128, XL, XL + NDT * N)

    S.dma("sync", lambda e: e.dma_start(hf[:, 4096:4128], dt["rtT"].ap()),
          reads=[R("hf", 0, 128, 4096, 4128)], writes=[R("hf", 0, 128, 4096, 4128)], tag="ld_rt")
    S.op("scalar", lambda e: e.activation(rtb[:, 0:32], hf[:, 4096:4128], AF.Copy),
         reads=[R("hf", 0, 128, 4096, 4128)], writes=[R("rtb", 0, 128, 0, 32)], tag="rth")
    S.op("vector", lambda e: e.tensor_tensor(rtb[:, 32:64], hf[:, 4096:4128], rtb[:, 0:32], op=OP.subtract),
         reads=[R("hf", 0, 128, 4096, 4128), R("rtb", 0, 128, 0, 32)], writes=[R("rtb", 0, 128, 32, 64)], tag="rtl")
    for ch in range(4):
        def mmrt(e, ch=ch):
            last = None
            for hi in (0, 32):
                for lo in (0, 1):
                    for j in range(NDT):
                        if lo == 0:
                            rhs = xnT[:, N * j + 512 * ch:N * j + 512 * (ch + 1)]
                        else:
                            rhs = hb[:, XL + N * j + 512 * ch:XL + N * j + 512 * (ch + 1)]
                        last = e.matmul(B[5][0:4, 0:512], rtb[:, hi + 4 * j:hi + 4 * (j + 1)], rhs,
                                        start=(hi == 0 and lo == 0 and j == 0),
                                        stop=(hi == 32 and lo == 1 and j == NDT - 1))
            return last
        S.op("tensor", mmrt, reads=[R("rtb", 0, 128, 0, 64), xnT_all, xnl_all],
             writes=[R("pb5", 0, 4, 0, 512)], tag="s_mm")
        S.op("vector", lambda e, ch=ch: e.tensor_copy(hf[0:4, 512 * ch:512 * (ch + 1)], B[5][0:4, 0:512]),
             reads=[R("pb5", 0, 4, 0, 512)],
             writes=[R("pb5", 0, 4, 0, 512), R("hf", 0, 4, 512 * ch, 512 * (ch + 1))], tag="s_ev")
    for w in range(3):
        S.dma("sync", lambda e, w=w: e.dma_start(routf[2 * w:2 * w + 2, 0:512],
                                                 hf[0:2, 512 * (w + 1):512 * (w + 2)]),
              reads=[R("hf", 0, 2, 512 * (w + 1), 512 * (w + 2))],
              writes=[R("routf", 2 * w, 2 * w + 2, 0, 512)], tag="s_scq")
    S.dma("sync", lambda e: e.dma_start(routf[6:8, 0:1536], hf[2:4, 0:1536]),
          reads=[R("hf", 2, 4, 0, 1536)], writes=[R("routf", 6, 8, 0, 1536)], tag="s_sck")
    S.op("vector", lambda e: e.memset(routf[0:6, 512:1536], NEG30),
         writes=[R("routf", 0, 6, 512, 1536)], tag="spad")

    if EMIT_ROUTERS < 2:
        return
    n = 1536
    sA = R("routf", 0, 8, 0, 1536)
    uA = R("routf", 0, 8, 1536, 3072)
    gA = R("routf", 0, 8, 3072, 4608)
    tA = R("routf", 0, 8, 4608, 6144)
    zA = R("routf", 0, 8, 6144, 7680)
    dA = R("hf", 0, 8, 2048, 3584)
    CC = lambda c: RF(0, 8, c, c + 1)
    S.dma("sync", lambda e: e.dma_start(rowf[0:8, 10:11], dt["logk_col"].ap()),
          writes=[CC(10)], tag="ld_logk")
    S.dma("sync", lambda e: e.dma_start(rowf[0:8, 11:12], dt["kk_col"].ap()),
          writes=[CC(11)], tag="ld_kk")
    S.dma("sync", lambda e: e.dma_start(rowf[0:1, 44:1580], dt["satkey_row"].ap()),
          writes=[RF(0, 1, 44, 1580)], tag="ld_sat")
    S.dma("sync", lambda e: e.dma_start(hf[0:1, 3584:5120][:, 0:1536] if False else hf[0:1, 2048:3584],
                                        dt["zerokey_row"].ap()),
          writes=[R("hf", 0, 1, 2048, 3584)], tag="ld_zero")
    for which, dst_c in ((0, 4608), (1, 6144)):
        for ch in range(3):
            def bkey(e, which=which, ch=ch):
                if which == 0:
                    rhs = rowf[0:1, 44 + 512 * ch:44 + 512 * (ch + 1)]
                else:
                    rhs = hf[0:1, 2048 + 512 * ch:2048 + 512 * (ch + 1)]
                return e.matmul(B[6][0:8, 0:512], M["ones_rf"][0:1, 0:8], rhs, start=True, stop=True)
            S.op("tensor", bkey,
                 reads=[RF(0, 1, 44, 1580), R("hf", 0, 1, 2048, 3584), CONST["ones_row_f32"]],
                 writes=[R("pb6", 0, 8, 0, 512)], tag="bkey")
            S.op("vector", lambda e, dst_c=dst_c, ch=ch: e.tensor_copy(
                routf[0:8, dst_c + 512 * ch:dst_c + 512 * (ch + 1)], B[6][0:8, 0:512]),
                reads=[R("pb6", 0, 8, 0, 512)],
                writes=[R("pb6", 0, 8, 0, 512), R("routf", 0, 8, dst_c + 512 * ch, dst_c + 512 * (ch + 1))],
                tag="bkey_ev")

    S.op("vector", lambda e: e.tensor_scalar(routf[0:8, 1536:3072], routf[0:8, 0:1536], -1e29, None,
                                             op0=OP.is_ge),
         reads=[sA], writes=[uA], tag="r_valid")
    S.op("vector", lambda e: e.tensor_tensor(routf[0:8, 1536:3072], routf[0:8, 1536:3072],
                                             routf[0:8, 0:1536], op=OP.mult),
         reads=[uA, sA], writes=[uA], tag="r_bn0")
    S.op("vector", lambda e: e.tensor_scalar(routf[0:8, 1536:3072], routf[0:8, 1536:3072], -1.0, None,
                                             op0=OP.mult),
         reads=[uA], writes=[uA], tag="r_bn1")

    for it, ce in enumerate(CE_SCHED):
        S.op("vector", lambda e: e.tensor_tensor(routf[0:8, 1536:3072], routf[0:8, 0:1536],
                                                 routf[0:8, 1536:3072], op=OP.add),
             reads=[sA, uA], writes=[uA], tag="r_u")
        S.op("vector", lambda e, ce=ce: e.tensor_scalar(routf[0:8, 1536:3072], routf[0:8, 1536:3072],
                                                        1.0 / ce, None, op0=OP.mult),
             reads=[uA], writes=[uA], tag="r_us")
        S.op("vector", lambda e: e.reduce_max(rowf[0:8, 0:1], routf[0:8, 1536:3072], axis=AX.X, negate=True),
             reads=[uA], writes=[CC(0)], tag="r_m")
        S.op("scalar", lambda e: e.activation(routf[0:8, 1536:3072], routf[0:8, 1536:3072], AF.Exp,
                                              bias=rowf[0:8, 0:1], scale=1.0),
             reads=[uA, CC(0)], writes=[uA], tag="r_e")
        S.op("vector", lambda e: e.reduce_sum(rowf[0:8, 1:2], routf[0:8, 1536:3072], axis=AX.X),
             reads=[uA], writes=[CC(1)], tag="r_s")
        S.op("scalar", lambda e: e.activation(rowf[0:8, 2:3], rowf[0:8, 1:2], AF.Ln),
             reads=[CC(1)], writes=[CC(2)], tag="r_ln")
        S.op("vector", lambda e: e.tensor_tensor(rowf[0:8, 2:3], rowf[0:8, 2:3], rowf[0:8, 0:1],
                                                 op=OP.subtract),
             reads=[CC(2), CC(0)], writes=[CC(2)], tag="r_lse")
        S.op("vector", lambda e: e.tensor_tensor(rowf[0:8, 9:10], rowf[0:8, 10:11], rowf[0:8, 2:3],
                                                 op=OP.subtract),
             reads=[CC(2), CC(10)], writes=[CC(9)], tag="r_lk")
        S.op("vector", lambda e, ce=ce: e.tensor_scalar(rowf[0:8, 3:4], rowf[0:8, 9:10], ce, None,
                                                        op0=OP.mult),
             reads=[CC(9)], writes=[CC(3)], tag="r_a")
        S.op("vector", lambda e: e.tensor_scalar(routf[0:8, 1536:3072], routf[0:8, 0:1536],
                                                 rowf[0:8, 3:4], None, op0=OP.add),
             reads=[sA, CC(3)], writes=[uA], tag="r_sa")
        if it < len(CE_SCHED) - 1:
            S.op("vector", lambda e: e.tensor_scalar(routf[0:8, 1536:3072], routf[0:8, 1536:3072],
                                                     0.0, -1.0, op0=OP.max, op1=OP.mult),
                 reads=[uA], writes=[uA], tag="r_bneg")

    if EMIT_ROUTERS < 3:
        return
    S.op("vector", lambda e: e.tensor_scalar(routf[0:8, 1536:3072], routf[0:8, 1536:3072], 0.0, -25.0,
                                             op0=OP.min, op1=OP.max),
         reads=[uA], writes=[uA], tag="k_u")
    S.op("scalar", lambda e: e.activation(routf[0:8, 3072:4608], routf[0:8, 1536:3072], AF.Exp,
                                          scale=1.0 / FE),
         reads=[uA], writes=[gA], tag="k_sc")
    S.op("vector", lambda e: e.memset(rowf[0:8, 12:13], 1e-30), writes=[CC(12)], tag="t_ms")
    S.op("vector", lambda e: e.tensor_scalar(rowf[0:8, 12:13], rowf[0:8, 12:13], 1e-30, None, op0=OP.mult),
         reads=[CC(12)], writes=[CC(12)], tag="t_u1")
    S.op("vector", lambda e: e.tensor_scalar(rowf[0:8, 12:13], rowf[0:8, 12:13], 0.0, None, op0=OP.is_le),
         reads=[CC(12)], writes=[CC(12)], tag="t_g")
    S.op("vector", lambda e: e.tensor_scalar(rowf[0:8, 12:13], rowf[0:8, 12:13], -1.0, 1.0,
                                             op0=OP.mult, op1=OP.add),
         reads=[CC(12)], writes=[CC(12)], tag="t_f")
    S.op("vector", lambda e: e.tensor_scalar(rowf[0:8, 13:14], rowf[0:8, 12:13], 2.0 ** -128, None, op0=OP.mult),
         reads=[CC(12)], writes=[CC(13)], tag="t_fb")
    S.op("vector", lambda e: e.tensor_scalar(rowf[0:8, 14:15], rowf[0:8, 12:13], -1.0, 1.0,
                                             op0=OP.mult, op1=OP.add),
         reads=[CC(12)], writes=[CC(14)], tag="t_1f")
    S.op("vector", lambda e: e.tensor_tensor(rowf[0:8, 13:14], rowf[0:8, 13:14], rowf[0:8, 14:15], op=OP.add),
         reads=[CC(13), CC(14)], writes=[CC(13)], tag="t_b")
    S.op("vector", lambda e: e.tensor_tensor(rowf[0:8, 14:15], rowf[0:8, 13:14], rowf[0:8, 13:14], op=OP.mult),
         reads=[CC(13)], writes=[CC(14)], tag="t_c")
    S.op("vector", lambda e: e.tensor_tensor(rowf[0:8, 15:16], rowf[0:8, 14:15], rowf[0:8, 14:15], op=OP.mult),
         reads=[CC(14)], writes=[CC(15)], tag="t_c2")
    S.op("vector", lambda e: e.tensor_scalar(rowf[0:8, 16:17], rowf[0:8, 13:14], 2.0 ** -126, None, op0=OP.mult),
         reads=[CC(13)], writes=[CC(16)], tag="t_ab")
    S.op("vector", lambda e: e.tensor_tensor(rowf[0:8, 16:17], rowf[0:8, 16:17], rowf[0:8, 14:15], op=OP.mult),
         reads=[CC(16), CC(14)], writes=[CC(16)], tag="t_ac")
    S.op("vector", lambda e: e.tensor_tensor(rowf[0:8, 16:17], rowf[0:8, 16:17], rowf[0:8, 15:16], op=OP.mult),
         reads=[CC(16), CC(15)], writes=[CC(16)], tag="t_T")
    S.op("vector", lambda e: e.tensor_scalar(routf[0:8, 0:1536], routf[0:8, 3072:4608], rowf[0:8, 16:17],
                                             None, op0=OP.is_ge),
         reads=[gA, CC(16)], writes=[sA], tag="k_mz")
    S.op("vector", lambda e: e.tensor_scalar(routf[0:8, 0:1536], routf[0:8, 0:1536], -1.0, 1.0,
                                             op0=OP.mult, op1=OP.add),
         reads=[sA], writes=[sA], tag="k_mzi")
    S.op("vector", lambda e: e.tensor_tensor(hf[0:8, 2048:3584], routf[0:8, 6144:7680],
                                             routf[0:8, 1536:3072], op=OP.subtract),
         reads=[zA, uA], writes=[dA], tag="k_zd")
    S.op("vector", lambda e: e.tensor_tensor(hf[0:8, 2048:3584], hf[0:8, 2048:3584],
                                             routf[0:8, 0:1536], op=OP.mult),
         reads=[dA, sA], writes=[dA], tag="k_zm")
    S.op("vector", lambda e: e.tensor_tensor(routf[0:8, 1536:3072], routf[0:8, 1536:3072],
                                             hf[0:8, 2048:3584], op=OP.add),
         reads=[uA, dA], writes=[uA], tag="k_za")
    S.op("vector", lambda e: e.tensor_scalar(routf[0:8, 0:1536], routf[0:8, 3072:4608], 1.0, None,
                                             op0=OP.is_ge),
         reads=[gA], writes=[sA], tag="k_ms")
    S.op("vector", lambda e: e.tensor_tensor(hf[0:8, 2048:3584], routf[0:8, 4608:6144],
                                             routf[0:8, 1536:3072], op=OP.subtract),
         reads=[tA, uA], writes=[dA], tag="k_sd")
    S.op("vector", lambda e: e.tensor_tensor(hf[0:8, 2048:3584], hf[0:8, 2048:3584],
                                             routf[0:8, 0:1536], op=OP.mult),
         reads=[dA, sA], writes=[dA], tag="k_sm")
    S.op("vector", lambda e: e.tensor_tensor(routf[0:8, 1536:3072], routf[0:8, 1536:3072],
                                             hf[0:8, 2048:3584], op=OP.add),
         reads=[uA, dA], writes=[uA], tag="k_sa")

    if EMIT_ROUTERS < 4:
        return
    S.op("vector", lambda e: e.memset(rowf[0:8, 4:5], -31.0), writes=[CC(4)], tag="b_lo")
    S.op("vector", lambda e: e.memset(rowf[0:8, 5:6], 2.0), writes=[CC(5)], tag="b_hi")
    for _ in range(BISECT_ITERS):
        S.op("vector", lambda e: e.tensor_tensor(rowf[0:8, 6:7], rowf[0:8, 4:5], rowf[0:8, 5:6], op=OP.add),
             reads=[CC(4), CC(5)], writes=[CC(6)], tag="b_m0")
        S.op("vector", lambda e: e.tensor_scalar(rowf[0:8, 6:7], rowf[0:8, 6:7], 0.5, None, op0=OP.mult),
             reads=[CC(6)], writes=[CC(6)], tag="b_m1")
        S.op("vector", lambda e: e.tensor_scalar(routf[0:8, 3072:4608], routf[0:8, 1536:3072],
                                                 rowf[0:8, 6:7], None, op0=OP.is_ge),
             reads=[uA, CC(6)], writes=[gA], tag="b_ge")
        S.op("vector", lambda e: e.reduce_sum(rowf[0:8, 7:8], routf[0:8, 3072:4608], axis=AX.X),
             reads=[gA], writes=[CC(7)], tag="b_cnt")
        S.op("vector", lambda e: e.tensor_tensor(rowf[0:8, 8:9], rowf[0:8, 7:8], rowf[0:8, 11:12],
                                                 op=OP.is_ge),
             reads=[CC(7), CC(11)], writes=[CC(8)], tag="b_cmp")
        S.op("vector", lambda e: e.tensor_tensor(rowf[0:8, 9:10], rowf[0:8, 6:7], rowf[0:8, 4:5],
                                                 op=OP.subtract),
             reads=[CC(6), CC(4)], writes=[CC(9)], tag="b_d1")
        S.op("vector", lambda e: e.tensor_tensor(rowf[0:8, 9:10], rowf[0:8, 9:10], rowf[0:8, 8:9],
                                                 op=OP.mult),
             reads=[CC(9), CC(8)], writes=[CC(9)], tag="b_m1b")
        S.op("vector", lambda e: e.tensor_tensor(rowf[0:8, 4:5], rowf[0:8, 4:5], rowf[0:8, 9:10],
                                                 op=OP.add),
             reads=[CC(4), CC(9)], writes=[CC(4)], tag="b_slo")
        S.op("vector", lambda e: e.tensor_tensor(rowf[0:8, 9:10], rowf[0:8, 5:6], rowf[0:8, 6:7],
                                                 op=OP.subtract),
             reads=[CC(5), CC(6)], writes=[CC(9)], tag="b_d2")
        S.op("vector", lambda e: e.tensor_tensor(rowf[0:8, 9:10], rowf[0:8, 9:10], rowf[0:8, 8:9],
                                                 op=OP.mult),
             reads=[CC(9), CC(8)], writes=[CC(9)], tag="b_m2b")
        S.op("vector", lambda e: e.tensor_tensor(rowf[0:8, 5:6], rowf[0:8, 6:7], rowf[0:8, 9:10],
                                                 op=OP.add),
             reads=[CC(6), CC(9)], writes=[CC(5)], tag="b_shi")
    S.op("vector", lambda e: e.tensor_scalar(routf[0:8, 3072:4608], routf[0:8, 1536:3072],
                                             rowf[0:8, 4:5], None, op0=OP.is_ge),
         reads=[uA, CC(4)], writes=[gA], tag="b_fin")

    if EMIT_ROUTERS < 5:
        return
    for c in range(12):
        S.op("tensor", lambda e, c=c: e.transpose(B[6][:, 0:8], routf[0:8, 3072 + 128 * c:3072 + 128 * (c + 1)],
                                                  M["idf"][0:8, 0:8]),
             reads=[gA, CONST["idf"]], writes=[R("pb6", 0, 128, 0, 128)], tag="selT")
        S.op("vector", lambda e, c=c: e.tensor_copy(M["selcol"][:, 8 * c:8 * c + 8], B[6][:, 0:8]),
             reads=[R("pb6", 0, 128, 0, 128)],
             writes=[R("pb6", 0, 128, 0, 128), R("selcol", 0, 128, 8 * c, 8 * c + 8)], tag="selTe")



def _l0_attn(S, dt, xres, M, PB, CONST):
    xnT, hb, hf, rowf, qa, qx = M["xnT"], M["hb"], M["hf"], M["rowf"], M["qa"], M["qx"]
    rtb = M["rtb"]
    B = M["B"]
    KT0, V0, WQ0, QC0, PT0 = 0, 16384, 32768, 40960, 45056
    VW = 1024
    WKS, WVS = 16384, 40960

    HB = lambda c0, c1: R("hb", 0, 128, c0, c1)
    HFc = lambda c0, c1: R("hf", 0, 128, c0, c1)
    RF = lambda r0, r1, c0, c1: R("rowf", r0, r1, c0, c1)

    S.dma("gpsimd", lambda e: e.dma_start(hb[:, WKS:WKS + 8192], dt["wkT"].ap()),
          reads=[HB(WKS, WKS + 8192)], writes=[HB(WKS, WKS + 8192)], tag="ld_wk")
    S.dma("gpsimd", lambda e: e.dma_start(hb[:, WVS:WVS + 8192], dt["wvT"].ap()),
          reads=[HB(WVS, WVS + 8192)], writes=[HB(WVS, WVS + 8192)], tag="ld_wv")
    S.dma("gpsimd", lambda e: e.dma_start(hb[:, WQ0:WQ0 + 8192], dt["wqT"].ap()),
          reads=[HB(WQ0, WQ0 + 8192)], writes=[HB(WQ0, WQ0 + 8192)], tag="ld_wq")
    S.dma("sync", lambda e: e.dma_start(hf[:, 4128:4136], dt["bqc"].ap()),
          writes=[HFc(4128, 4136)], tag="ld_bq")
    S.dma("sync", lambda e: e.dma_start(hf[:, 4136:4144], dt["bkc"].ap()),
          writes=[HFc(4136, 4144)], tag="ld_bk")

    S.op("vector", lambda e: e.tensor_scalar(hf[:, 4144:4152], hf[:, 4128:4136], 1.0 / 32.0, None, op0=OP.mult),
         reads=[HFc(4128, 4136)], writes=[HFc(4144, 4152)], tag="bq32")
    S.op("vector", lambda e: e.memset(hf[:, 4185:4186], 36.04365158081055),
         writes=[HFc(4185, 4186)], tag="ln2c")
    S.op("vector", lambda e: e.tensor_scalar(hf[:, 4185:4186], hf[:, 4185:4186], 1.0, 1.8083065924656694e-06,
                                             op0=OP.mult, op1=OP.add),
         reads=[HFc(4185, 4186)], writes=[HFc(4185, 4186)], tag="ln2c2")

    xnT_all = R("xnT", 0, 128, 0, NDT * N)

    for mt in range(NDT):
        for ch in range(4):
            b = (mt * 4 + ch) % 4
            def mmk(e, mt=mt, ch=ch, b=b):
                last = None
                for j in range(NDT):
                    last = e.matmul(B[b][:, 0:512],
                                    hb[:, WKS + 1024 * j + 128 * mt:WKS + 1024 * j + 128 * (mt + 1)],
                                    xnT[:, N * j + 512 * ch:N * j + 512 * (ch + 1)],
                                    start=(j == 0), stop=(j == NDT - 1))
                return last
            S.op("tensor", mmk, reads=[HB(WKS, WKS + 8192), xnT_all], writes=[PB(b)], tag="kmm")
            S.op("scalar", lambda e, mt=mt, ch=ch, b=b: e.activation(
                hb[:, KT0 + 2048 * mt + 512 * ch:KT0 + 2048 * mt + 512 * (ch + 1)], B[b][:, 0:512],
                AF.Identity, bias=hf[:, 4136 + mt:4137 + mt]),
                reads=[PB(b), HFc(4136, 4144)],
                writes=[PB(b), HB(KT0 + 2048 * mt + 512 * ch, KT0 + 2048 * mt + 512 * (ch + 1))], tag="kev")
    Krect = HB(KT0, KT0 + 16384)

    for i in range(NT):
        for ch in range(2):
            b = (i * 2 + ch) % 4
            def mmv(e, i=i, ch=ch, b=b):
                last = None
                for j in range(NDT):
                    last = e.matmul(B[b][:, 0:512], xnT[:, N * j + 128 * i:N * j + 128 * (i + 1)],
                                    hb[:, WVS + 1024 * j + 512 * ch:WVS + 1024 * j + 512 * (ch + 1)],
                                    start=(j == 0), stop=(j == NDT - 1))
                return last
            S.op("tensor", mmv, reads=[HB(WVS, WVS + 8192), xnT_all], writes=[PB(b)], tag="vmm")
            S.op("vector", lambda e, i=i, ch=ch, b=b: e.tensor_copy(
                hb[:, V0 + VW * i + 512 * ch:V0 + VW * i + 512 * (ch + 1)], B[b][:, 0:512]),
                reads=[PB(b)],
                writes=[PB(b), HB(V0 + VW * i + 512 * ch, V0 + VW * i + 512 * (ch + 1))], tag="vev")
    Vrect = HB(V0, V0 + NT * VW)

    for mt in range(NDT):
        S.op("vector", lambda e, mt=mt: e.reduce_max(hf[:, 4152 + mt:4153 + mt],
                                                     hb[:, KT0 + 2048 * mt:KT0 + 2048 * (mt + 1)], axis=AX.X),
             reads=[HB(KT0 + 2048 * mt, KT0 + 2048 * (mt + 1))], writes=[HFc(4152 + mt, 4153 + mt)], tag="kmx")
        S.op("vector", lambda e, mt=mt: e.tensor_reduce(hf[:, 4160 + mt:4161 + mt],
                                                        hb[:, KT0 + 2048 * mt:KT0 + 2048 * (mt + 1)],
                                                        axis=AX.X, op=OP.min),
             reads=[HB(KT0 + 2048 * mt, KT0 + 2048 * (mt + 1))], writes=[HFc(4160 + mt, 4161 + mt)], tag="kmn")
    S.op("vector", lambda e: e.tensor_tensor(hf[:, 4168:4176], hf[:, 4152:4160], hf[:, 4160:4168], op=OP.add),
         reads=[HFc(4152, 4168)], writes=[HFc(4168, 4176)], tag="c0")
    S.op("vector", lambda e: e.tensor_scalar(hf[:, 4168:4176], hf[:, 4168:4176], 0.5, None, op0=OP.mult),
         reads=[HFc(4168, 4176)], writes=[HFc(4168, 4176)], tag="c1")
    S.op("vector", lambda e: e.tensor_tensor(hf[:, 4176:4184], hf[:, 4152:4160], hf[:, 4160:4168], op=OP.subtract),
         reads=[HFc(4152, 4168)], writes=[HFc(4176, 4184)], tag="r0")
    S.op("vector", lambda e: e.tensor_scalar(hf[:, 4176:4184], hf[:, 4176:4184], 0.5, None, op0=OP.mult),
         reads=[HFc(4176, 4184)], writes=[HFc(4176, 4184)], tag="r1")
    S.op("scalar", lambda e: e.activation(rtb[:, 64:72], hf[:, 4168:4176], AF.Copy),
         reads=[HFc(4168, 4176)], writes=[R("rtb", 0, 128, 64, 72)], tag="crcb")
    S.op("scalar", lambda e: e.activation(rtb[:, 72:80], hf[:, 4176:4184], AF.Copy),
         reads=[HFc(4176, 4184)], writes=[R("rtb", 0, 128, 72, 80)], tag="crrb")
    for ch in range(4):
        for j in range(NDT):
            sl = 0
            S.op("scalar", lambda e, ch=ch, j=j, sl=sl: e.activation(
                qa[:, sl:sl + 512], hb[:, KT0 + 2048 * j + 512 * ch:KT0 + 2048 * j + 512 * (ch + 1)], AF.Square),
                reads=[HB(KT0 + 2048 * j + 512 * ch, KT0 + 2048 * j + 512 * (ch + 1))],
                writes=[R("qa", 0, 128, sl, sl + 512)], tag="k2")
            S.op("tensor", lambda e, j=j, sl=sl: e.matmul(B[4][0:1, 0:512], M["ones_col"][:, 0:1],
                                                          qa[:, sl:sl + 512], start=(j == 0), stop=(j == NDT - 1)),
                 reads=[R("qa", 0, 128, sl, sl + 512), CONST["ones_col_bf"]],
                 writes=[R("pb4", 0, 1, 0, 512)], tag="k2mm")
        S.op("vector", lambda e, ch=ch: e.tensor_copy(rowf[0:1, 512 * ch:512 * (ch + 1)], B[4][0:1, 0:512]),
             reads=[R("pb4", 0, 1, 0, 512)], writes=[R("pb4", 0, 1, 0, 512), RF(0, 1, 512 * ch, 512 * (ch + 1))],
             tag="k2ev")
    S.op("vector", lambda e: e.reduce_max(rowf[0:1, 2048:2049], rowf[0:1, 0:2048], axis=AX.X),
         reads=[RF(0, 1, 0, 2048)], writes=[RF(0, 1, 2048, 2049)], tag="knm")
    S.op("scalar", lambda e: e.activation(rowf[0:1, 2048:2049], rowf[0:1, 2048:2049], AF.Sqrt),
         reads=[RF(0, 1, 2048, 2049)], writes=[RF(0, 1, 2048, 2049)], tag="kns")

    for qc in range(L0A_CHUNKS):
        q0 = 256 * qc
        ktmax = (q0 + 256) // 128
        Qrect = HB(QC0, QC0 + 2048)
        for mt in range(NDT):
            b = mt % 2
            def mmq(e, mt=mt, b=b, q0=q0):
                last = None
                for j in range(NDT):
                    last = e.matmul(B[b][:, 0:256],
                                    hb[:, WQ0 + 1024 * j + 128 * mt:WQ0 + 1024 * j + 128 * (mt + 1)],
                                    xnT[:, N * j + q0:N * j + q0 + 256],
                                    start=(j == 0), stop=(j == NDT - 1))
                return last
            S.op("tensor", mmq, reads=[HB(WQ0, WQ0 + 8192), xnT_all], writes=[R(f"pb{b}", 0, 128, 0, 256)],
                 tag="qmm")
            S.op("scalar", lambda e, mt=mt, b=b: e.activation(
                hb[:, QC0 + 256 * mt:QC0 + 256 * (mt + 1)], B[b][:, 0:256], AF.Identity,
                bias=hf[:, 4144 + mt:4145 + mt], scale=1.0 / 32.0),
                reads=[R(f"pb{b}", 0, 128, 0, 256), HFc(4144, 4152)],
                writes=[R(f"pb{b}", 0, 128, 0, 256), HB(QC0 + 256 * mt, QC0 + 256 * (mt + 1))], tag="qev")
        for j in range(NDT):
            S.op("scalar", lambda e, j=j: e.activation(qa[:, 0:256], hb[:, QC0 + 256 * j:QC0 + 256 * (j + 1)],
                                                       AF.Abs),
                 reads=[HB(QC0 + 256 * j, QC0 + 256 * (j + 1))], writes=[R("qa", 0, 128, 0, 256)], tag="qabs")
            S.op("scalar", lambda e, j=j: e.activation(qa[:, 256:512], hb[:, QC0 + 256 * j:QC0 + 256 * (j + 1)],
                                                       AF.Square),
                 reads=[HB(QC0 + 256 * j, QC0 + 256 * (j + 1))], writes=[R("qa", 0, 128, 256, 512)], tag="qsq")
            S.op("tensor", lambda e, j=j: e.matmul(B[4][0:1, 0:256], rtb[:, 64 + j:65 + j],
                                                   hb[:, QC0 + 256 * j:QC0 + 256 * (j + 1)],
                                                   start=(j == 0), stop=False),
                 reads=[HB(QC0 + 256 * j, QC0 + 256 * (j + 1)), R("rtb", 0, 128, 64, 72)],
                 writes=[R("pb4", 0, 1, 0, 256)], tag="bc")
            S.op("tensor", lambda e, j=j: e.matmul(B[4][0:1, 0:256], rtb[:, 72 + j:73 + j], qa[:, 0:256],
                                                   start=False, stop=(j == NDT - 1)),
                 reads=[R("qa", 0, 128, 0, 256), R("rtb", 0, 128, 72, 80)],
                 writes=[R("pb4", 0, 1, 0, 256)], tag="br")
            S.op("tensor", lambda e, j=j: e.matmul(B[3][0:1, 0:256], M["ones_col"][:, 0:1], qa[:, 256:512],
                                                   start=(j == 0), stop=(j == NDT - 1)),
                 reads=[R("qa", 0, 128, 256, 512), CONST["ones_col_bf"]],
                 writes=[R("pb3", 0, 1, 0, 256)], tag="bq2")
        S.op("vector", lambda e: e.tensor_copy(rowf[0:1, 2052:2308], B[4][0:1, 0:256]),
             reads=[R("pb4", 0, 1, 0, 256)], writes=[R("pb4", 0, 1, 0, 256), RF(0, 1, 2052, 2308)], tag="bsum")
        S.op("scalar", lambda e: e.activation(rowf[0:1, 2308:2564], B[3][0:1, 0:256], AF.Sqrt),
             reads=[R("pb3", 0, 1, 0, 256)], writes=[R("pb3", 0, 1, 0, 256), RF(0, 1, 2308, 2564)], tag="bqn")
        S.op("vector", lambda e: e.tensor_scalar(rowf[0:1, 2308:2564], rowf[0:1, 2308:2564],
                                                 rowf[0:1, 2048:2049], None, op0=OP.mult),
             reads=[RF(0, 1, 2308, 2564), RF(0, 1, 2048, 2049)], writes=[RF(0, 1, 2308, 2564)], tag="bcs")
        S.op("vector", lambda e: e.tensor_tensor(rowf[0:1, 2052:2308], rowf[0:1, 2052:2308],
                                                 rowf[0:1, 2308:2564], op=OP.min),
             reads=[RF(0, 1, 2052, 2308), RF(0, 1, 2308, 2564)], writes=[RF(0, 1, 2052, 2308)], tag="bmin")
        S.op("vector", lambda e: e.tensor_scalar(qx[0:1, 0:256], rowf[0:1, 2052:2308], -1.0, None, op0=OP.mult),
             reads=[RF(0, 1, 2052, 2308)], writes=[R("qx", 0, 1, 0, 256)], tag="qex")

        for kt in range(ktmax):
            b = 4 + (kt % 2)
            pbr = R(f"pb{b}", 0, 128, 0, 256)
            def mms(e, kt=kt, b=b):
                for j in range(NDT):
                    e.matmul(B[b][:, 0:256], hb[:, KT0 + 2048 * j + 128 * kt:KT0 + 2048 * j + 128 * (kt + 1)],
                             hb[:, QC0 + 256 * j:QC0 + 256 * (j + 1)], start=(j == 0), stop=False)
                return e.matmul(B[b][:, 0:256], M["ones_row"][0:1, 0:128], qx[0:1, 0:256],
                                start=False, stop=True)
            S.op("tensor", mms, reads=[Qrect, R("qx", 0, 1, 0, 256), CONST["ones_row_bf"], Krect],
                 writes=[pbr], tag="smm")
            S.op("scalar", lambda e, kt=kt, b=b: e.activation(
                hb[:, PT0 + 256 * kt:PT0 + 256 * (kt + 1)], B[b][:, 0:256], AF.Exp),
                reads=[pbr], writes=[pbr, HB(PT0 + 256 * kt, PT0 + 256 * (kt + 1))], tag="sexp")
            if 128 * kt + 127 >= q0:
                r_ = kt - (q0 // 512) * 4
                off = q0 % 512
                S.op("vector", lambda e, kt=kt, r_=r_, off=off: e.tensor_tensor(
                    hb[:, PT0 + 256 * kt:PT0 + 256 * (kt + 1)], hb[:, PT0 + 256 * kt:PT0 + 256 * (kt + 1)],
                    M["cm"][:, 512 * r_ + off:512 * r_ + off + 256], op=OP.mult),
                    reads=[HB(PT0 + 256 * kt, PT0 + 256 * (kt + 1)), CONST["cmask"]],
                    writes=[HB(PT0 + 256 * kt, PT0 + 256 * (kt + 1))], tag="smask")
        PTrect = HB(PT0, PT0 + 256 * ktmax)

        for qs in range(2):
            i_tok = 2 * qc + qs
            def mk_mo(vlo, bnk):
                def mo(e, vlo=vlo, bnk=bnk, qs=qs, ktmax=ktmax):
                    last = None
                    for kt in range(ktmax):
                        last = e.matmul(B[bnk][:, 0:512],
                                        hb[:, PT0 + 256 * kt + 128 * qs:PT0 + 256 * kt + 128 * (qs + 1)],
                                        hb[:, V0 + VW * kt + vlo:V0 + VW * kt + vlo + 512],
                                        start=(kt == 0), stop=(kt == ktmax - 1))
                    return last
                return mo
            S.op("tensor", mk_mo(0, 1), reads=[PTrect, Vrect], writes=[PB(1)], tag="ommA")
            S.op("tensor", mk_mo(512, 2), reads=[PTrect, Vrect], writes=[PB(2)], tag="ommB")
            def moden(e, qs=qs, ktmax=ktmax):
                last = None
                for kt in range(ktmax):
                    last = e.matmul(B[3][:, 0:1],
                                    hb[:, PT0 + 256 * kt + 128 * qs:PT0 + 256 * kt + 128 * (qs + 1)],
                                    M["ones_col"][:, 0:1], start=(kt == 0), stop=(kt == ktmax - 1))
                return last
            S.op("tensor", moden, reads=[PTrect, CONST["ones_col_bf"]], writes=[R("pb3", 0, 128, 0, 1)],
                 tag="ommD")
            S.op("scalar", lambda e: e.activation(hf[:, 4184:4185], B[3][:, 0:1], AF.Ln, scale=2.0 ** 52),
                 reads=[R("pb3", 0, 128, 0, 1)], writes=[R("pb3", 0, 128, 0, 1), HFc(4184, 4185)], tag="ln")
            S.op("scalar", lambda e: e.activation(hf[:, 4184:4185], hf[:, 4184:4185], AF.Exp, scale=-1.0,
                                                  bias=hf[:, 4185:4186]),
                 reads=[HFc(4184, 4186)], writes=[HFc(4184, 4185)], tag="exn")
            r = 1024 * (i_tok % 2)
            xr = R("hf", 0, 128, r, r + D)
            S.dma("sync", lambda e, i_tok=i_tok, r=r: e.dma_start(
                hf[:, r:r + D], dt["x"].ap()[128 * i_tok:128 * (i_tok + 1), :]),
                reads=[xr], writes=[xr], tag="ld_xacc")
            S.op("vector", lambda e, r=r: e.scalar_tensor_tensor(
                hf[:, r:r + 512], B[1][:, 0:512], hf[:, 4184:4185], hf[:, r:r + 512],
                op0=OP.mult, op1=OP.add),
                reads=[PB(1), HFc(4184, 4185), R("hf", 0, 128, r, r + 512)],
                writes=[PB(1), R("hf", 0, 128, r, r + 512)], tag="oaccA")
            S.op("vector", lambda e, r=r: e.scalar_tensor_tensor(
                hf[:, r + 512:r + D], B[2][:, 0:512], hf[:, 4184:4185], hf[:, r + 512:r + D],
                op0=OP.mult, op1=OP.add),
                reads=[PB(2), HFc(4184, 4185), R("hf", 0, 128, r + 512, r + D)],
                writes=[PB(2), R("hf", 0, 128, r + 512, r + D)], tag="oaccB")
            S.dma("sync", lambda e, i_tok=i_tok, r=r: e.dma_start(
                xres.ap()[128 * i_tok:128 * (i_tok + 1), :], hf[:, r:r + D]),
                reads=[xr], writes=[R("xres", i_tok, i_tok + 1, 0, 1)], tag="st_xacc")



def _l0_local(S, dt, xres, M, PB, CONST):
    xnT, hb, hf, rowf, stats, qx, ki = M["xnT"], M["hb"], M["hf"], M["rowf"], M["stats"], M["qx"], M["ki"]
    B = M["B"]
    HB = lambda c0, c1: R("hb", 0, 128, c0, c1)
    HB2 = HB
    HFr = lambda r0, r1, c0, c1: R("hf", r0, r1, c0, c1)
    RF = lambda r0, r1, c0, c1: R("rowf", r0, r1, c0, c1)
    STT = lambda c0, c1: R("stats", 0, 128, c0, c1)

    BMK = 43168
    S.dma("sync", lambda e: e.dma_start(hb[:, 43168:43168 + 4096], dt["bmask"].ap()),
          reads=[HB2(43168, 47264)], writes=[HB2(43168, 47264)], tag="ld_bm")
    WST = 0
    QH = 4096
    KH = 12288
    VR0, VR1 = 20480, 22560
    XLW = 24736
    PTL = 28832
    ATT = 32928
    ATTT = 34976
    TABC = 37024
    TABS = 39072
    CTB = 41120
    PRT = PTL

    S.op("vector", lambda e: e.memset(qx[0:1, 0:512], 0.0), reads=[R("qx", 0, 1, 0, 512)],
         writes=[R("qx", 0, 1, 0, 512)], tag="qx0")
    for i in range(NT):
        if i < 4:
            S.op("vector", lambda e, i=i: e.memset(stats[:, 112 + i:113 + i], 1.0),
                 writes=[STT(112 + i, 113 + i)], tag="cnt1")
        else:
            w = i // 4 - 1
            c = i % 4
            S.op("vector", lambda e, i=i, c=c, w=w: e.tensor_tensor(
                stats[:, 112 + i:113 + i], M["selcol"][:, 8 * c + 2 * w:8 * c + 2 * w + 1],
                M["selcol"][:, 8 * c + 2 * w + 1:8 * c + 2 * w + 2], op=OP.add),
                reads=[R("selcol", 0, 128, 8 * c, 8 * c + 8)], writes=[STT(112 + i, 113 + i)], tag="cnt2")
            S.op("vector", lambda e, i=i: e.tensor_scalar(stats[:, 112 + i:113 + i],
                                                          stats[:, 112 + i:113 + i], 1.0, 1.0,
                                                          op0=OP.mult, op1=OP.add),
                 reads=[STT(112 + i, 113 + i)], writes=[STT(112 + i, 113 + i)], tag="cnt3")
            S.op("vector", lambda e, i=i: e.reciprocal(stats[:, 112 + i:113 + i], stats[:, 112 + i:113 + i]),
                 reads=[STT(112 + i, 113 + i)], writes=[STT(112 + i, 113 + i)], tag="cnt4")

    S.op("tensor", lambda e: e.transpose(B[6][0:16, 0:128], stats[:, 80:96], M["idf"][:, :]),
         reads=[STT(80, 96), CONST["idf"]], writes=[R("pb6", 0, 128, 0, 128)], tag="ctT")
    S.op("vector", lambda e: e.tensor_copy(hf[0:16, 3584:3712], B[6][0:16, 0:128]),
         reads=[R("pb6", 0, 128, 0, 16)], writes=[R("pb6", 0, 128, 0, 16), HFr(0, 16, 3584, 3712)], tag="ctTe")
    S.dma("sync", lambda e: e.dma_start(hf[0:1, 0:2048], hf[0:16, 3584:3712]),
          reads=[HFr(0, 16, 3584, 3712), HFr(0, 1, 0, 2048)], writes=[HFr(0, 1, 0, 2048)], tag="ct_sc")
    for ch in range(4):
        S.op("tensor", lambda e, ch=ch: e.matmul(B[6][:, 0:512], M["ones_rf"][0:1, 0:128],
                                                 hf[0:1, 512 * ch:512 * (ch + 1)], start=True, stop=True),
             reads=[HFr(0, 1, 0, 2048), CONST["ones_row_f32"]], writes=[PB(6)], tag="ctbc")
        S.op("scalar", lambda e, ch=ch: e.activation(hb[:, CTB + 512 * ch:CTB + 512 * (ch + 1)],
                                                     B[6][:, 0:512], AF.Copy),
             reads=[PB(6)], writes=[PB(6), HB(CTB + 512 * ch, CTB + 512 * (ch + 1))], tag="ctbe")

    c13a = float(np.float32(13 * math.log(2)))
    c13b = float(np.float32(13 * math.log(2) - np.float32(13 * math.log(2))))
    S.op("vector", lambda e: e.memset(hf[:, 3600:3601], 1.220703125), writes=[HFr(0, 128, 3600, 3601)], tag="tb0")
    S.op("scalar", lambda e: e.activation(hf[:, 3600:3601], hf[:, 3600:3601], AF.Ln),
         reads=[HFr(0, 128, 3600, 3601)], writes=[HFr(0, 128, 3600, 3601)], tag="tb1")
    S.op("vector", lambda e: e.tensor_scalar(hf[:, 3600:3601], hf[:, 3600:3601], c13a, c13b,
                                             op0=OP.add, op1=OP.add),
         reads=[HFr(0, 128, 3600, 3601)], writes=[HFr(0, 128, 3600, 3601)], tag="tb2")
    S.dma("sync", lambda e: e.dma_start(hf[:, 3601:3602], dt["j128_col"].ap()),
          writes=[HFr(0, 128, 3601, 3602)], tag="tb3")
    S.op("vector", lambda e: e.tensor_tensor(hf[:, 3602:3603], hf[:, 3601:3602], hf[:, 3600:3601], op=OP.mult),
         reads=[HFr(0, 128, 3600, 3602)], writes=[HFr(0, 128, 3602, 3603)], tag="tb4")
    S.op("scalar", lambda e: e.activation(hf[:, 3602:3603], hf[:, 3602:3603], AF.Exp, scale=-1.0),
         reads=[HFr(0, 128, 3602, 3603)], writes=[HFr(0, 128, 3602, 3603)], tag="tb5")
    C1 = float(np.float32(2 * math.pi))
    DLT = float(np.float32(np.float64(np.float32(2 * math.pi)) - 2 * math.pi))
    INVC1 = float(np.float32(1.0 / (2 * math.pi)))
    HALF = float(np.float32(math.pi))
    PLO = float(np.nextafter(np.float32(math.pi), np.float32(0)))
    QPI = float(np.float32(math.pi / 2))
    for tgt, tcol, shift in ((TABC, 0, QPI), (TABS, 1, 0.0)):
        for ch in range(4):
            S.op("tensor", lambda e, ch=ch: e.matmul(B[6][:, 0:512], M["ones_rf"][0:1, 0:128],
                                                     M["pos"][0:1, 512 * ch:512 * (ch + 1)], start=True, stop=True),
                 reads=[CONST["pos_row"], CONST["ones_row_f32"]], writes=[PB(6)], tag="ph0")
            S.op("vector", lambda e, shift=shift: e.tensor_scalar(hf[:, 2048:2560], B[6][:, 0:512],
                                                                  hf[:, 3602:3603], shift,
                                                                  op0=OP.mult, op1=OP.add),
                 reads=[PB(6), HFr(0, 128, 3602, 3603)], writes=[PB(6), HFr(0, 128, 2048, 2560)], tag="ph1")
            if shift != 0.0:
                S.op("vector", lambda e: e.tensor_scalar(hf[:, 2048:2560], hf[:, 2048:2560],
                                                         -4.371139000186241e-08, None, op0=OP.add),
                     reads=[HFr(0, 128, 2048, 2560)], writes=[HFr(0, 128, 2048, 2560)], tag="ph1b")
            S.op("vector", lambda e: e.tensor_scalar(hf[:, 2560:3072], hf[:, 2048:2560], INVC1, None,
                                                     op0=OP.mult),
                 reads=[HFr(0, 128, 2048, 2560)], writes=[HFr(0, 128, 2560, 3072)], tag="ph2")
            S.op("vector", lambda e: e.tensor_copy(ki[:, 0:512], hf[:, 2560:3072]),
                 reads=[HFr(0, 128, 2560, 3072)], writes=[R("ki", 0, 128, 0, 512)], tag="ph3")
            S.op("vector", lambda e: e.tensor_copy(hf[:, 2560:3072], ki[:, 0:512]),
                 reads=[R("ki", 0, 128, 0, 512)], writes=[HFr(0, 128, 2560, 3072)], tag="ph4")
            S.op("vector", lambda e: e.tensor_scalar(hf[:, 3072:3584], hf[:, 2560:3072], C1, None,
                                                     op0=OP.mult),
                 reads=[HFr(0, 128, 2560, 3072)], writes=[HFr(0, 128, 3072, 3584)], tag="ph5")
            S.op("vector", lambda e: e.tensor_tensor(hf[:, 2048:2560], hf[:, 2048:2560], hf[:, 3072:3584],
                                                     op=OP.subtract),
                 reads=[HFr(0, 128, 2048, 2560), HFr(0, 128, 3072, 3584)],
                 writes=[HFr(0, 128, 2048, 2560)], tag="ph6")
            S.op("vector", lambda e: e.tensor_scalar(hf[:, 2560:3072], hf[:, 2560:3072], DLT, None,
                                                     op0=OP.mult),
                 reads=[HFr(0, 128, 2560, 3072)], writes=[HFr(0, 128, 2560, 3072)], tag="ph7")
            S.op("vector", lambda e: e.tensor_tensor(hf[:, 2048:2560], hf[:, 2048:2560], hf[:, 2560:3072],
                                                     op=OP.add),
                 reads=[HFr(0, 128, 2048, 2560), HFr(0, 128, 2560, 3072)],
                 writes=[HFr(0, 128, 2048, 2560)], tag="ph8")
            S.op("vector", lambda e: e.tensor_scalar(hf[:, 2560:3072], hf[:, 2048:2560], HALF, C1,
                                                     op0=OP.is_ge, op1=OP.mult),
                 reads=[HFr(0, 128, 2048, 2560)], writes=[HFr(0, 128, 2560, 3072)], tag="ph9")
            S.op("vector", lambda e: e.tensor_tensor(hf[:, 2048:2560], hf[:, 2048:2560], hf[:, 2560:3072],
                                                     op=OP.subtract),
                 reads=[HFr(0, 128, 2048, 2560), HFr(0, 128, 2560, 3072)],
                 writes=[HFr(0, 128, 2048, 2560)], tag="ph10")
            S.op("vector", lambda e: e.tensor_scalar(hf[:, 2048:2560], hf[:, 2048:2560], PLO, -PLO,
                                                     op0=OP.min, op1=OP.max),
                 reads=[HFr(0, 128, 2048, 2560)], writes=[HFr(0, 128, 2048, 2560)], tag="ph11")
            S.op("scalar", lambda e, tgt=tgt, ch=ch: e.activation(
                hb[:, tgt + 512 * ch:tgt + 512 * (ch + 1)], hf[:, 2048:2560], AF.Sin),
                reads=[HFr(0, 128, 2048, 2560)], writes=[HB(tgt + 512 * ch, tgt + 512 * (ch + 1))], tag="ph12")
    for r0 in (0, 64):
        S.op("vector", lambda e, r0=r0: e.tensor_scalar(hb[r0:r0 + 32, TABS:TABS + 2048],
                                                        hb[r0:r0 + 32, TABS:TABS + 2048], -1.0, None,
                                                        op0=OP.mult),
             reads=[HB(TABS, TABS + 2048)], writes=[HB(TABS, TABS + 2048)], tag="sneg")

    xnT_all = R("xnT", 0, 128, 0, NDT * N)

    def build_xlw(w):
        for j in range(NDT):
            S.op("vector", lambda e, w=w, j=j: e.tensor_tensor(
                hb[:, XLW + 512 * j:XLW + 512 * (j + 1)], xnT[:, N * j + 512 * w:N * j + 512 * (w + 1)],
                hb[:, CTB + 512 * w:CTB + 512 * (w + 1)], op=OP.mult),
                reads=[xnT_all, HB(CTB, CTB + 2048)], writes=[HB(XLW + 512 * j, XLW + 512 * (j + 1))],
                tag="xl0")
            S.op("scalar", lambda e, j=j: e.activation(
                hb[:, XLW + 512 * j:XLW + 512 * (j + 1)], hb[:, XLW + 512 * j:XLW + 512 * (j + 1)],
                AF.Identity, bias=hf[:, 4200 + 8 + j:4200 + 9 + j], scale=hf[:, 4200 + j:4200 + 1 + j]),
                reads=[HB(XLW + 512 * j, XLW + 512 * (j + 1)), HFr(0, 128, 4200, 4216)],
                writes=[HB(XLW + 512 * j, XLW + 512 * (j + 1))], tag="xl1")

    S.dma("sync", lambda e: e.dma_start(hf[:, 4200:4208], dt["lngc"].ap()),
          writes=[HFr(0, 128, 4200, 4208)], tag="ld_lng")
    S.dma("sync", lambda e: e.dma_start(hf[:, 4208:4216], dt["lnbc"].ap()),
          writes=[HFr(0, 128, 4208, 4216)], tag="ld_lnb")

    wq_ap = dt["wqkvT"].ap().rearrange("p (j o) -> p j o", j=8)

    for half in range(LOC_HALVES):
        hbase = 8 * half
        for kind, dst in (("q", QH), ("k", KH)):
            ocol = (0 if kind == "q" else 1024) + 512 * half
            S.dma("gpsimd", lambda e, ocol=ocol: e.dma_start(
                hb[:, WST:WST + 4096].rearrange("p (j o) -> p j o", j=8),
                wq_ap[:, :, ocol:ocol + 512]),
                reads=[HB(WST, WST + 4096)], writes=[HB(WST, WST + 4096)], tag="ld_wl")
            for w in range(NW):
                build_xlw(w)
                for mt in range(4):
                    b = mt % 2
                    def mmp(e, w=w, mt=mt, b=b):
                        last = None
                        for j in range(NDT):
                            last = e.matmul(B[b][:, 0:512],
                                            hb[:, WST + 512 * j + 128 * mt:WST + 512 * j + 128 * (mt + 1)],
                                            hb[:, XLW + 512 * j:XLW + 512 * (j + 1)],
                                            start=(j == 0), stop=(j == NDT - 1))
                        return last
                    S.op("tensor", mmp, reads=[HB(WST, WST + 4096), HB(XLW, XLW + 4096)],
                         writes=[PB(b)], tag="lqk_mm")
                    S.op("scalar", lambda e, dst=dst, mt=mt, w=w, b=b: e.activation(
                        hb[:, dst + 2048 * mt + 512 * w:dst + 2048 * mt + 512 * (w + 1)], B[b][:, 0:512],
                        AF.Copy),
                        reads=[PB(b)],
                        writes=[PB(b), HB(dst + 2048 * mt + 512 * w, dst + 2048 * mt + 512 * (w + 1))],
                        tag="lqk_ev")
            for sl in range(4):
                r0 = 32 * sl
                pr = 32 * (sl + 1 if sl % 2 == 0 else sl - 1)
                S.dma("sync", lambda e, dst=dst, r0=r0, pr=pr: e.dma_start(
                    hb[pr:pr + 32, PRT:PRT + 8192], hb[r0:r0 + 32, dst:dst + 8192]),
                    reads=[HB(dst, dst + 8192), R("hb", pr, pr + 32, PRT, PRT + 8192)],
                    writes=[R("hb", pr, pr + 32, PRT, PRT + 8192)], tag="rp_sw")
            for mt in range(4):
                sc = 0.125 if kind == "q" else 1.0
                S.op("vector", lambda e, dst=dst, mt=mt: e.tensor_tensor(
                    hb[:, PRT + 2048 * mt:PRT + 2048 * (mt + 1)], hb[:, PRT + 2048 * mt:PRT + 2048 * (mt + 1)],
                    hb[:, TABS:TABS + 2048], op=OP.mult),
                    reads=[R("hb", 0, 128, PRT + 2048 * mt, PRT + 2048 * (mt + 1)), HB(TABS, TABS + 2048)],
                    writes=[R("hb", 0, 128, PRT + 2048 * mt, PRT + 2048 * (mt + 1))], tag="rp_m1")
                S.op("vector", lambda e, dst=dst, mt=mt: e.tensor_tensor(
                    hb[:, dst + 2048 * mt:dst + 2048 * (mt + 1)], hb[:, dst + 2048 * mt:dst + 2048 * (mt + 1)],
                    hb[:, TABC:TABC + 2048], op=OP.mult),
                    reads=[HB(dst + 2048 * mt, dst + 2048 * (mt + 1)), HB(TABC, TABC + 2048)],
                    writes=[HB(dst + 2048 * mt, dst + 2048 * (mt + 1))], tag="rp_m2")
                def rp3(e, dst=dst, mt=mt, sc=sc):
                    if sc == 1.0:
                        return e.tensor_tensor(hb[:, dst + 2048 * mt:dst + 2048 * (mt + 1)],
                                               hb[:, dst + 2048 * mt:dst + 2048 * (mt + 1)],
                                               hb[:, PRT + 2048 * mt:PRT + 2048 * (mt + 1)], op=OP.add)
                    return e.scalar_tensor_tensor(hb[:, dst + 2048 * mt:dst + 2048 * (mt + 1)],
                                                  hb[:, dst + 2048 * mt:dst + 2048 * (mt + 1)], 1.0,
                                                  hb[:, PRT + 2048 * mt:PRT + 2048 * (mt + 1)],
                                                  op0=OP.mult, op1=OP.add)
                S.op("vector", rp3, reads=[HB(dst + 2048 * mt, dst + 2048 * (mt + 1)),
                                           R("hb", 0, 128, PRT + 2048 * mt, PRT + 2048 * (mt + 1))],
                     writes=[HB(dst + 2048 * mt, dst + 2048 * (mt + 1))], tag="rp_a")
                if sc != 1.0:
                    S.op("vector", lambda e, dst=dst, mt=mt: e.tensor_scalar(
                        hb[:, dst + 2048 * mt:dst + 2048 * (mt + 1)],
                        hb[:, dst + 2048 * mt:dst + 2048 * (mt + 1)], 0.125, None, op0=OP.mult),
                        reads=[HB(dst + 2048 * mt, dst + 2048 * (mt + 1))],
                        writes=[HB(dst + 2048 * mt, dst + 2048 * (mt + 1))], tag="rp_s")

        S.dma("gpsimd", lambda e, half=half: e.dma_start(
            hb[:, WST:WST + 4096].rearrange("p (j o) -> p j o", j=8),
            wq_ap[:, :, 2048 + 512 * half:2048 + 512 * (half + 1)]),
            reads=[HB(WST, WST + 4096)], writes=[HB(WST, WST + 4096)], tag="ld_wv2")

        def head_rows(h):
            mt = (h % 8) // 2
            r0 = 64 * (h % 2)
            return mt, r0

        for w in range(LOC_WINDOWS):
            vdst = VR0 if w % 2 == 0 else VR1
            build_xlw(w)
            for tt in range(4):
                b = tt % 2
                def mmv(e, tt=tt, b=b, w=w):
                    last = None
                    for j in range(NDT):
                        last = e.matmul(B[b][:, 0:512],
                                        hb[:, XLW + 512 * j + 128 * tt:XLW + 512 * j + 128 * tt + 128],
                                        hb[:, WST + 512 * j:WST + 512 * (j + 1)],
                                        start=(j == 0), stop=(j == NDT - 1))
                    return last
                S.op("tensor", mmv, reads=[HB(WST, WST + 4096), HB(XLW, XLW + 4096)],
                     writes=[PB(b)], tag="lv_mm")
                def evv(e, vdst=vdst, tt=tt, b=b):
                    return e.tensor_copy(
                        hb[:, vdst + 520 * tt:vdst + 520 * tt + 520].rearrange("p (h c) -> p h c", h=8)[:, :, 0:64],
                        B[b][:, 0:512].rearrange("p (h c) -> p h c", h=8))
                S.op("vector", evv, reads=[PB(b)],
                     writes=[PB(b), HB(vdst + 520 * tt, vdst + 520 * tt + 520)], tag="lv_ev")
                S.op("vector", lambda e, vdst=vdst, tt=tt: e.memset(
                    hb[:, vdst + 520 * tt:vdst + 520 * tt + 520].rearrange("p (h c) -> p h c", h=8)[:, :, 64:65], 1.0),
                    reads=[HB(vdst + 520 * tt, vdst + 520 * tt + 520)],
                    writes=[HB(vdst + 520 * tt, vdst + 520 * tt + 520)], tag="lv_on")

            for h8 in range(8):
                h = hbase + h8
                mt, r0 = head_rows(h8 + 0)
                qsl = QH + 2048 * mt + 512 * w
                ksl = KH + 2048 * mt + 512 * w
                S.op("scalar", lambda e, qsl=qsl, r0=r0: e.activation(
                    M["qa"][r0:r0 + 64, 0:512], hb[r0:r0 + 64, qsl:qsl + 512], AF.Square),
                    reads=[R("hb", r0, r0 + 64, qsl, qsl + 512)], writes=[R("qa", r0, r0 + 64, 0, 512)],
                    tag="lb_q2")
                S.op("tensor", lambda e, r0=r0: e.matmul(B[4][0:1, 0:512], M["ones_col"][r0:r0 + 64, 0:1],
                                                         M["qa"][r0:r0 + 64, 0:512], start=True, stop=True),
                     reads=[R("qa", r0, r0 + 64, 0, 512), CONST["ones_col_bf"]],
                     writes=[R("pb4", 0, 1, 0, 512)], tag="lb_qm")
                S.op("vector", lambda e, w=w, h8=h8: e.reduce_max(
                    rowf[0:1, 1100 + 32 * w + h8:1101 + 32 * w + h8], B[4][0:1, 0:512], axis=AX.X),
                    reads=[R("pb4", 0, 1, 0, 512)],
                    writes=[R("pb4", 0, 1, 0, 512), RF(0, 1, 1100 + 32 * w + h8, 1101 + 32 * w + h8)], tag="lb_qx")
                S.op("scalar", lambda e, ksl=ksl, r0=r0: e.activation(
                    M["qa"][r0:r0 + 64, 0:512], hb[r0:r0 + 64, ksl:ksl + 512], AF.Square),
                    reads=[R("hb", r0, r0 + 64, ksl, ksl + 512)], writes=[R("qa", r0, r0 + 64, 0, 512)],
                    tag="lb_k2")
                S.op("tensor", lambda e, r0=r0: e.matmul(B[4][0:1, 0:512], M["ones_col"][r0:r0 + 64, 0:1],
                                                         M["qa"][r0:r0 + 64, 0:512], start=True, stop=True),
                     reads=[R("qa", r0, r0 + 64, 0, 512), CONST["ones_col_bf"]],
                     writes=[R("pb4", 0, 1, 0, 512)], tag="lb_km")
                S.op("vector", lambda e, w=w, h8=h8: e.reduce_max(
                    rowf[0:1, 1100 + 16 + 32 * w + h8:1101 + 16 + 32 * w + h8], B[4][0:1, 0:512], axis=AX.X),
                    reads=[R("pb4", 0, 1, 0, 512)],
                    writes=[R("pb4", 0, 1, 0, 512), RF(0, 1, 1116 + 32 * w + h8, 1117 + 32 * w + h8)], tag="lb_kx")

            nkt = 4 if w == 0 else 8
            for h8 in range(8):
                mt, r0 = head_rows(h8)
                qsl = QH + 2048 * mt + 512 * w
                c0 = 1230 + 32 * w + h8
                def mkM(e, w=w, h8=h8, c0=c0):
                    return e.tensor_tensor(rowf[0:1, c0:c0 + 1], rowf[0:1, 1116 + 32 * w + h8:1117 + 32 * w + h8],
                                           rowf[0:1, 1116 + 32 * (w - 1) + h8:1117 + 32 * (w - 1) + h8]
                                           if w >= 1 else rowf[0:1, 1116 + 32 * w + h8:1117 + 32 * w + h8],
                                           op=OP.max)
                S.op("vector", mkM, reads=[RF(0, 1, 1100, 1230)], writes=[RF(0, 1, c0, c0 + 1)], tag="lM0")
                S.op("vector", lambda e, w=w, h8=h8, c0=c0: e.tensor_tensor(
                    rowf[0:1, c0:c0 + 1], rowf[0:1, c0:c0 + 1],
                    rowf[0:1, 1100 + 32 * w + h8:1101 + 32 * w + h8], op=OP.mult),
                    reads=[RF(0, 1, c0, c0 + 1), RF(0, 1, 1100, 1230)], writes=[RF(0, 1, c0, c0 + 1)], tag="lM1")
                S.op("scalar", lambda e, c0=c0: e.activation(rowf[0:1, c0:c0 + 1], rowf[0:1, c0:c0 + 1], AF.Sqrt),
                     reads=[RF(0, 1, c0, c0 + 1)], writes=[RF(0, 1, c0, c0 + 1)], tag="lM2")
                S.op("vector", lambda e, c0=c0: e.tensor_scalar(qx[0:1, 0:512], qx[0:1, 0:512], 0.0,
                                                                rowf[0:1, c0:c0 + 1],
                                                                op0=OP.mult, op1=OP.subtract),
                     reads=[R("qx", 0, 1, 0, 512), RF(0, 1, c0, c0 + 1)], writes=[R("qx", 0, 1, 0, 512)],
                     tag="lqex")
                for kt in range(nkt):
                    b = 4 + (kt % 2)
                    kw = w - 1 + kt // 4 if w >= 1 else w
                    kk_t = kt % 4
                    ksl = KH + 2048 * mt + 512 * kw + 128 * kk_t
                    def mms(e, ksl=ksl, qsl=qsl, r0=r0, b=b):
                        e.matmul(B[b][:, 0:512], hb[r0:r0 + 64, ksl:ksl + 128], hb[r0:r0 + 64, qsl:qsl + 512],
                                 start=True, stop=False)
                        return e.matmul(B[b][:, 0:512], M["ones_row"][0:1, 0:128], qx[0:1, 0:512],
                                        start=False, stop=True)
                    S.op("tensor", mms, reads=[HB(KH, KH + 8192), HB(QH, QH + 8192),
                                               R("qx", 0, 1, 0, 512), CONST["ones_row_bf"]],
                         writes=[PB(b)], tag="ls_mm")
                    S.op("scalar", lambda e, kt=kt, b=b: e.activation(
                        hb[:, PTL + 512 * kt:PTL + 512 * (kt + 1)], B[b][:, 0:512], AF.Exp),
                        reads=[PB(b)], writes=[PB(b), HB(PTL + 512 * kt, PTL + 512 * (kt + 1))], tag="ls_ex")
                    def mkmask(e, kt=kt, w=w):
                        if w == 0:
                            msl = M["cm"][:, 512 * kt:512 * (kt + 1)]
                        else:
                            msl = hb[:, BMK + 512 * kt:BMK + 512 * (kt + 1)]
                        return e.tensor_tensor(hb[:, PTL + 512 * kt:PTL + 512 * (kt + 1)],
                                               hb[:, PTL + 512 * kt:PTL + 512 * (kt + 1)], msl, op=OP.mult)
                    S.op("vector", mkmask,
                         reads=[HB(PTL + 512 * kt, PTL + 512 * (kt + 1)), CONST["cmask"], HB(BMK, BMK + 4096)],
                         writes=[HB(PTL + 512 * kt, PTL + 512 * (kt + 1))], tag="ls_mk")
                for qs in range(4):
                    def mo(e, qs=qs, nkt=nkt, w=w, h8=h8):
                        last = None
                        for kt in range(nkt):
                            kw = w - 1 + kt // 4 if w >= 1 else w
                            vdst = VR0 if kw % 2 == 0 else VR1
                            last = e.matmul(B[2][:, 0:65],
                                            hb[:, PTL + 512 * kt + 128 * qs:PTL + 512 * kt + 128 * (qs + 1)],
                                            hb[:, vdst + 520 * (kt % 4) + 65 * h8:vdst + 520 * (kt % 4) + 65 * (h8 + 1)],
                                            start=(kt == 0), stop=(kt == nkt - 1))
                        return last
                    S.op("tensor", mo, reads=[HB(PTL, PTL + 4096), HB(VR0, VR0 + 4160)],
                         writes=[R("pb2", 0, 128, 0, 65)], tag="lo_mm")
                    S.op("scalar", lambda e: e.activation(hf[:, 4184:4185], B[2][:, 64:65], AF.Ln,
                                                          scale=2.0 ** 52),
                         reads=[R("pb2", 0, 128, 0, 65)], writes=[HFr(0, 128, 4184, 4185)], tag="lo_ln")
                    S.op("scalar", lambda e: e.activation(hf[:, 4184:4185], hf[:, 4184:4185], AF.Exp,
                                                          scale=-1.0, bias=hf[:, 4185:4186]),
                         reads=[HFr(0, 128, 4184, 4186)], writes=[HFr(0, 128, 4184, 4185)], tag="lo_ex")
                    S.op("scalar", lambda e, qs=qs, h8=h8: e.activation(
                        hb[:, ATT + 512 * qs + 64 * h8:ATT + 512 * qs + 64 * (h8 + 1)], B[2][:, 0:64],
                        AF.Identity, scale=hf[:, 4184:4185]),
                        reads=[R("pb2", 0, 128, 0, 65), HFr(0, 128, 4184, 4185)],
                        writes=[R("pb2", 0, 128, 0, 65),
                                HB(ATT + 512 * qs + 64 * h8, ATT + 512 * qs + 64 * (h8 + 1))], tag="lo_ev")
            for tt in range(4):
                for jj in range(4):
                    po = 512 * (jj % 2)
                    b = (tt + jj) % 2
                    S.op("tensor", lambda e, tt=tt, jj=jj, b=b: e.matmul(
                        B[b][:, 0:128], hb[:, ATT + 512 * tt + 128 * jj:ATT + 512 * tt + 128 * (jj + 1)],
                        M["idb"][:, :], start=True, stop=True),
                        reads=[HB(ATT, ATT + 2048), CONST["idb"]], writes=[PB(b)], tag="lt_T")
                    S.op("vector", lambda e, tt=tt, jj=jj, b=b: e.tensor_copy(
                        hb[:, ATTT + 512 * jj + 128 * tt:ATTT + 512 * jj + 128 * (tt + 1)], B[b][:, 0:128]),
                        reads=[PB(b)],
                        writes=[PB(b),
                                HB(ATTT + 512 * jj + 128 * tt, ATTT + 512 * jj + 128 * (tt + 1))], tag="lt_e")
            if w == 0 and half == 0:
                pass
            S.dma("gpsimd", lambda e, half=half: e.dma_start(
                hb[:, XLW:XLW + 4096].rearrange("p (j o) -> p j o", j=4),
                dt["wloT"].ap().rearrange("p (j o) -> p j o", j=8)[:, 4 * half:4 * half + 4, :]),
                reads=[HB(XLW, XLW + 4096)], writes=[HB(XLW, XLW + 4096)], tag="ld_wlo")
            for tt in range(4):
                i_tok = 4 * w + tt
                r = 1024 * (i_tok % 2)
                xr = R("hf", 0, 128, r, r + D)
                for chv in range(2):
                    b = 2 + chv
                    def mmo2(e, tt=tt, chv=chv, b=b):
                        last = None
                        for jj in range(4):
                            last = e.matmul(B[b][:, 0:512],
                                            hb[:, ATTT + 512 * jj + 128 * tt:ATTT + 512 * jj + 128 * (tt + 1)],
                                            hb[:, XLW + 1024 * jj + 512 * chv:XLW + 1024 * jj + 512 * (chv + 1)],
                                            start=(jj == 0), stop=(jj == 3))
                        return last
                    S.op("tensor", mmo2, reads=[HB(ATTT, ATTT + 2048), HB(XLW, XLW + 4096)],
                         writes=[PB(b)], tag="lp_mm")
                src_x = dt["x"] if False else xres
                S.dma("sync", lambda e, i_tok=i_tok, r=r: e.dma_start(
                    hf[:, r:r + D], xres.ap()[128 * i_tok:128 * (i_tok + 1), :]),
                    reads=[xr, R("xres", i_tok, i_tok + 1, 0, 1)], writes=[xr], tag="lx_ld")
                for chv in range(2):
                    b = 2 + chv
                    S.op("vector", lambda e, r=r, chv=chv, b=b, i_tok=i_tok: e.scalar_tensor_tensor(
                        hf[:, r + 512 * chv:r + 512 * (chv + 1)], B[b][:, 0:512],
                        stats[:, 112 + i_tok:113 + i_tok], hf[:, r + 512 * chv:r + 512 * (chv + 1)],
                        op0=OP.mult, op1=OP.add),
                        reads=[PB(b), STT(112 + i_tok, 113 + i_tok), R("hf", 0, 128, r + 512 * chv, r + 512 * (chv + 1))],
                        writes=[PB(b), R("hf", 0, 128, r + 512 * chv, r + 512 * (chv + 1))], tag="lx_acc")
                S.dma("sync", lambda e, i_tok=i_tok, r=r: e.dma_start(
                    xres.ap()[128 * i_tok:128 * (i_tok + 1), :], hf[:, r:r + D]),
                    reads=[xr], writes=[R("xres", i_tok, i_tok + 1, 0, 1)], tag="lx_st")


def _l0_grouped(S, dt, xres, M, PB, CONST):
    xnT, hb, hf, rowf, stats, qa, qx = (M["xnT"], M["hb"], M["hf"], M["rowf"], M["stats"],
                                        M["qa"], M["qx"])
    B = M["B"]
    HB = lambda c0, c1: R("hb", 0, 128, c0, c1)
    HFr = lambda r0, r1, c0, c1: R("hf", r0, r1, c0, c1)
    RF = lambda r0, r1, c0, c1: R("rowf", r0, r1, c0, c1)
    STT = lambda c0, c1: R("stats", 0, 128, c0, c1)

    GK, GV, GQ = 0, 12288, 24768
    PT3 = 28864
    ATT3 = 35008
    WS3 = 39104
    WS3B = GQ
    PN3 = 43200
    NK3 = 43712
    NV3 = 43720
    IRMB = 44760

    for i in range(4, NT):
        w = i // 4 - 1
        for g in range(2):
            c = 3072 + 2 * (i - 4) + g
            S.op("vector", lambda e, i=i, w=w, g=g, c=c: e.tensor_tensor(
                hf[:, c:c + 1], M["selcol"][:, 8 * (i % 4) + 2 * w + g:8 * (i % 4) + 2 * w + g + 1],
                stats[:, 112 + i:113 + i], op=OP.mult),
                reads=[R("selcol", 0, 128, 8 * (i % 4), 8 * (i % 4) + 8), STT(112 + i, 113 + i)],
                writes=[HFr(0, 128, c, c + 1)], tag="gqw")

    S.op("tensor", lambda e: e.transpose(B[6][0:16, 0:128], stats[:, 96:112], M["idf"][:, :]),
         reads=[STT(96, 112), CONST["idf"]], writes=[R("pb6", 0, 128, 0, 128)], tag="irT")
    S.op("vector", lambda e: e.tensor_copy(hf[0:16, 3584:3712], B[6][0:16, 0:128]),
         reads=[R("pb6", 0, 128, 0, 128)],
         writes=[R("pb6", 0, 128, 0, 128), HFr(0, 16, 3584, 3712)], tag="irTe")
    S.dma("sync", lambda e: e.dma_start(hf[0:1, 0:2048], hf[0:16, 3584:3712]),
          reads=[HFr(0, 16, 3584, 3712), HFr(0, 1, 0, 2048)], writes=[HFr(0, 1, 0, 2048)], tag="ir_sc")
    for ch in range(4):
        S.op("tensor", lambda e, ch=ch: e.matmul(B[6][:, 0:512], M["ones_rf"][0:1, 0:128],
                                                 hf[0:1, 512 * ch:512 * (ch + 1)], start=True, stop=True),
             reads=[HFr(0, 1, 0, 2048), CONST["ones_row_f32"]], writes=[PB(6)], tag="irbc")
        S.op("scalar", lambda e, ch=ch: e.activation(hb[:, IRMB + 512 * ch:IRMB + 512 * (ch + 1)],
                                                     B[6][:, 0:512], AF.Copy),
             reads=[PB(6)], writes=[PB(6), HB(IRMB + 512 * ch, IRMB + 512 * (ch + 1))], tag="irbe")
    for j in range(NDT):
        for c in range(4):
            S.op("vector", lambda e, j=j, c=c: e.tensor_tensor(
                xnT[:, N * j + 512 * c:N * j + 512 * (c + 1)],
                xnT[:, N * j + 512 * c:N * j + 512 * (c + 1)],
                hb[:, IRMB + 512 * c:IRMB + 512 * (c + 1)], op=OP.mult),
                reads=[R("xnT", 0, 128, N * j + 512 * c, N * j + 512 * (c + 1)),
                       HB(IRMB + 512 * c, IRMB + 512 * (c + 1))],
                writes=[R("xnT", 0, 128, N * j + 512 * c, N * j + 512 * (c + 1))], tag="gxr")

    wkv_ap = dt["wgkvT"].ap().rearrange("g p (j o) -> g p j o", j=8)
    wq_ap = dt["wgqT"].ap().rearrange("g p (j o) -> g p j o", j=8)
    wo_ap = dt["wgoT"].ap().rearrange("g p (j o) -> g p j o", j=8)

    def load_ws(dst, src_ap, tag):
        S.dma("gpsimd", lambda e: e.dma_start(
            hb[:, dst:dst + 4096].rearrange("p (j o) -> p j o", j=8), src_ap),
            reads=[HB(dst, dst + 4096)], writes=[HB(dst, dst + 4096)], tag=tag)

    S.dma("sync", lambda e: e.dma_start(hf[:, 3232:3248], dt["qngc"].ap()),
          reads=[HFr(0, 128, 3232, 3248)], writes=[HFr(0, 128, 3232, 3248)], tag="ld_qng")
    S.dma("sync", lambda e: e.dma_start(hf[:, 3248:3264], dt["cngc"].ap()),
          reads=[HFr(0, 128, 3248, 3264)], writes=[HFr(0, 128, 3248, 3264)], tag="ld_cng")

    def gain_ws(gc0, g, tag):
        for j in range(NDT):
            c = gc0 + 8 * g + j
            S.op("scalar", lambda e, j=j, c=c: e.activation(
                hb[:, WS3 + 512 * j:WS3 + 512 * (j + 1)], hb[:, WS3 + 512 * j:WS3 + 512 * (j + 1)],
                AF.Identity, scale=hf[:, c:c + 1]),
                reads=[HB(WS3 + 512 * j, WS3 + 512 * (j + 1)), HFr(0, 128, c, c + 1)],
                writes=[HB(WS3 + 512 * j, WS3 + 512 * (j + 1))], tag=tag)

    for g in range(2):
        S.dma("gpsimd", lambda e, g=g: e.dma_start(hb[:, NK3:NK3 + 8], dt["nullk"].ap()[g]),
              reads=[HB(NK3, NK3 + 8)], writes=[HB(NK3, NK3 + 8)], tag="ld_nk")
        S.dma("gpsimd", lambda e, g=g: e.dma_start(hb[0:1, NV3:NV3 + 1040], dt["nullv_row"].ap()[g:g + 1, :]),
              reads=[R("hb", 0, 1, NV3, NV3 + 1040)], writes=[R("hb", 0, 1, NV3, NV3 + 1040)], tag="ld_nv")

        for kc in range(2):
            load_ws(WS3, wkv_ap[g][:, :, 512 * kc:512 * (kc + 1)], "ld_gkv")
            gain_ws(3248, g, "ggc")
            for m4 in range(4):
                mt = 4 * kc + m4
                for ch in range(3):
                    b = (mt + ch) % 2
                    def mmk(e, m4=m4, ch=ch, b=b):
                        last = None
                        for j in range(NDT):
                            last = e.matmul(B[b][:, 0:512],
                                            hb[:, WS3 + 512 * j + 128 * m4:WS3 + 512 * j + 128 * (m4 + 1)],
                                            xnT[:, N * j + 512 * ch:N * j + 512 * (ch + 1)],
                                            start=(j == 0), stop=(j == NDT - 1))
                        return last
                    S.op("tensor", mmk, reads=[HB(WS3, WS3 + 4096), R("xnT", 0, 128, 0, NDT * N)],
                         writes=[PB(b)], tag="gk_mm")
                    S.op("scalar", lambda e, mt=mt, ch=ch, b=b: e.activation(
                        hb[:, GK + 1536 * mt + 512 * ch:GK + 1536 * mt + 512 * (ch + 1)], B[b][:, 0:512],
                        AF.Copy),
                        reads=[PB(b)],
                        writes=[PB(b), HB(GK + 1536 * mt + 512 * ch, GK + 1536 * mt + 512 * (ch + 1))],
                        tag="gk_ev")
        for vc in range(2):
            load_ws(WS3, wkv_ap[g][:, :, 1024 + 512 * vc:1024 + 512 * (vc + 1)], "ld_gvw")
            gain_ws(3248, g, "ggv")
            for t in range(12):
                b = (t + vc) % 2
                def mmv3(e, t=t, b=b):
                    last = None
                    for j in range(NDT):
                        last = e.matmul(B[b][:, 0:512],
                                        xnT[:, N * j + 128 * t:N * j + 128 * (t + 1)],
                                        hb[:, WS3 + 512 * j:WS3 + 512 * (j + 1)],
                                        start=(j == 0), stop=(j == NDT - 1))
                    return last
                S.op("tensor", mmv3, reads=[HB(WS3, WS3 + 4096), R("xnT", 0, 128, 0, NDT * N)],
                     writes=[PB(b)], tag="gv_mm")
                def evv3(e, t=t, vc=vc, b=b):
                    return e.tensor_copy(
                        hb[:, GV + 1040 * t + 520 * vc:GV + 1040 * t + 520 * (vc + 1)]
                        .rearrange("p (h c) -> p h c", h=8)[:, :, 0:64],
                        B[b][:, 0:512].rearrange("p (h c) -> p h c", h=8))
                S.op("vector", evv3, reads=[PB(b)],
                     writes=[PB(b), HB(GV + 1040 * t + 520 * vc, GV + 1040 * t + 520 * (vc + 1))],
                     tag="gv_ev")
        for t in range(12):
            S.op("vector", lambda e, t=t: e.memset(
                hb[:, GV + 1040 * t:GV + 1040 * (t + 1)].rearrange("p (h c) -> p h c", h=16)[:, :, 64:65],
                1.0),
                reads=[HB(GV + 1040 * t, GV + 1040 * (t + 1))],
                writes=[HB(GV + 1040 * t, GV + 1040 * (t + 1))], tag="gv_on")

        for mt in range(NDT):
            for ch in range(3):
                S.op("scalar", lambda e, mt=mt, ch=ch: e.activation(
                    qa[:, 0:512], hb[:, GK + 1536 * mt + 512 * ch:GK + 1536 * mt + 512 * (ch + 1)],
                    AF.Square),
                    reads=[HB(GK + 1536 * mt + 512 * ch, GK + 1536 * mt + 512 * (ch + 1))],
                    writes=[R("qa", 0, 128, 0, 512)], tag="gkb_sq")
                for par in range(2):
                    r0 = 64 * par
                    h = 2 * mt + par
                    S.op("tensor", lambda e, r0=r0: e.matmul(
                        B[3][0:1, 0:512], M["ones_col"][r0:r0 + 64, 0:1], qa[r0:r0 + 64, 0:512],
                        start=True, stop=True),
                        reads=[R("qa", 0, 128, 0, 512), CONST["ones_col_bf"]],
                        writes=[R("pb3", 0, 1, 0, 512)], tag="gkb_mm")
                    S.op("vector", lambda e, h=h, ch=ch: e.reduce_max(
                        rowf[0:1, 1408 + 3 * h + ch:1409 + 3 * h + ch], B[3][0:1, 0:512], axis=AX.X),
                        reads=[R("pb3", 0, 1, 0, 512)],
                        writes=[R("pb3", 0, 1, 0, 512), RF(0, 1, 1408 + 3 * h + ch, 1409 + 3 * h + ch)],
                        tag="gkb_mx")
        S.op("scalar", lambda e: e.activation(qa[:, 0:8], hb[:, NK3:NK3 + 8], AF.Square),
             reads=[HB(NK3, NK3 + 8)], writes=[R("qa", 0, 128, 0, 8)], tag="gnk_sq")
        for par in range(2):
            r0 = 64 * par
            S.op("tensor", lambda e, r0=r0: e.matmul(B[3][0:1, 0:8], M["ones_col"][r0:r0 + 64, 0:1],
                                                     qa[r0:r0 + 64, 0:8], start=True, stop=True),
                 reads=[R("qa", 0, 128, 0, 8), CONST["ones_col_bf"]],
                 writes=[R("pb3", 0, 1, 0, 512)], tag="gnk_mm")
            S.op("vector", lambda e, par=par: e.tensor_copy(rowf[0:1, 1456 + 8 * par:1464 + 8 * par],
                                                            B[3][0:1, 0:8]),
                 reads=[R("pb3", 0, 1, 0, 512)],
                 writes=[R("pb3", 0, 1, 0, 512), RF(0, 1, 1456 + 8 * par, 1464 + 8 * par)], tag="gnk_ev")
        for h in range(16):
            nk2c = 1456 + 8 * (h % 2) + h // 2
            S.op("vector", lambda e, h=h, nk2c=nk2c: e.tensor_tensor(
                rowf[0:1, 1472 + 3 * h:1473 + 3 * h], rowf[0:1, 1408 + 3 * h:1409 + 3 * h],
                rowf[0:1, nk2c:nk2c + 1], op=OP.max),
                reads=[RF(0, 1, 1408, 1472)], writes=[RF(0, 1, 1472 + 3 * h, 1473 + 3 * h)], tag="gkm0")
            for wi in (1, 2):
                S.op("vector", lambda e, h=h, wi=wi: e.tensor_tensor(
                    rowf[0:1, 1472 + 3 * h + wi:1473 + 3 * h + wi],
                    rowf[0:1, 1472 + 3 * h + wi - 1:1473 + 3 * h + wi - 1],
                    rowf[0:1, 1408 + 3 * h + wi:1409 + 3 * h + wi], op=OP.max),
                    reads=[RF(0, 1, 1408, 1520)],
                    writes=[RF(0, 1, 1472 + 3 * h + wi, 1473 + 3 * h + wi)], tag="gkm1")

        for w in (1, 2, 3):
            if w == 3:
                for t in range(12):
                    S.op("vector", lambda e, t=t, g=g: e.tensor_scalar(
                        hb[:, GV + 1040 * t:GV + 1040 * (t + 1)], hb[:, GV + 1040 * t:GV + 1040 * (t + 1)],
                        M["selcol"][:, 8 * t + 6 + g:8 * t + 7 + g], None, op0=OP.mult),
                        reads=[HB(GV + 1040 * t, GV + 1040 * (t + 1)),
                               R("selcol", 0, 128, 8 * t, 8 * t + 8)],
                        writes=[HB(GV + 1040 * t, GV + 1040 * (t + 1))], tag="gv_mk")
            for qc in range(2):
                load_ws(WS3, wq_ap[g][:, :, 512 * qc:512 * (qc + 1)], "ld_gq")
                gain_ws(3232, g, "ggq")
                for m4 in range(4):
                    mt = 4 * qc + m4
                    b = (mt) % 2
                    def mmq(e, m4=m4, w=w, b=b):
                        last = None
                        for j in range(NDT):
                            last = e.matmul(B[b][:, 0:512],
                                            hb[:, WS3 + 512 * j + 128 * m4:WS3 + 512 * j + 128 * (m4 + 1)],
                                            xnT[:, N * j + 512 * w:N * j + 512 * (w + 1)],
                                            start=(j == 0), stop=(j == NDT - 1))
                        return last
                    S.op("tensor", mmq, reads=[HB(WS3, WS3 + 4096), R("xnT", 0, 128, 0, NDT * N)],
                         writes=[PB(b)], tag="gq_mm")
                    S.op("scalar", lambda e, mt=mt, b=b: e.activation(
                        hb[:, GQ + 512 * mt:GQ + 512 * (mt + 1)], B[b][:, 0:512], AF.Copy, scale=0.125),
                        reads=[PB(b)], writes=[PB(b), HB(GQ + 512 * mt, GQ + 512 * (mt + 1))], tag="gq_ev")
            for mt in range(NDT):
                S.op("scalar", lambda e, mt=mt: e.activation(qa[:, 0:512],
                                                             hb[:, GQ + 512 * mt:GQ + 512 * (mt + 1)],
                                                             AF.Square),
                     reads=[HB(GQ + 512 * mt, GQ + 512 * (mt + 1))], writes=[R("qa", 0, 128, 0, 512)],
                     tag="gqb_sq")
                for par in range(2):
                    r0 = 64 * par
                    h = 2 * mt + par
                    S.op("tensor", lambda e, r0=r0: e.matmul(
                        B[3][0:1, 0:512], M["ones_col"][r0:r0 + 64, 0:1], qa[r0:r0 + 64, 0:512],
                        start=True, stop=True),
                        reads=[R("qa", 0, 128, 0, 512), CONST["ones_col_bf"]],
                        writes=[R("pb3", 0, 1, 0, 512)], tag="gqb_mm")
                    S.op("vector", lambda e, h=h: e.reduce_max(
                        rowf[0:1, 1520 + h:1521 + h], B[3][0:1, 0:512], axis=AX.X),
                        reads=[R("pb3", 0, 1, 0, 512)],
                        writes=[R("pb3", 0, 1, 0, 512), RF(0, 1, 1520 + h, 1521 + h)], tag="gqb_mx")
            for h in range(16):
                S.op("vector", lambda e, h=h, w=w: e.tensor_tensor(
                    rowf[0:1, 1536 + h:1537 + h], rowf[0:1, 1520 + h:1521 + h],
                    rowf[0:1, 1472 + 3 * h + w - 1:1472 + 3 * h + w], op=OP.mult),
                    reads=[RF(0, 1, 1472, 1536)], writes=[RF(0, 1, 1536 + h, 1537 + h)], tag="gM0")
                S.op("scalar", lambda e, h=h: e.activation(rowf[0:1, 1536 + h:1537 + h],
                                                           rowf[0:1, 1536 + h:1537 + h], AF.Sqrt),
                     reads=[RF(0, 1, 1536 + h, 1537 + h)], writes=[RF(0, 1, 1536 + h, 1537 + h)],
                     tag="gM1")

            nkt = 4 * w
            for h in range(16):
                mt, r0 = h // 2, 64 * (h % 2)
                S.op("vector", lambda e, h=h: e.tensor_scalar(qx[0:1, 0:512], qx[0:1, 0:512], 0.0,
                                                              rowf[0:1, 1536 + h:1537 + h],
                                                              op0=OP.mult, op1=OP.subtract),
                     reads=[R("qx", 0, 1, 0, 512), RF(0, 1, 1536 + h, 1537 + h)],
                     writes=[R("qx", 0, 1, 0, 512)], tag="gqex")
                for kt in range(nkt):
                    b = 4 + (kt % 2)
                    def mms3(e, mt=mt, r0=r0, kt=kt, b=b):
                        e.matmul(B[b][:, 0:512], hb[r0:r0 + 64, GK + 1536 * mt + 128 * kt:GK + 1536 * mt + 128 * (kt + 1)],
                                 hb[r0:r0 + 64, GQ + 512 * mt:GQ + 512 * (mt + 1)], start=True, stop=False)
                        return e.matmul(B[b][:, 0:512], M["ones_row"][0:1, 0:128], qx[0:1, 0:512],
                                        start=False, stop=True)
                    S.op("tensor", mms3, reads=[HB(GK, GK + 12288), HB(GQ, GQ + 4096),
                                                R("qx", 0, 1, 0, 512), CONST["ones_row_bf"]],
                         writes=[PB(b)], tag="gs_mm")
                    S.op("scalar", lambda e, kt=kt, b=b: e.activation(
                        hb[:, PT3 + 512 * kt:PT3 + 512 * (kt + 1)], B[b][:, 0:512], AF.Exp),
                        reads=[PB(b)], writes=[PB(b), HB(PT3 + 512 * kt, PT3 + 512 * (kt + 1))],
                        tag="gs_ex")
                def mmn(e, mt=mt, r0=r0):
                    e.matmul(B[6][0:1, 0:512], hb[r0:r0 + 64, NK3 + mt:NK3 + mt + 1],
                             hb[r0:r0 + 64, GQ + 512 * mt:GQ + 512 * (mt + 1)], start=True, stop=False)
                    return e.matmul(B[6][0:1, 0:512], M["ones_row"][0:1, 0:1], qx[0:1, 0:512],
                                    start=False, stop=True)
                S.op("tensor", mmn, reads=[HB(NK3, NK3 + 8), HB(GQ, GQ + 4096),
                                           R("qx", 0, 1, 0, 512), CONST["ones_row_bf"]],
                     writes=[R("pb6", 0, 1, 0, 512)], tag="gn_mm")
                S.op("scalar", lambda e: e.activation(hb[0:1, PN3:PN3 + 512], B[6][0:1, 0:512], AF.Exp),
                     reads=[R("pb6", 0, 1, 0, 512)],
                     writes=[R("pb6", 0, 1, 0, 512), R("hb", 0, 1, PN3, PN3 + 512)], tag="gn_ex")
                for qs in range(4):
                    b = 2 + (qs + h) % 2
                    def mo3(e, qs=qs, h=h, nkt=nkt, b=b):
                        last = None
                        for kt in range(nkt):
                            last = e.matmul(B[b][:, 0:65],
                                            hb[:, PT3 + 512 * kt + 128 * qs:PT3 + 512 * kt + 128 * (qs + 1)],
                                            hb[:, GV + 1040 * kt + 65 * h:GV + 1040 * kt + 65 * (h + 1)],
                                            start=(kt == 0), stop=False)
                        return e.matmul(B[b][:, 0:65], hb[0:1, PN3 + 128 * qs:PN3 + 128 * (qs + 1)],
                                        hb[0:1, NV3 + 65 * h:NV3 + 65 * (h + 1)], start=False, stop=True)
                    S.op("tensor", mo3, reads=[HB(PT3, PT3 + 6144), HB(GV, GV + 12480),
                                               R("hb", 0, 1, PN3, PN3 + 512), R("hb", 0, 1, NV3, NV3 + 1040)],
                         writes=[R(f"pb{b}", 0, 128, 0, 512)], tag="go_mm")
                    dnc = 3104 + 16 * qs + h
                    S.op("vector", lambda e, dnc=dnc, b=b: e.tensor_copy(hf[:, dnc:dnc + 1],
                                                                         B[b][:, 64:65]),
                         reads=[R(f"pb{b}", 0, 128, 0, 512)],
                         writes=[R(f"pb{b}", 0, 128, 0, 512), HFr(0, 128, dnc, dnc + 1)], tag="go_dn")
                    S.op("scalar", lambda e, qs=qs, h=h, b=b: e.activation(
                        hb[:, ATT3 + 1024 * qs + 64 * h:ATT3 + 1024 * qs + 64 * (h + 1)], B[b][:, 0:64],
                        AF.Copy),
                        reads=[R(f"pb{b}", 0, 128, 0, 512)],
                        writes=[R(f"pb{b}", 0, 128, 0, 512),
                                HB(ATT3 + 1024 * qs + 64 * h, ATT3 + 1024 * qs + 64 * (h + 1))], tag="go_ev")
            S.op("scalar", lambda e: e.activation(hf[:, 3168:3232], hf[:, 3104:3168], AF.Ln,
                                                  scale=2.0 ** 52),
                 reads=[HFr(0, 128, 3104, 3168)], writes=[HFr(0, 128, 3168, 3232)], tag="gr_ln")
            S.op("scalar", lambda e: e.activation(hf[:, 3168:3232], hf[:, 3168:3232], AF.Exp, scale=-1.0,
                                                  bias=hf[:, 4185:4186]),
                 reads=[HFr(0, 128, 3168, 3232), HFr(0, 128, 4185, 4186)],
                 writes=[HFr(0, 128, 3168, 3232)], tag="gr_ex")
            for qs in range(4):
                for h in range(16):
                    rc = 3168 + 16 * qs + h
                    S.op("vector", lambda e, qs=qs, h=h, rc=rc: e.tensor_scalar(
                        hb[:, ATT3 + 1024 * qs + 64 * h:ATT3 + 1024 * qs + 64 * (h + 1)],
                        hb[:, ATT3 + 1024 * qs + 64 * h:ATT3 + 1024 * qs + 64 * (h + 1)],
                        hf[:, rc:rc + 1], None, op0=OP.mult),
                        reads=[HB(ATT3 + 1024 * qs + 64 * h, ATT3 + 1024 * qs + 64 * (h + 1)),
                               HFr(0, 128, rc, rc + 1)],
                        writes=[HB(ATT3 + 1024 * qs + 64 * h, ATT3 + 1024 * qs + 64 * (h + 1))],
                        tag="gr_sc")
            for qs in range(4):
                for ot in range(8):
                    b = (qs + ot) % 2
                    S.op("tensor", lambda e, qs=qs, ot=ot, b=b: e.matmul(
                        B[b][:, 0:128], hb[:, ATT3 + 1024 * qs + 128 * ot:ATT3 + 1024 * qs + 128 * (ot + 1)],
                        M["idb"][:, :], start=True, stop=True),
                        reads=[HB(ATT3, ATT3 + 4096), CONST["idb"]], writes=[PB(b)], tag="gt_T")
                    S.op("vector", lambda e, qs=qs, ot=ot, b=b: e.tensor_copy(
                        hb[:, PT3 + 512 * ot + 128 * qs:PT3 + 512 * ot + 128 * (qs + 1)], B[b][:, 0:128]),
                        reads=[PB(b)],
                        writes=[PB(b), HB(PT3 + 512 * ot + 128 * qs, PT3 + 512 * ot + 128 * (qs + 1))],
                        tag="gt_e")
            load_ws(WS3, wo_ap[g][:, :, 0:512], "ld_go0")
            load_ws(WS3B, wo_ap[g][:, :, 512:1024], "ld_go1")
            for tt in range(4):
                i_tok = 4 * w + tt
                r = 1024 * (i_tok % 2)
                xr = R("hf", 0, 128, r, r + D)
                for dc in range(2):
                    ws = WS3 if dc == 0 else WS3B
                    def mmo3(e, tt=tt, ws=ws, dc=dc):
                        last = None
                        for jj in range(NDT):
                            last = e.matmul(B[dc][:, 0:512],
                                            hb[:, PT3 + 512 * jj + 128 * tt:PT3 + 512 * jj + 128 * (tt + 1)],
                                            hb[:, ws + 512 * jj:ws + 512 * (jj + 1)],
                                            start=(jj == 0), stop=(jj == NDT - 1))
                        return last
                    S.op("tensor", mmo3, reads=[HB(PT3, PT3 + 6144), HB(WS3, WS3 + 4096),
                                                HB(WS3B, WS3B + 4096)],
                         writes=[PB(dc)], tag="gp_mm")
                S.dma("sync", lambda e, i_tok=i_tok, r=r: e.dma_start(
                    hf[:, r:r + D], xres.ap()[128 * i_tok:128 * (i_tok + 1), :]),
                    reads=[xr, R("xres", i_tok, i_tok + 1, 0, 1)], writes=[xr], tag="gx_ld")
                qwc = 3072 + 2 * (i_tok - 4) + g
                for dc in range(2):
                    S.op("vector", lambda e, r=r, dc=dc, qwc=qwc: e.scalar_tensor_tensor(
                        hf[:, r + 512 * dc:r + 512 * (dc + 1)], B[dc][:, 0:512],
                        hf[:, qwc:qwc + 1], hf[:, r + 512 * dc:r + 512 * (dc + 1)],
                        op0=OP.mult, op1=OP.add),
                        reads=[PB(dc), HFr(0, 128, qwc, qwc + 1),
                               R("hf", 0, 128, r + 512 * dc, r + 512 * (dc + 1))],
                        writes=[PB(dc), R("hf", 0, 128, r + 512 * dc, r + 512 * (dc + 1))], tag="gx_acc")
                S.dma("sync", lambda e, i_tok=i_tok, r=r: e.dma_start(
                    xres.ap()[128 * i_tok:128 * (i_tok + 1), :], hf[:, r:r + D]),
                    reads=[xr], writes=[R("xres", i_tok, i_tok + 1, 0, 1)], tag="gx_st")


def _bv_fold(S, dt, xres, M, PB, CONST):
    hf, rowf = M["hf"], M["rowf"]
    B = M["B"]
    RF = lambda r0, r1, c0, c1: R("rowf", r0, r1, c0, c1)
    S.dma("sync", lambda e: e.dma_start(rowf[0:1, 0:1024], dt["bv_row"].ap()),
          reads=[RF(0, 1, 0, 1024)], writes=[RF(0, 1, 0, 1024)], tag="ld_bv4")
    for half in range(2):
        S.op("tensor", lambda e, half=half: e.matmul(
            B[6][:, 0:512], M["ones_rf"][0:1, 0:128],
            rowf[0:1, 512 * half:512 * (half + 1)], start=True, stop=True),
            reads=[RF(0, 1, 0, 1024), CONST["ones_row_f32"]], writes=[PB(6)], tag="bvmm4")
        S.op("vector", lambda e, half=half: e.tensor_copy(
            hf[:, 2048 + 512 * half:2048 + 512 * (half + 1)], B[6][:, 0:512]),
            reads=[PB(6)], writes=[PB(6), R("hf", 0, 128, 2048 + 512 * half, 2048 + 512 * (half + 1))],
            tag="bvev4")
    for i in range(NT):
        r = 1024 * (i % 2)
        xr = R("hf", 0, 128, r, r + D)
        S.dma("sync", lambda e, i=i, r=r: e.dma_start(hf[:, r:r + D], xres.ap()[128 * i:128 * (i + 1), :]),
              reads=[xr, R("xres", i, i + 1, 0, 1)], writes=[xr], tag="bv_ld")
        S.op("vector", lambda e, r=r: e.tensor_tensor(hf[:, r:r + D], hf[:, r:r + D], hf[:, 2048:3072],
                                                      op=OP.add),
             reads=[xr, R("hf", 0, 128, 2048, 3072)], writes=[xr], tag="bv_add")
        S.dma("sync", lambda e, i=i, r=r: e.dma_start(xres.ap()[128 * i:128 * (i + 1), :], hf[:, r:r + D]),
              reads=[xr], writes=[R("xres", i, i + 1, 0, 1)], tag="bv_st")


def _l1_attn(S, dt, xres, M, PB, CONST):
    xnT, hb, hf, rowf, stats, qa = M["xnT"], M["hb"], M["hf"], M["rowf"], M["stats"], M["qa"]
    B = M["B"]
    HB = lambda c0, c1: R("hb", 0, 128, c0, c1)
    HBr = lambda r0, r1, c0, c1: R("hb", r0, r1, c0, c1)
    HFr = lambda r0, r1, c0, c1: R("hf", r0, r1, c0, c1)
    RF = lambda r0, r1, c0, c1: R("rowf", r0, r1, c0, c1)

    KP4 = 0
    V4 = 8192
    QP4 = 12352
    PT4 = 16448
    AT4 = 24640
    WL4 = 41024
    WKV4 = 45120
    SCR = 47168

    S.dma("sync", lambda e: e.dma_start(hf[0:1, 3300:3316], dt["phase_row"].ap()),
          reads=[HFr(0, 1, 3300, 3316)], writes=[HFr(0, 1, 3300, 3316)], tag="ld_ph")
    S.dma("sync", lambda e: e.dma_start(hf[0:1, 3316:3332], dt["als_row"].ap()),
          reads=[HFr(0, 1, 3316, 3332)], writes=[HFr(0, 1, 3316, 3332)], tag="ld_als")
    S.op("vector", lambda e: e.tensor_scalar(hf[0:1, 3300:3316], hf[0:1, 3300:3316], 1.5, -1.5,
                                             op0=OP.min, op1=OP.max),
         reads=[HFr(0, 1, 3300, 3316)], writes=[HFr(0, 1, 3300, 3316)], tag="phclamp")
    S.op("scalar", lambda e: e.activation(hf[0:1, 3332:3348], hf[0:1, 3300:3316], AF.Sin),
         reads=[HFr(0, 1, 3300, 3316)], writes=[HFr(0, 1, 3332, 3348)], tag="srow")
    S.op("vector", lambda e: e.memset(hf[0:1, 3364:3365], 1.5707963705062866),
         writes=[HFr(0, 1, 3364, 3365)], tag="hpi")
    S.op("vector", lambda e: e.tensor_scalar(hf[0:1, 3364:3365], hf[0:1, 3364:3365],
                                             1.0, -4.371139000186241e-08, op0=OP.mult, op1=OP.add),
         reads=[HFr(0, 1, 3364, 3365)], writes=[HFr(0, 1, 3364, 3365)], tag="hpib")
    S.op("scalar", lambda e: e.activation(hf[0:1, 3348:3364], hf[0:1, 3300:3316], AF.Sin,
                                          bias=hf[0:1, 3364:3365]),
         reads=[HFr(0, 1, 3300, 3316), HFr(0, 1, 3364, 3365)], writes=[HFr(0, 1, 3348, 3364)], tag="crow")
    S.op("scalar", lambda e: e.activation(hf[0:1, 3365:3381], hf[0:1, 3316:3332], AF.Exp),
         reads=[HFr(0, 1, 3316, 3332)], writes=[HFr(0, 1, 3365, 3381)], tag="slrow")
    S.op("vector", lambda e: e.tensor_scalar(hf[0:1, 3381:3397], hf[0:1, 3365:3381], -1.0, None,
                                             op0=OP.mult),
         reads=[HFr(0, 1, 3365, 3381)], writes=[HFr(0, 1, 3381, 3397)], tag="nslrow")
    S.op("vector", lambda e: e.tensor_scalar(hf[0:1, 3397:3413], hf[0:1, 3332:3348], -1.0, None,
                                             op0=OP.mult),
         reads=[HFr(0, 1, 3332, 3348)], writes=[HFr(0, 1, 3397, 3413)], tag="nsrow")
    for h in range(16):
        S.op("tensor", lambda e, h=h: e.matmul(B[6][:, 0:1], M["ones_rf"][0:1, 0:128],
                                               hf[0:1, 3348 + h:3349 + h], start=True, stop=True),
             reads=[HFr(0, 1, 3348, 3364), CONST["ones_row_f32"]], writes=[PB(6)], tag="cAm")
        S.op("vector", lambda e, h=h: e.tensor_copy(hf[:, 3420 + h:3421 + h], B[6][:, 0:1]),
             reads=[PB(6)], writes=[PB(6), HFr(0, 128, 3420 + h, 3421 + h)], tag="cAe")
        def cBm(e, h=h):
            e.matmul(B[6][0:32, 0:1], M["ones_rf"][0:1, 0:32], hf[0:1, 3397 + h:3398 + h],
                     start=True, stop=True)
            return e.matmul(B[6][32:64, 0:1], M["ones_rf"][0:1, 0:32], hf[0:1, 3332 + h:3333 + h],
                            start=True, stop=True)
        S.op("tensor", cBm, reads=[HFr(0, 1, 3332, 3348), HFr(0, 1, 3397, 3413), CONST["ones_row_f32"]],
             writes=[PB(6)], tag="cBm")
        S.op("vector", lambda e, h=h: e.tensor_copy(hf[0:64, 3440 + h:3441 + h], B[6][0:64, 0:1]),
             reads=[PB(6)], writes=[PB(6), HFr(0, 64, 3440 + h, 3441 + h)], tag="cBe")

    S.op("vector", lambda e: e.tensor_scalar(hf[:, 1160:1672], M["cm"][:, 0:512], 1e30, -1e30,
                                             op0=OP.mult, op1=OP.add),
         reads=[CONST["cmask"]], writes=[HFr(0, 128, 1160, 1672)], tag="amask")

    S.dma("gpsimd", lambda e: e.dma_start(hb[:, WL4:WL4 + 4096], dt["l1wq"].ap()),
          reads=[HB(WL4, WL4 + 4096)], writes=[HB(WL4, WL4 + 4096)], tag="ld_l1wq")
    S.dma("gpsimd", lambda e: e.dma_start(hb[:, WKV4:WKV4 + 1024], dt["l1wk"].ap()),
          reads=[HB(WKV4, WKV4 + 1024)], writes=[HB(WKV4, WKV4 + 1024)], tag="ld_l1wk")
    S.dma("gpsimd", lambda e: e.dma_start(hb[:, WKV4 + 1024:WKV4 + 2048], dt["l1wv"].ap()),
          reads=[HB(WKV4 + 1024, WKV4 + 2048)], writes=[HB(WKV4 + 1024, WKV4 + 2048)], tag="ld_l1wv")

    for t in range(NT):
        b = t % 2
        def mmv4(e, t=t, b=b):
            last = None
            for j in range(4):
                last = e.matmul(B[b][:, 0:256], xnT[:, N * j + 128 * t:N * j + 128 * (t + 1)],
                                hb[:, WKV4 + 1024 + 256 * j:WKV4 + 1024 + 256 * (j + 1)],
                                start=(j == 0), stop=(j == 3))
            return last
        S.op("tensor", mmv4, reads=[HB(WKV4 + 1024, WKV4 + 2048), R("xnT", 0, 128, 0, NDT * N)],
             writes=[PB(b)], tag="v4_mm")
        def evv4(e, t=t, b=b):
            return e.tensor_copy(
                hb[:, V4 + 260 * t:V4 + 260 * (t + 1)].rearrange("p (h c) -> p h c", h=4)[:, :, 0:64],
                B[b][:, 0:256].rearrange("p (h c) -> p h c", h=4))
        S.op("vector", evv4, reads=[PB(b)], writes=[PB(b), HB(V4 + 260 * t, V4 + 260 * (t + 1))],
             tag="v4_ev")
        S.op("vector", lambda e, t=t: e.memset(
            hb[:, V4 + 260 * t:V4 + 260 * (t + 1)].rearrange("p (h c) -> p h c", h=4)[:, :, 64:65], 1.0),
            reads=[HB(V4 + 260 * t, V4 + 260 * (t + 1))],
            writes=[HB(V4 + 260 * t, V4 + 260 * (t + 1))], tag="v4_on")

    S.op("vector", lambda e: e.memset(hb[0:1, SCR:SCR + 2048], 1.0),
         writes=[HBr(0, 1, SCR, SCR + 2048)], tag="onerow")
    for kvh in range(4):
        kp = KP4 + 2048 * kvh
        for c in range(4):
            b = c % 2
            def mmk4(e, kvh=kvh, c=c, b=b):
                last = None
                for j in range(4):
                    last = e.matmul(B[b][0:64, 0:512],
                                    hb[:, WKV4 + 256 * j + 64 * kvh:WKV4 + 256 * j + 64 * (kvh + 1)],
                                    xnT[:, N * (4 + j) + 512 * c:N * (4 + j) + 512 * (c + 1)],
                                    start=(j == 0), stop=(j == 3))
                return last
            S.op("tensor", mmk4, reads=[HB(WKV4, WKV4 + 1024), R("xnT", 0, 128, 0, NDT * N)],
                 writes=[PB(b)], tag="k4_mm")
            S.op("scalar", lambda e, kp=kp, c=c, b=b: e.activation(
                hb[0:64, kp + 512 * c:kp + 512 * (c + 1)], B[b][0:64, 0:512], AF.Copy),
                reads=[PB(b)], writes=[PB(b), HBr(0, 64, kp + 512 * c, kp + 512 * (c + 1))], tag="k4_ev")
        for rr in range(3):
            S.dma("sync", lambda e, kp=kp, rr=rr: e.dma_start(hb[64 + rr:65 + rr, kp:kp + 2048],
                                                              hb[0:1, SCR:SCR + 2048]),
                  reads=[HBr(0, 1, SCR, SCR + 2048), HBr(64 + rr, 65 + rr, kp, kp + 2048)],
                  writes=[HBr(64 + rr, 65 + rr, kp, kp + 2048)], tag="k4_on")
    for rr in range(3):
        S.dma("sync", lambda e, rr=rr: e.dma_start(hb[67 + rr:68 + rr, QP4:QP4 + 2048],
                                                   hb[0:1, SCR:SCR + 2048]),
              reads=[HBr(0, 1, SCR, SCR + 2048), HBr(67 + rr, 68 + rr, QP4, QP4 + 2048)],
              writes=[HBr(67 + rr, 68 + rr, QP4, QP4 + 2048)], tag="q4_on")

    def split3(vrect_fn, vcols, dstc, width, tag):
        vr = HFr(0, 1, vcols, vcols + width)
        t1 = HFr(0, 1, 2184, 2184 + width)
        for k in range(3):
            dr = HBr(0, 1, dstc + width * k, dstc + width * (k + 1))
            srcr = vr if k == 0 else t1
            src = (lambda e: hf[0:1, vcols:vcols + width]) if k == 0 else \
                  (lambda e: hf[0:1, 2184:2184 + width])
            S.op("vector", lambda e, k=k, src=src: e.tensor_copy(
                hb[0:1, dstc + width * k:dstc + width * (k + 1)], src(e)),
                reads=[srcr], writes=[dr], tag=tag + "h")
            if k < 2:
                S.op("vector", lambda e, k=k, src=src: e.tensor_tensor(
                    hf[0:1, 2184:2184 + width], src(e),
                    hb[0:1, dstc + width * k:dstc + width * (k + 1)], op=OP.subtract),
                    reads=[srcr, dr], writes=[t1], tag=tag + "l")

    nqc = 4
    for h in range(16):
        kvh = h // 4
        kp = KP4 + 2048 * kvh
        for ch in range(4):
            S.op("vector", lambda e, h=h, ch=ch: e.tensor_scalar(
                hf[0:1, 0:512], M["pos"][0:1, 512 * ch:512 * (ch + 1)],
                hf[0:1, 3381 + h:3382 + h], None, op0=OP.mult),
                reads=[CONST["pos_row"], HFr(0, 1, 3381, 3397)], writes=[HFr(0, 1, 0, 512)], tag="ckv")
            split3(None, 0, SCR, 512, "ck")
            S.dma("sync", lambda e, kp=kp, ch=ch: e.dma_start(
                hb[67:70, kp + 512 * ch:kp + 512 * (ch + 1)], hb[0:1, SCR:SCR + 1536]),
                reads=[HBr(0, 1, SCR, SCR + 1536), HBr(67, 70, kp + 512 * ch, kp + 512 * (ch + 1))],
                writes=[HBr(67, 70, kp + 512 * ch, kp + 512 * (ch + 1))], tag="ck_dma")
        for c in range(4):
            b = c % 2
            def mmq4(e, h=h, c=c, b=b):
                last = None
                for j in range(4):
                    last = e.matmul(B[b][0:64, 0:512],
                                    hb[:, WL4 + 1024 * j + 64 * h:WL4 + 1024 * j + 64 * (h + 1)],
                                    xnT[:, N * (4 + j) + 512 * c:N * (4 + j) + 512 * (c + 1)],
                                    start=(j == 0), stop=(j == 3))
                return last
            S.op("tensor", mmq4, reads=[HB(WL4, WL4 + 4096), R("xnT", 0, 128, 0, NDT * N)],
                 writes=[PB(b)], tag="q4_mm")
            S.op("scalar", lambda e, c=c, b=b: e.activation(
                hb[0:64, QP4 + 512 * c:QP4 + 512 * (c + 1)], B[b][0:64, 0:512], AF.Copy, scale=0.125),
                reads=[PB(b)], writes=[PB(b), HBr(0, 64, QP4 + 512 * c, QP4 + 512 * (c + 1))],
                tag="q4_ev")
        for sl in range(2):
            r0, pr = 32 * sl, 32 * (1 - sl)
            S.dma("sync", lambda e, r0=r0, pr=pr: e.dma_start(
                hb[pr:pr + 32, PT4:PT4 + 2048], hb[r0:r0 + 32, QP4:QP4 + 2048]),
                reads=[HBr(0, 64, QP4, QP4 + 2048), HBr(pr, pr + 32, PT4, PT4 + 2048)],
                writes=[HBr(pr, pr + 32, PT4, PT4 + 2048)], tag="q4_sw")
        S.op("vector", lambda e, h=h: e.tensor_scalar(
            hb[0:64, QP4:QP4 + 2048], hb[0:64, QP4:QP4 + 2048], hf[0:64, 3420 + h:3421 + h], None,
            op0=OP.mult),
            reads=[HBr(0, 64, QP4, QP4 + 2048), HFr(0, 64, 3420 + h, 3421 + h)],
            writes=[HBr(0, 64, QP4, QP4 + 2048)], tag="q4_rA")
        S.op("vector", lambda e, h=h: e.scalar_tensor_tensor(
            hb[0:64, QP4:QP4 + 2048], hb[0:64, PT4:PT4 + 2048], hf[0:64, 3440 + h:3441 + h],
            hb[0:64, QP4:QP4 + 2048], op0=OP.mult, op1=OP.add),
            reads=[HBr(0, 64, PT4, PT4 + 2048), HFr(0, 64, 3440 + h, 3441 + h),
                   HBr(0, 64, QP4, QP4 + 2048)],
            writes=[HBr(0, 64, QP4, QP4 + 2048)], tag="q4_rB")

        for qc in range(nqc):
            S.op("vector", lambda e, h=h, qc=qc: e.tensor_scalar(
                hf[0:1, 0:512], M["pos"][0:1, 512 * qc:512 * (qc + 1)],
                hf[0:1, 3365 + h:3366 + h], None, op0=OP.mult),
                reads=[CONST["pos_row"], HFr(0, 1, 3365, 3381)], writes=[HFr(0, 1, 0, 512)], tag="bqa")
            split3(None, 0, SCR, 512, "bqs")
            S.dma("sync", lambda e, qc=qc: e.dma_start(
                hb[64:67, QP4 + 512 * qc:QP4 + 512 * (qc + 1)], hb[0:1, SCR:SCR + 1536]),
                reads=[HBr(0, 1, SCR, SCR + 1536), HBr(64, 67, QP4 + 512 * qc, QP4 + 512 * (qc + 1))],
                writes=[HBr(64, 67, QP4 + 512 * qc, QP4 + 512 * (qc + 1))], tag="bq_dma")
            S.op("tensor", lambda e, kp=kp, qc=qc: e.matmul(
                B[4][:, 0:512], hb[0:70, kp:kp + 128], hb[0:70, QP4 + 512 * qc:QP4 + 512 * (qc + 1)],
                start=True, stop=True),
                reads=[HBr(0, 70, kp, kp + 2048), HBr(0, 70, QP4, QP4 + 2048)], writes=[PB(4)],
                tag="pA_mm")
            S.op("vector", lambda e: e.tensor_copy(hf[:, 1672:2184], B[4][:, 0:512]),
                 reads=[PB(4)], writes=[PB(4), HFr(0, 128, 1672, 2184)], tag="pA_ev")
            if qc == 0:
                S.op("vector", lambda e: e.tensor_tensor(hf[:, 1672:2184], hf[:, 1672:2184],
                                                         hf[:, 1160:1672], op=OP.add),
                     reads=[HFr(0, 128, 1672, 2184), HFr(0, 128, 1160, 1672)],
                     writes=[HFr(0, 128, 1672, 2184)], tag="pA_msk")
            for sub in range(4):
                S.op("tensor", lambda e, sub=sub: e.matmul(
                    B[3][:, 0:128], hf[:, 1672 + 128 * sub:1672 + 128 * (sub + 1)], M["idf"][:, :],
                    start=True, stop=True),
                    reads=[HFr(0, 128, 1672, 2184), CONST["idf"]], writes=[PB(3)], tag="mT")
                S.op("vector", lambda e, sub=sub: e.reduce_max(hf[:, 516 + sub:517 + sub],
                                                               B[3][:, 0:128], axis=AX.X),
                     reads=[PB(3)], writes=[PB(3), HFr(0, 128, 516 + sub, 517 + sub)], tag="mR")
            S.op("tensor", lambda e: e.matmul(B[3][0:4, 0:128], hf[:, 516:520], M["idf"][:, :],
                                              start=True, stop=True),
                 reads=[HFr(0, 128, 516, 520), CONST["idf"]], writes=[PB(3)], tag="mT2")
            S.op("vector", lambda e: e.tensor_copy(hf[0:4, 520:648], B[3][0:4, 0:128]),
                 reads=[PB(3)], writes=[PB(3), HFr(0, 4, 520, 648)], tag="mT2e")
            S.dma("sync", lambda e: e.dma_start(hf[0:1, 648:1160], hf[0:4, 520:648]),
                  reads=[HFr(0, 4, 520, 648), HFr(0, 1, 648, 1160)], writes=[HFr(0, 1, 648, 1160)],
                  tag="m_sc")
            S.op("vector", lambda e: e.tensor_tensor(hf[0:1, 0:512], hf[0:1, 0:512],
                                                     hf[0:1, 648:1160], op=OP.subtract),
                 reads=[HFr(0, 1, 0, 512), HFr(0, 1, 648, 1160)], writes=[HFr(0, 1, 0, 512)], tag="bqm")
            split3(None, 0, SCR, 512, "bqf")
            S.dma("sync", lambda e, qc=qc: e.dma_start(
                hb[64:67, QP4 + 512 * qc:QP4 + 512 * (qc + 1)], hb[0:1, SCR:SCR + 1536]),
                reads=[HBr(0, 1, SCR, SCR + 1536), HBr(64, 67, QP4 + 512 * qc, QP4 + 512 * (qc + 1))],
                writes=[HBr(64, 67, QP4 + 512 * qc, QP4 + 512 * (qc + 1))], tag="bq_dma2")
            for kt in range(4 * qc + 4):
                b = 4 + (kt % 2)
                S.op("tensor", lambda e, kp=kp, kt=kt, qc=qc, b=b: e.matmul(
                    B[b][:, 0:512], hb[0:70, kp + 128 * kt:kp + 128 * (kt + 1)],
                    hb[0:70, QP4 + 512 * qc:QP4 + 512 * (qc + 1)], start=True, stop=True),
                    reads=[HBr(0, 70, kp, kp + 2048), HBr(0, 70, QP4, QP4 + 2048)], writes=[PB(b)],
                    tag="s4_mm")
                S.op("scalar", lambda e, kt=kt, b=b: e.activation(
                    hb[:, PT4 + 512 * kt:PT4 + 512 * (kt + 1)], B[b][:, 0:512], AF.Exp),
                    reads=[PB(b)], writes=[PB(b), HB(PT4 + 512 * kt, PT4 + 512 * (kt + 1))], tag="s4_ex")
                if kt >= 4 * qc:
                    S.op("vector", lambda e, kt=kt, qc=qc: e.tensor_tensor(
                        hb[:, PT4 + 512 * kt:PT4 + 512 * (kt + 1)],
                        hb[:, PT4 + 512 * kt:PT4 + 512 * (kt + 1)],
                        M["cm"][:, 512 * (kt - 4 * qc):512 * (kt - 4 * qc + 1)], op=OP.mult),
                        reads=[HB(PT4 + 512 * kt, PT4 + 512 * (kt + 1)), CONST["cmask"]],
                        writes=[HB(PT4 + 512 * kt, PT4 + 512 * (kt + 1))], tag="s4_mk")
            obanks = (2, 3, 6, 0)
            for qs in range(4):
                b = obanks[qs]
                def mo4(e, qs=qs, qc=qc, kvh=kvh, b=b):
                    last = None
                    for kt in range(4 * qc + 4):
                        last = e.matmul(B[b][:, 0:65],
                                        hb[:, PT4 + 512 * kt + 128 * qs:PT4 + 512 * kt + 128 * (qs + 1)],
                                        hb[:, V4 + 260 * kt + 65 * kvh:V4 + 260 * kt + 65 * (kvh + 1)],
                                        start=(kt == 0), stop=(kt == 4 * qc + 3))
                    return last
                S.op("tensor", mo4, reads=[HB(PT4, PT4 + 8192), HB(V4, V4 + 4160)],
                     writes=[PB(b)], tag="o4_mm")
                S.op("vector", lambda e, qs=qs, b=b: e.tensor_scalar(
                    hf[:, 2696 + qs:2697 + qs], B[b][:, 64:65], 1.0, 1.0, op0=OP.mult, op1=OP.add),
                    reads=[PB(b)], writes=[PB(b), HFr(0, 128, 2696 + qs, 2697 + qs)], tag="o4_dn")
            S.op("scalar", lambda e: e.activation(hf[:, 2700:2704], hf[:, 2696:2700], AF.Ln,
                                                  scale=2.0 ** 52),
                 reads=[HFr(0, 128, 2696, 2700)], writes=[HFr(0, 128, 2700, 2704)], tag="r4_ln")
            S.op("scalar", lambda e: e.activation(hf[:, 2700:2704], hf[:, 2700:2704], AF.Exp, scale=-1.0,
                                                  bias=hf[:, 4185:4186]),
                 reads=[HFr(0, 128, 2700, 2704), HFr(0, 128, 4185, 4186)],
                 writes=[HFr(0, 128, 2700, 2704)], tag="r4_ex")
            for qs in range(4):
                b = obanks[qs]
                t = 4 * qc + qs
                S.op("vector", lambda e, qs=qs, t=t, h=h, b=b: e.tensor_scalar(
                    hb[:, AT4 + 1024 * t + 64 * h:AT4 + 1024 * t + 64 * (h + 1)], B[b][:, 0:64],
                    hf[:, 2700 + qs:2701 + qs], None, op0=OP.mult),
                    reads=[PB(b), HFr(0, 128, 2700 + qs, 2701 + qs)],
                    writes=[PB(b), HB(AT4 + 1024 * t + 64 * h, AT4 + 1024 * t + 64 * (h + 1))],
                    tag="o4_ev")

    S.dma("gpsimd", lambda e: e.dma_start(
        hb[:, WL4:WL4 + 4096].rearrange("p (j o) -> p j o", j=8),
        dt["l1wo"].ap().rearrange("p (j o) -> p j o", j=8)[:, :, 0:512]),
        reads=[HB(WL4, WL4 + 4096)], writes=[HB(WL4, WL4 + 4096)], tag="ld_l1wo0")
    S.dma("gpsimd", lambda e: e.dma_start(
        hb[:, KP4:KP4 + 4096].rearrange("p (j o) -> p j o", j=8),
        dt["l1wo"].ap().rearrange("p (j o) -> p j o", j=8)[:, :, 512:1024]),
        reads=[HB(KP4, KP4 + 4096)], writes=[HB(KP4, KP4 + 4096)], tag="ld_l1wo1")
    for t in range(NT):
        for oj in range(8):
            b = 4 + (oj % 2)
            S.op("tensor", lambda e, t=t, oj=oj, b=b: e.matmul(
                B[b][:, 0:128], hb[:, AT4 + 1024 * t + 128 * oj:AT4 + 1024 * t + 128 * (oj + 1)],
                M["idb"][:, :], start=True, stop=True),
                reads=[HB(AT4 + 1024 * t, AT4 + 1024 * (t + 1)), CONST["idb"]], writes=[PB(b)],
                tag="t4_T")
            S.op("vector", lambda e, oj=oj, b=b: e.tensor_copy(
                hb[:, PT4 + 128 * oj:PT4 + 128 * (oj + 1)], B[b][:, 0:128]),
                reads=[PB(b)], writes=[PB(b), HB(PT4 + 128 * oj, PT4 + 128 * (oj + 1))], tag="t4_e")
        r = 1024 * (t % 2)
        xr = R("hf", 0, 128, r, r + D)
        for dc in range(2):
            ws = WL4 if dc == 0 else KP4
            def mmo4(e, ws=ws, dc=dc):
                last = None
                for jj in range(8):
                    last = e.matmul(B[dc][:, 0:512], hb[:, PT4 + 128 * jj:PT4 + 128 * (jj + 1)],
                                    hb[:, ws + 512 * jj:ws + 512 * (jj + 1)],
                                    start=(jj == 0), stop=(jj == 7))
                return last
            S.op("tensor", mmo4, reads=[HB(PT4, PT4 + 1024), HB(WL4, WL4 + 4096), HB(KP4, KP4 + 4096)],
                 writes=[PB(dc)], tag="p4_mm")
        S.dma("sync", lambda e, t=t, r=r: e.dma_start(hf[:, r:r + D], xres.ap()[128 * t:128 * (t + 1), :]),
              reads=[xr, R("xres", t, t + 1, 0, 1)], writes=[xr], tag="x4_ld")
        for dc in range(2):
            S.op("vector", lambda e, r=r, dc=dc: e.tensor_tensor(
                hf[:, r + 512 * dc:r + 512 * (dc + 1)], B[dc][:, 0:512],
                hf[:, r + 512 * dc:r + 512 * (dc + 1)], op=OP.add),
                reads=[PB(dc), R("hf", 0, 128, r + 512 * dc, r + 512 * (dc + 1))],
                writes=[PB(dc), R("hf", 0, 128, r + 512 * dc, r + 512 * (dc + 1))], tag="x4_acc")
        S.dma("sync", lambda e, t=t, r=r: e.dma_start(xres.ap()[128 * t:128 * (t + 1), :], hf[:, r:r + D]),
              reads=[xr], writes=[R("xres", t, t + 1, 0, 1)], tag="x4_st")


def _l1_ffn_pre(S, dt, xres, M, PB, CONST):
    xnT, hb, hf, rowf, stats, qa = M["xnT"], M["hb"], M["hf"], M["rowf"], M["stats"], M["qa"]
    B = M["B"]
    HB = lambda c0, c1: R("hb", 0, 128, c0, c1)
    HBr = lambda r0, r1, c0, c1: R("hb", r0, r1, c0, c1)
    HFr = lambda r0, r1, c0, c1: R("hf", r0, r1, c0, c1)
    RF = lambda r0, r1, c0, c1: R("rowf", r0, r1, c0, c1)
    STT = lambda c0, c1: R("stats", 0, 128, c0, c1)

    XC5 = 0
    SO5 = 14400
    ET5 = 16448
    DTM = 20544
    SIT = 24640
    F5MU = 26688
    SOT = 28736
    CMS = 30784
    CMT = 31040
    CF5 = 35136
    W5A = 37184
    W5B = 39232
    W5C = 41280
    HT5 = 42304
    G2B = 42336
    BETB = 42352

    for i in range(NT):
        S.op("vector", lambda e, i=i: e.tensor_tensor(stats[:, 48 + i:49 + i], stats[:, 32 + i:33 + i],
                                                      stats[:, 80 + i:81 + i], op=OP.mult),
             reads=[STT(32, 48), STT(80, 96)], writes=[STT(48 + i, 49 + i)], tag="f_isc")
    for i in range(NT):
        r = 1024 * (i % 2)
        xr = R("hf", 0, 128, r, r + D)
        S.dma("sync", lambda e, i=i, r=r: e.dma_start(hf[:, r:r + D], xres.ap()[128 * i:128 * (i + 1), :]),
              reads=[xr, R("xres", i, i + 1, 0, 1)], writes=[xr], tag="f_xld")
        S.op("vector", lambda e, i=i, r=r: e.tensor_scalar(
            hb[:, XC5 + 1024 * i:XC5 + 1024 * (i + 1)], hf[:, r:r + D],
            stats[:, i:i + 1], stats[:, 48 + i:49 + i], op0=OP.subtract, op1=OP.mult),
            reads=[xr, STT(0, 16), STT(48, 64)],
            writes=[HB(XC5 + 1024 * i, XC5 + 1024 * (i + 1))], tag="f_xc")

    S.dma("sync", lambda e: e.dma_start(hf[:, 4128:4136], dt["f_gamc"].ap()),
          reads=[HFr(0, 128, 4128, 4136)], writes=[HFr(0, 128, 4128, 4136)], tag="f_ldg")
    S.dma("sync", lambda e: e.dma_start(hf[:, 4136:4144], dt["f_betc"].ap()),
          reads=[HFr(0, 128, 4136, 4144)], writes=[HFr(0, 128, 4136, 4144)], tag="f_ldb")
    S.dma("sync", lambda e: e.dma_start(hf[:, 4096:4128], dt["f_b1c"].ap()),
          reads=[HFr(0, 128, 4096, 4128)], writes=[HFr(0, 128, 4096, 4128)], tag="f_ldb1")
    S.dma("sync", lambda e: e.dma_start(hf[0:1, 2700:2701], dt["f_scale"].ap()),
          reads=[HFr(0, 1, 2700, 2701)], writes=[HFr(0, 1, 2700, 2701)], tag="f_ldsc")
    S.op("vector", lambda e: e.tensor_tensor(hb[:, G2B:G2B + 8], hf[:, 4128:4136], hf[:, 4128:4136],
                                             op=OP.mult),
         reads=[HFr(0, 128, 4128, 4136)], writes=[HB(G2B, G2B + 8)], tag="f_g2")
    S.op("vector", lambda e: e.tensor_tensor(hb[:, G2B + 8:G2B + 16], hf[:, 4128:4136], hf[:, 4136:4144],
                                             op=OP.mult),
         reads=[HFr(0, 128, 4128, 4144)], writes=[HB(G2B + 8, G2B + 16)], tag="f_gb")
    S.op("vector", lambda e: e.tensor_copy(hb[:, BETB:BETB + 8], hf[:, 4136:4144]),
         reads=[HFr(0, 128, 4136, 4144)], writes=[HB(BETB, BETB + 8)], tag="f_bb")

    for ch in range(4):
        for j in range(NDT):
            S.op("scalar", lambda e, ch=ch, j=j: e.activation(
                qa[:, 0:512], xnT[:, N * j + 512 * ch:N * j + 512 * (ch + 1)], AF.Square),
                reads=[R("xnT", 0, 128, 0, NDT * N)], writes=[R("qa", 0, 128, 0, 512)], tag="f_sq")
            S.op("tensor", lambda e, j=j: e.matmul(B[3][0:1, 0:512], hb[:, G2B + j:G2B + j + 1],
                                                   qa[:, 0:512], start=(j == 0), stop=(j == NDT - 1)),
                 reads=[R("qa", 0, 128, 0, 512), HB(G2B, G2B + 8)],
                 writes=[R("pb3", 0, 1, 0, 512)], tag="f_s2mm")
        S.op("vector", lambda e, ch=ch: e.tensor_copy(hf[0:1, 512 * ch:512 * (ch + 1)], B[3][0:1, 0:512]),
             reads=[R("pb3", 0, 1, 0, 512)],
             writes=[R("pb3", 0, 1, 0, 512), HFr(0, 1, 512 * ch, 512 * (ch + 1))], tag="f_s2e")
        def s1mm(e, ch=ch):
            last = None
            for j in range(NDT):
                last = e.matmul(B[6][0:1, 0:512], hb[:, G2B + 8 + j:G2B + 9 + j],
                                xnT[:, N * j + 512 * ch:N * j + 512 * (ch + 1)],
                                start=(j == 0), stop=(j == NDT - 1))
            return last
        S.op("tensor", s1mm, reads=[R("xnT", 0, 128, 0, NDT * N), HB(G2B + 8, G2B + 16)],
             writes=[R("pb6", 0, 1, 0, 512)], tag="f_s1mm")
        S.op("vector", lambda e, ch=ch: e.tensor_copy(hf[0:1, 2048 + 512 * ch:2048 + 512 * (ch + 1)],
                                                      B[6][0:1, 0:512]),
             reads=[R("pb6", 0, 1, 0, 512)],
             writes=[R("pb6", 0, 1, 0, 512), HFr(0, 1, 2048 + 512 * ch, 2048 + 512 * (ch + 1))],
             tag="f_s1e")
    S.op("vector", lambda e: e.tensor_tensor(qa[:, 0:8], hf[:, 4136:4144], hf[:, 4136:4144], op=OP.mult),
         reads=[HFr(0, 128, 4136, 4144)], writes=[R("qa", 0, 128, 0, 8)], tag="f_b2sq")
    S.op("tensor", lambda e: e.matmul(B[3][0:1, 0:8], M["ones_col"][:, 0:1], qa[:, 0:8],
                                      start=True, stop=True),
         reads=[R("qa", 0, 128, 0, 8), CONST["ones_col_bf"]], writes=[R("pb3", 0, 1, 0, 512)],
         tag="f_b2mm")
    S.op("vector", lambda e: e.reduce_sum(rowf[0:1, 2200:2201], B[3][0:1, 0:8], axis=AX.X),
         reads=[R("pb3", 0, 1, 0, 512)], writes=[R("pb3", 0, 1, 0, 512), RF(0, 1, 2200, 2201)],
         tag="f_b2s")
    S.op("tensor", lambda e: e.transpose(B[6][0:16, 0:128], stats[:, 80:96], M["idf"][:, :]),
         reads=[STT(80, 96), CONST["idf"]], writes=[R("pb6", 0, 128, 0, 128)], tag="f_ctT")
    S.op("vector", lambda e: e.tensor_copy(hf[0:16, 3584:3712], B[6][0:16, 0:128]),
         reads=[R("pb6", 0, 128, 0, 128)],
         writes=[R("pb6", 0, 128, 0, 128), HFr(0, 16, 3584, 3712)], tag="f_ctTe")
    S.dma("sync", lambda e: e.dma_start(rowf[0:1, 0:2048], hf[0:16, 3584:3712]),
          reads=[HFr(0, 16, 3584, 3712), RF(0, 1, 0, 2048)], writes=[RF(0, 1, 0, 2048)], tag="f_ctsc")
    S.op("vector", lambda e: e.tensor_tensor(hf[0:1, 0:2048], hf[0:1, 0:2048], rowf[0:1, 0:2048],
                                             op=OP.mult),
         reads=[HFr(0, 1, 0, 2048), RF(0, 1, 0, 2048)], writes=[HFr(0, 1, 0, 2048)], tag="f_n0")
    S.op("vector", lambda e: e.scalar_tensor_tensor(hf[0:1, 0:2048], hf[0:1, 2048:4096], 2.0,
                                                    hf[0:1, 0:2048], op0=OP.mult, op1=OP.add),
         reads=[HFr(0, 1, 0, 4096)], writes=[HFr(0, 1, 0, 2048)], tag="f_n1")
    S.op("vector", lambda e: e.tensor_tensor(hf[0:1, 0:2048], hf[0:1, 0:2048], rowf[0:1, 0:2048],
                                             op=OP.mult),
         reads=[HFr(0, 1, 0, 2048), RF(0, 1, 0, 2048)], writes=[HFr(0, 1, 0, 2048)], tag="f_n2")
    S.op("vector", lambda e: e.tensor_scalar(hf[0:1, 0:2048], hf[0:1, 0:2048], rowf[0:1, 2200:2201],
                                             None, op0=OP.add),
         reads=[HFr(0, 1, 0, 2048), RF(0, 1, 2200, 2201)], writes=[HFr(0, 1, 0, 2048)], tag="f_n3")
    S.op("scalar", lambda e: e.activation(hf[0:1, 0:2048], hf[0:1, 0:2048], AF.Sqrt),
         reads=[HFr(0, 1, 0, 2048)], writes=[HFr(0, 1, 0, 2048)], tag="f_n4")
    S.op("vector", lambda e: e.tensor_scalar(hf[0:1, 0:2048], hf[0:1, 0:2048], 1e-12, None, op0=OP.max),
         reads=[HFr(0, 1, 0, 2048)], writes=[HFr(0, 1, 0, 2048)], tag="f_n5")
    S.op("vector", lambda e: e.reciprocal(hf[0:1, 0:2048], hf[0:1, 0:2048]),
         reads=[HFr(0, 1, 0, 2048)], writes=[HFr(0, 1, 0, 2048)], tag="f_n6")
    S.op("vector", lambda e: e.tensor_scalar(hf[0:1, 0:2048], hf[0:1, 0:2048], hf[0:1, 2700:2701],
                                             None, op0=OP.mult),
         reads=[HFr(0, 1, 0, 2048), HFr(0, 1, 2700, 2701)], writes=[HFr(0, 1, 0, 2048)], tag="f_n8")
    for ch in range(4):
        S.op("tensor", lambda e, ch=ch: e.matmul(B[6][:, 0:512], M["ones_rf"][0:1, 0:128],
                                                 hf[0:1, 512 * ch:512 * (ch + 1)], start=True, stop=True),
             reads=[HFr(0, 1, 0, 2048), CONST["ones_row_f32"]], writes=[PB(6)], tag="f_cfm")
        S.op("scalar", lambda e, ch=ch: e.activation(hb[:, CF5 + 512 * ch:CF5 + 512 * (ch + 1)],
                                                     B[6][:, 0:512], AF.Copy),
             reads=[PB(6)], writes=[PB(6), HB(CF5 + 512 * ch, CF5 + 512 * (ch + 1))], tag="f_cfe")
        S.op("tensor", lambda e, ch=ch: e.matmul(B[6][:, 0:512], M["ones_rf"][0:1, 0:128],
                                                 rowf[0:1, 512 * ch:512 * (ch + 1)], start=True, stop=True),
             reads=[RF(0, 1, 0, 2048), CONST["ones_row_f32"]], writes=[PB(6)], tag="f_ctm")
        S.op("scalar", lambda e, ch=ch: e.activation(hb[:, W5A + 512 * ch:W5A + 512 * (ch + 1)],
                                                     B[6][:, 0:512], AF.Copy),
             reads=[PB(6)], writes=[PB(6), HB(W5A + 512 * ch, W5A + 512 * (ch + 1))], tag="f_cte")

    S.dma("gpsimd", lambda e: e.dma_start(hb[:, F5MU:F5MU + 2048], dt["f_mu"].ap()),
          reads=[HB(F5MU, F5MU + 2048)], writes=[HB(F5MU, F5MU + 2048)], tag="f_ldmu")
    for et in range(2):
        def mubm(e, et=et):
            last = None
            for j in range(NDT):
                last = e.matmul(B[3][:, 0:1], hb[:, F5MU + 256 * j + 128 * et:F5MU + 256 * j + 128 * (et + 1)],
                                hb[:, BETB + j:BETB + j + 1], start=(j == 0), stop=(j == NDT - 1))
            return last
        S.op("tensor", mubm, reads=[HB(F5MU, F5MU + 2048), HB(BETB, BETB + 8)],
             writes=[R("pb3", 0, 128, 0, 512)], tag="f_mbm")
        S.op("vector", lambda e, et=et: e.tensor_copy(hf[:, 4160 + et:4161 + et], B[3][:, 0:1]),
             reads=[R("pb3", 0, 128, 0, 512)],
             writes=[R("pb3", 0, 128, 0, 512), HFr(0, 128, 4160 + et, 4161 + et)], tag="f_mbe")
    for j in range(NDT):
        S.op("scalar", lambda e, j=j: e.activation(qa[:, 0:256], hb[:, F5MU + 256 * j:F5MU + 256 * (j + 1)],
                                                   AF.Square),
             reads=[HB(F5MU, F5MU + 2048)], writes=[R("qa", 0, 128, 0, 256)], tag="f_musq")
        S.op("tensor", lambda e, j=j: e.matmul(B[6][0:1, 0:256], M["ones_col"][:, 0:1], qa[:, 0:256],
                                               start=(j == 0), stop=(j == NDT - 1)),
             reads=[R("qa", 0, 128, 0, 256), CONST["ones_col_bf"]], writes=[R("pb6", 0, 1, 0, 512)],
             tag="f_mum")
    S.op("vector", lambda e: e.tensor_copy(rowf[0:1, 2204:2460], B[6][0:1, 0:256]),
         reads=[R("pb6", 0, 1, 0, 512)], writes=[R("pb6", 0, 1, 0, 512), RF(0, 1, 2204, 2460)],
         tag="f_mue")
    S.op("scalar", lambda e: e.activation(rowf[0:1, 2204:2460], rowf[0:1, 2204:2460], AF.Sqrt),
         reads=[RF(0, 1, 2204, 2460)], writes=[RF(0, 1, 2204, 2460)], tag="f_mus")
    S.op("vector", lambda e: e.tensor_scalar(rowf[0:1, 2204:2460], rowf[0:1, 2204:2460], 1e-12, None,
                                             op0=OP.max),
         reads=[RF(0, 1, 2204, 2460)], writes=[RF(0, 1, 2204, 2460)], tag="f_mux")
    S.op("vector", lambda e: e.reciprocal(rowf[0:1, 2204:2460], rowf[0:1, 2204:2460]),
         reads=[RF(0, 1, 2204, 2460)], writes=[RF(0, 1, 2204, 2460)], tag="f_mur")
    for et in range(2):
        S.op("tensor", lambda e, et=et: e.matmul(B[3][:, 0:1], rowf[0:1, 2204 + 128 * et:2204 + 128 * (et + 1)],
                                                 M["ones_rf"][0:1, 0:1], start=True, stop=True),
             reads=[RF(0, 1, 2204, 2460), CONST["ones_row_f32"]], writes=[R("pb3", 0, 128, 0, 512)],
             tag="f_mcm")
        S.op("vector", lambda e, et=et: e.tensor_copy(hf[:, 4162 + et:4163 + et], B[3][:, 0:1]),
             reads=[R("pb3", 0, 128, 0, 512)],
             writes=[R("pb3", 0, 128, 0, 512), HFr(0, 128, 4162 + et, 4163 + et)], tag="f_mce")
    for j in range(NDT):
        S.op("scalar", lambda e, j=j: e.activation(
            hb[:, F5MU + 256 * j:F5MU + 256 * (j + 1)], hb[:, F5MU + 256 * j:F5MU + 256 * (j + 1)],
            AF.Identity, scale=hf[:, 4128 + j:4129 + j]),
            reads=[HB(F5MU, F5MU + 2048), HFr(0, 128, 4128, 4136)],
            writes=[HB(F5MU, F5MU + 2048)], tag="f_mg")

    for et in range(2):
        for ch in range(4):
            b = (et * 4 + ch) % 2
            def gmm(e, et=et, ch=ch, b=b):
                last = None
                for j in range(NDT):
                    last = e.matmul(B[b][:, 0:512],
                                    hb[:, F5MU + 256 * j + 128 * et:F5MU + 256 * j + 128 * (et + 1)],
                                    xnT[:, N * j + 512 * ch:N * j + 512 * (ch + 1)],
                                    start=(j == 0), stop=(j == NDT - 1))
                return last
            S.op("tensor", gmm, reads=[HB(F5MU, F5MU + 2048), R("xnT", 0, 128, 0, NDT * N)],
                 writes=[PB(b)], tag="f_gmm")
            dst = 2048 * et + 512 * ch
            S.op("vector", lambda e, dst=dst, ch=ch, b=b: e.tensor_tensor(
                hf[:, dst:dst + 512], B[b][:, 0:512], hb[:, W5A + 512 * ch:W5A + 512 * (ch + 1)],
                op=OP.mult),
                reads=[PB(b), HB(W5A + 512 * ch, W5A + 512 * (ch + 1))],
                writes=[PB(b), HFr(0, 128, dst, dst + 512)], tag="f_gct")
            S.op("vector", lambda e, dst=dst, et=et, ch=ch: e.scalar_tensor_tensor(
                hf[:, dst:dst + 512], hf[:, dst:dst + 512], hf[:, 4160 + et:4161 + et],
                hb[:, CF5 + 512 * ch:CF5 + 512 * (ch + 1)], op0=OP.add, op1=OP.mult),
                reads=[HFr(0, 128, dst, dst + 512), HFr(0, 128, 4160 + et, 4161 + et),
                       HB(CF5 + 512 * ch, CF5 + 512 * (ch + 1))],
                writes=[HFr(0, 128, dst, dst + 512)], tag="f_gev")
            S.op("vector", lambda e, dst=dst, et=et: e.tensor_scalar(
                hf[:, dst:dst + 512], hf[:, dst:dst + 512], hf[:, 4162 + et:4163 + et], None,
                op0=OP.mult),
                reads=[HFr(0, 128, dst, dst + 512), HFr(0, 128, 4162 + et, 4163 + et)],
                writes=[HFr(0, 128, dst, dst + 512)], tag="f_gsc")

    for et in range(2):
        g0 = 2048 * et
        S.op("vector", lambda e, g0=g0, et=et: e.reduce_max(hf[:, 4164 + et:4165 + et],
                                                            hf[:, g0:g0 + 2048], axis=AX.X, negate=True),
             reads=[HFr(0, 128, g0, g0 + 2048)], writes=[HFr(0, 128, 4164 + et, 4165 + et)], tag="f_dmx")
        S.op("scalar", lambda e, g0=g0, et=et: e.activation(
            hb[:, ET5 + g0:ET5 + g0 + 2048], hf[:, g0:g0 + 2048], AF.Exp, bias=hf[:, 4164 + et:4165 + et]),
            reads=[HFr(0, 128, g0, g0 + 2048), HFr(0, 128, 4164 + et, 4165 + et)],
            writes=[HB(ET5 + g0, ET5 + g0 + 2048)], tag="f_dex")
        S.op("vector", lambda e, g0=g0, et=et: e.reduce_sum(hf[:, 4166 + et:4167 + et],
                                                            hb[:, ET5 + g0:ET5 + g0 + 2048], axis=AX.X),
             reads=[HB(ET5 + g0, ET5 + g0 + 2048)], writes=[HFr(0, 128, 4166 + et, 4167 + et)],
             tag="f_dsm")
        S.op("vector", lambda e, et=et: e.reciprocal(hf[:, 4166 + et:4167 + et], hf[:, 4166 + et:4167 + et]),
             reads=[HFr(0, 128, 4166 + et, 4167 + et)], writes=[HFr(0, 128, 4166 + et, 4167 + et)],
             tag="f_drc")
        S.op("vector", lambda e, g0=g0, et=et: e.tensor_scalar(
            hb[:, ET5 + g0:ET5 + g0 + 2048], hb[:, ET5 + g0:ET5 + g0 + 2048],
            hf[:, 4166 + et:4167 + et], None, op0=OP.mult),
            reads=[HB(ET5 + g0, ET5 + g0 + 2048), HFr(0, 128, 4166 + et, 4167 + et)],
            writes=[HB(ET5 + g0, ET5 + g0 + 2048)], tag="f_dnm")
        for t in range(NT):
            b = t % 2
            S.op("tensor", lambda e, g0=g0, t=t, b=b: e.matmul(
                B[b][:, 0:128], hb[:, ET5 + g0 + 128 * t:ET5 + g0 + 128 * (t + 1)], M["idb"][:, :],
                start=True, stop=True),
                reads=[HB(ET5 + g0, ET5 + g0 + 2048), CONST["idb"]], writes=[PB(b)], tag="f_dT")
            S.op("vector", lambda e, et=et, t=t, b=b: e.tensor_copy(
                hb[:, DTM + 256 * t + 128 * et:DTM + 256 * t + 128 * (et + 1)], B[b][:, 0:128]),
                reads=[PB(b)],
                writes=[PB(b), HB(DTM + 256 * t + 128 * et, DTM + 256 * t + 128 * (et + 1))], tag="f_dTe")

    for mt in range(NDT):
        b = mt % 2
        def simm(e, mt=mt, b=b):
            last = None
            for t in range(NT):
                last = e.matmul(B[b][:, 0:256], hb[:, XC5 + 1024 * t + 128 * mt:XC5 + 1024 * t + 128 * (mt + 1)],
                                hb[:, DTM + 256 * t:DTM + 256 * (t + 1)], start=(t == 0), stop=(t == NT - 1))
            return last
        S.op("tensor", simm, reads=[HB(XC5, XC5 + 16384), HB(DTM, DTM + 4096)], writes=[PB(b)],
             tag="f_simm")
        S.op("scalar", lambda e, mt=mt, b=b: e.activation(
            hb[:, SIT + 256 * mt:SIT + 256 * (mt + 1)], B[b][:, 0:256], AF.Identity,
            bias=hf[:, 4136 + mt:4137 + mt], scale=hf[:, 4128 + mt:4129 + mt]),
            reads=[PB(b), HFr(0, 128, 4128, 4144)],
            writes=[PB(b), HB(SIT + 256 * mt, SIT + 256 * (mt + 1))], tag="f_siev")

    for ex in range(16):
        S.dma("gpsimd", lambda e, ex=ex: e.dma_start(hb[:, W5A:W5A + 2048], dt["f_w1"].ap()[ex]),
              reads=[HB(W5A, W5A + 2048)], writes=[HB(W5A, W5A + 2048)], tag="f_lw1")
        S.dma("gpsimd", lambda e, ex=ex: e.dma_start(hb[:, W5B:W5B + 2048], dt["f_w2a"].ap()[ex]),
              reads=[HB(W5B, W5B + 2048)], writes=[HB(W5B, W5B + 2048)], tag="f_lw2")
        S.dma("gpsimd", lambda e, ex=ex: e.dma_start(hb[0:1, W5C:W5C + 1024], dt["f_w2b"].ap()[ex]),
              reads=[HBr(0, 1, W5C, W5C + 1024)], writes=[HBr(0, 1, W5C, W5C + 1024)], tag="f_lw2b")
        for ht in range(2):
            b = ht % 2
            def hmm(e, ex=ex, ht=ht, b=b):
                last = None
                for j in range(NDT):
                    last = e.matmul(B[b][:, 0:16],
                                    hb[:, W5A + 256 * j + 128 * ht:W5A + 256 * j + 128 * (ht + 1)],
                                    hb[:, SIT + 256 * j + 16 * ex:SIT + 256 * j + 16 * (ex + 1)],
                                    start=(j == 0), stop=(j == NDT - 1))
                return last
            S.op("tensor", hmm, reads=[HB(W5A, W5A + 2048), HB(SIT, SIT + 2048)], writes=[PB(b)],
                 tag="f_hmm")
            S.op("scalar", lambda e, ex=ex, ht=ht, b=b: e.activation(
                hb[:, HT5 + 16 * ht:HT5 + 16 * (ht + 1)], B[b][:, 0:16], AF.Gelu,
                bias=hf[:, 4096 + 2 * ex + ht:4097 + 2 * ex + ht]),
                reads=[PB(b), HFr(0, 128, 4096, 4128)],
                writes=[PB(b), HB(HT5 + 16 * ht, HT5 + 16 * (ht + 1))], tag="f_hev")
        for mt in range(NDT):
            b = mt % 2
            def somm(e, ex=ex, mt=mt, b=b):
                for ht in range(2):
                    e.matmul(B[b][:, 0:16], hb[:, W5B + 1024 * ht + 128 * mt:W5B + 1024 * ht + 128 * (mt + 1)],
                             hb[:, HT5 + 16 * ht:HT5 + 16 * (ht + 1)], start=(ht == 0), stop=False)
                return e.matmul(B[b][:, 0:16], hb[0:1, W5C + 128 * mt:W5C + 128 * (mt + 1)],
                                M["ones_row"][0:1, 0:16], start=False, stop=True)
            S.op("tensor", somm, reads=[HB(W5B, W5B + 2048), HB(HT5, HT5 + 32),
                                        HBr(0, 1, W5C, W5C + 1024), CONST["ones_row_bf"]],
                 writes=[PB(b)], tag="f_somm")
            S.op("vector", lambda e, ex=ex, mt=mt, b=b: e.tensor_copy(
                hb[:, SOT + 256 * mt + 16 * ex:SOT + 256 * mt + 16 * (ex + 1)], B[b][:, 0:16]),
                reads=[PB(b)], writes=[PB(b), HB(SOT + 256 * mt + 16 * ex, SOT + 256 * mt + 16 * (ex + 1))],
                tag="f_soev")

    for mt in range(NDT):
        for et in range(2):
            b = (mt + et) % 2
            S.op("tensor", lambda e, mt=mt, et=et, b=b: e.matmul(
                B[b][:, 0:128], hb[:, SOT + 256 * mt + 128 * et:SOT + 256 * mt + 128 * (et + 1)],
                M["idb"][:, :], start=True, stop=True),
                reads=[HB(SOT, SOT + 2048), CONST["idb"]], writes=[PB(b)], tag="f_sT")
            S.op("vector", lambda e, mt=mt, et=et, b=b: e.tensor_copy(
                hb[:, SO5 + 1024 * et + 128 * mt:SO5 + 1024 * et + 128 * (mt + 1)], B[b][:, 0:128]),
                reads=[PB(b)], writes=[PB(b), HB(SO5 + 1024 * et + 128 * mt, SO5 + 1024 * et + 128 * (mt + 1))],
                tag="f_sTe")

    for t in range(NT):
        for et in range(2):
            b = 2 + et
            S.op("tensor", lambda e, t=t, et=et, b=b: e.matmul(
                B[b][:, 0:128], hf[:, 2048 * et + 128 * t:2048 * et + 128 * (t + 1)], M["idf"][:, :],
                start=True, stop=True),
                reads=[HFr(0, 128, 0, 4096), CONST["idf"]], writes=[PB(b)], tag="f_cT")
        S.op("vector", lambda e: e.reduce_max(hf[:, 4168:4169], B[2][:, 0:128], axis=AX.X, negate=True),
             reads=[PB(2)], writes=[PB(2), HFr(0, 128, 4168, 4169)], tag="f_cm0")
        S.op("vector", lambda e: e.reduce_max(hf[:, 4169:4170], B[3][:, 0:128], axis=AX.X, negate=True),
             reads=[PB(3)], writes=[PB(3), HFr(0, 128, 4169, 4170)], tag="f_cm1")
        S.op("vector", lambda e: e.tensor_tensor(hf[:, 4168:4169], hf[:, 4168:4169], hf[:, 4169:4170],
                                                 op=OP.min),
             reads=[HFr(0, 128, 4168, 4170)], writes=[HFr(0, 128, 4168, 4169)], tag="f_cm2")
        for et in range(2):
            b = 2 + et
            S.op("scalar", lambda e, et=et, b=b: e.activation(
                hb[:, CMS + 128 * et:CMS + 128 * (et + 1)], B[b][:, 0:128], AF.Exp,
                bias=hf[:, 4168:4169]),
                reads=[PB(b), HFr(0, 128, 4168, 4169)],
                writes=[PB(b), HB(CMS + 128 * et, CMS + 128 * (et + 1))], tag="f_cex")
        S.op("vector", lambda e, t=t: e.reduce_sum(hf[:, 4096 + t:4097 + t], hb[:, CMS:CMS + 256],
                                                   axis=AX.X),
             reads=[HB(CMS, CMS + 256)], writes=[HFr(0, 128, 4096 + t, 4097 + t)], tag="f_csm")
        S.op("vector", lambda e, t=t: e.reciprocal(hf[:, 4096 + t:4097 + t], hf[:, 4096 + t:4097 + t]),
             reads=[HFr(0, 128, 4096 + t, 4097 + t)], writes=[HFr(0, 128, 4096 + t, 4097 + t)],
             tag="f_crc")
        for et in range(2):
            b = 4 + et
            S.op("tensor", lambda e, t=t, et=et, b=b: e.matmul(
                B[b][:, 0:128], hb[:, CMS + 128 * et:CMS + 128 * (et + 1)], M["idb"][:, :],
                start=True, stop=True),
                reads=[HB(CMS, CMS + 256), CONST["idb"]], writes=[PB(b)], tag="f_c2T")
            S.op("vector", lambda e, t=t, et=et, b=b: e.tensor_copy(
                hb[:, CMT + 2048 * et + 128 * t:CMT + 2048 * et + 128 * (t + 1)], B[b][:, 0:128]),
                reads=[PB(b)],
                writes=[PB(b), HB(CMT + 2048 * et + 128 * t, CMT + 2048 * et + 128 * (t + 1))],
                tag="f_c2e")
    S.dma("sync", lambda e: e.dma_start(dt["cmt_d"].ap(), hb[:, CMT:CMT + 4096]),
          reads=[HB(CMT, CMT + 4096)], writes=[R("cmt_d", 0, 128, 0, 4096)], tag="f_cst")


def _l1_ffn_out(S, dt, out_d, xres, M, PB, CONST):
    hb, hf = M["hb"], M["hf"]
    B = M["B"]
    HB = lambda c0, c1: R("hb", 0, 128, c0, c1)
    HFr = lambda r0, r1, c0, c1: R("hf", r0, r1, c0, c1)
    SO5 = 14400
    CMB = 0
    S.dma("sync", lambda e: e.dma_start(hb[:, CMB:CMB + 4096], dt["cmt_d"].ap()),
          reads=[R("cmt_d", 0, 128, 0, 4096), HB(CMB, CMB + 4096)], writes=[HB(CMB, CMB + 4096)],
          tag="f_cld")
    for t in range(NT):
        r = 1024 * (t % 2)
        xr = R("hf", 0, 128, r, r + D)
        for dc in range(2):
            def cmm(e, t=t, dc=dc):
                last = None
                for et in range(2):
                    last = e.matmul(B[dc][:, 0:512],
                                    hb[:, CMB + 2048 * et + 128 * t:CMB + 2048 * et + 128 * (t + 1)],
                                    hb[:, SO5 + 1024 * et + 512 * dc:SO5 + 1024 * et + 512 * (dc + 1)],
                                    start=(et == 0), stop=(et == 1))
                return last
            S.op("tensor", cmm, reads=[HB(CMB, CMB + 4096), HB(SO5, SO5 + 2048)], writes=[PB(dc)],
                 tag="f_omm")
        S.dma("sync", lambda e, t=t, r=r: e.dma_start(hf[:, r:r + D], xres.ap()[128 * t:128 * (t + 1), :]),
              reads=[xr, R("xres", t, t + 1, 0, 1)], writes=[xr], tag="f_oxld")
        for dc in range(2):
            S.op("vector", lambda e, t=t, r=r, dc=dc: e.scalar_tensor_tensor(
                hf[:, r + 512 * dc:r + 512 * (dc + 1)], B[dc][:, 0:512], hf[:, 4096 + t:4097 + t],
                hf[:, r + 512 * dc:r + 512 * (dc + 1)], op0=OP.mult, op1=OP.add),
                reads=[PB(dc), HFr(0, 128, 4096 + t, 4097 + t),
                       R("hf", 0, 128, r + 512 * dc, r + 512 * (dc + 1))],
                writes=[PB(dc), R("hf", 0, 128, r + 512 * dc, r + 512 * (dc + 1))], tag="f_oacc")
        S.dma("sync", lambda e, t=t, r=r: e.dma_start(out_d.ap()[128 * t:128 * (t + 1), :], hf[:, r:r + D]),
              reads=[xr], writes=[R("outd", t, t + 1, 0, 1)], tag="f_ost")


def _finish(S, dt, out_d, xres, M, PB, CONST):
    hf, rowf, qx = M["hf"], M["rowf"], M["qx"]
    B = M["B"]
    RF = lambda r0, r1, c0, c1: R("rowf", r0, r1, c0, c1)
    add_bv = BUILD_UPTO in (1, 2, 3)
    if add_bv:
        S.dma("sync", lambda e: e.dma_start(rowf[0:1, 0:1024], dt["bv_row"].ap()),
              reads=[RF(0, 1, 0, 1024)], writes=[RF(0, 1, 0, 1024)], tag="ld_bv")
        for half in range(2):
            S.op("tensor", lambda e, half=half: e.matmul(
                B[6][:, 0:512], M["ones_rf"][0:1, 0:128],
                rowf[0:1, 512 * half:512 * (half + 1)], start=True, stop=True),
                reads=[RF(0, 1, 0, 1024), CONST["ones_row_f32"]], writes=[PB(6)], tag="bvmm")
            S.op("vector", lambda e, half=half: e.tensor_copy(
                hf[:, 2048 + 512 * half:2048 + 512 * (half + 1)], B[6][:, 0:512]),
                reads=[PB(6)], writes=[PB(6), R("hf", 0, 128, 2048 + 512 * half, 2048 + 512 * (half + 1))],
                tag="bvev")
    for i in range(NT if BUILD_UPTO < 5 else 0):
        r = 1024 * (i % 2)
        xr = R("hf", 0, 128, r, r + D)
        S.dma("sync", lambda e, i=i, r=r: e.dma_start(hf[:, r:r + D], xres.ap()[128 * i:128 * (i + 1), :]),
              reads=[xr, R("xres", i, i + 1, 0, 1)], writes=[xr], tag="ld_xfin")
        if add_bv:
            S.op("vector", lambda e, r=r: e.tensor_tensor(hf[:, r:r + D], hf[:, r:r + D], hf[:, 2048:3072],
                                                          op=OP.add),
                 reads=[xr, R("hf", 0, 128, 2048, 3072)], writes=[xr], tag="bvadd")
        S.dma("sync", lambda e, i=i, r=r: e.dma_start(out_d.ap()[128 * i:128 * (i + 1), :], hf[:, r:r + D]),
              reads=[xr], writes=[R("outd", i, i + 1, 0, 1)], tag="st_out")
    S.op("sync", lambda e: e.nop(), reads=[R("outd", 0, NT, 0, 1)], tag="final_wait")



def kernel(**inputs):
    W = {k: np.asarray(v) for k, v in inputs.items()}
    H = _prep_weights(W)
    T = _host_tables()
    nc = build_program()
    base = {**H, **T}
    xs = np.ascontiguousarray(np.asarray(W["L0_attn_x"], np.float32))
    in_maps = []
    for c in range(8):
        m = dict(base)
        m["x"] = np.ascontiguousarray(xs[c])
        in_maps.append(m)
    res = run_bass_kernel_spmd(nc, in_maps, core_ids=list(range(8)))
    out = np.stack([np.asarray(r["out"]) for r in res.results]).astype(np.float32)
    return out
